# version 05 (final)
# baseline (speedup 1.0000x reference)
import contextlib
import os
import numpy as np
import ml_dtypes
import bass_rust
import concourse.bass as bass
import concourse.mybir as mybir
from concourse.bass_utils import run_bass_kernel_spmd

F32 = mybir.dt.float32
BF16 = mybir.dt.bfloat16
I32 = mybir.dt.int32
U8 = mybir.dt.uint8
AF = mybir.ActivationFunctionType
OP = mybir.AluOpType

S = 4096
D = 4096
NH = 32
NKV = 8
DH = 11008
RK = 16
NDK = 32
NHC = 86
NDC = 32
XCH = 256
NCH = S // XCH
NVI = NCH * 4
NKT = 32
TOK = 1024
SCALE = float(1.0 / np.sqrt(128.0))
LNML = float(-0.5 * np.log(128.0))
HG_BOUNDS = [(0, 16), (16, 32), (32, 48), (48, 64), (64, 80), (80, 86)]
NPASS = len(HG_BOUNDS)
KB = 1024

ST_KMAX = 0
ST_QMAX = 1
ST_LNK = 2
ST_LNQ = 3
ST_LNS = 4
ST_M = 5
ST_LNML = 6
ST_EPS = 7
ST_RT0 = 8
ST_TMP = 12
ST_RSTD = [524, 1036]
ST_B = [1548, 2060]
ST_LEN = 2572

DMA_STREAMS = [
    ("dX", 2), ("dWQ", 2), ("dWO", 2), ("dG", 2), ("dU", 2), ("dD", 2),
    ("dVL", 4), ("dKL", 2), ("dQL", 2), ("dXR", 4), ("dKS", 2), ("dVS", 2),
    ("dQS", 2), ("dR1S", 4), ("dR1L", 4), ("dRAL", 4), ("dRAS", 4), ("dOUT", 2),
]
SEM_NAMES = [
    "dW", "dMSC", "dTAB",
    "cINI",
    "cKM", "cKN", "cXK", "cVM", "cQM", "cQN", "cMB", "cSC", "cPC", "cAV",
    "cBC", "cQT", "cHEAD", "cOM", "cST", "cABC", "cUG", "cGM", "cUM", "cDM",
    "cABC2",
    "cSH", "cQSH", "cEXP", "cPART", "cR", "cSQ", "cX1", "cSIL", "cROW2",
    "cKE", "cKNR", "cVE", "cQE", "cQNR", "cMSK", "cRB", "cATN", "cR1",
    "cR1B", "cROW", "cHID", "cUDE", "cST2", "cO8",
]


class PStream:

    def __init__(self, sems):
        self.sems = sems
        self.n = len(sems)

    def inc(self, ins, idx):
        return ins.then_inc(self.sems[idx % self.n], 16)

    def wait(self, eng, idx):
        if idx < 0:
            return
        eng.wait_ge(self.sems[idx % self.n], 16 * (idx // self.n + 1))

    def wait_all(self, eng, total):
        for k in range(self.n):
            cnt = (total - 1 - k) // self.n + 1 if total > k else 0
            if cnt:
                eng.wait_ge(self.sems[k], 16 * cnt)


def _nk(j, qg):
    return 4 * j + 4 if qg == 0 else 32 - 4 * j


def plan_p4(j):
    plan = []
    GG = 0
    E = 0
    for i in range(64):
        qg = i % 2
        NK = _nk(j, qg)
        ng = NK // 2
        plan.append(dict(NK=NK, ng=ng, GG=GG, E=E, P=2 * i))
        GG += ng
        E += ng - 2
    return plan


def group_kind(plan, gg):
    for it in plan:
        if gg < it["GG"] + it["ng"]:
            g = gg - it["GG"]
            if g < it["ng"] - 2:
                return ("e", it["E"] + g + 1)
            return ("p", it["P"] + (g - (it["ng"] - 2)) + 1)
    raise AssertionError(gg)


def build_program():
    PH = int(os.environ.get("KPH", "8"))
    NI4 = int(os.environ.get("KP4N", "64"))
    NH4 = min(NH, (NI4 + 1) // 2)
    nc = bass.Bass("TRN2", target_bir_lowering=False)

    xfm = nc.dram_tensor("xfm", [D, S], F32, kind="ExternalInput").ap()
    xres = nc.dram_tensor("xres", [128, NDC, TOK], F32, kind="ExternalInput").ap()
    wq = nc.dram_tensor("wq", [128, NH, NDK, 128], F32, kind="ExternalInput").ap()
    wk = nc.dram_tensor("wk", [128, NKV, NDK, 128], F32, kind="ExternalInput").ap()
    wv = nc.dram_tensor("wv", [128, NDK, 1024], F32, kind="ExternalInput").ap()
    wo = nc.dram_tensor("wo", [128, NDC, NH, 128], F32, kind="ExternalInput").ap()
    gw = nc.dram_tensor("gw", [128, NHC, NDK + 1, 128], F32, kind="ExternalInput").ap()
    uw = nc.dram_tensor("uw", [128, NHC, NDK + 1, 128], F32, kind="ExternalInput").ap()
    dw = nc.dram_tensor("dw", [128, NDC, NHC + 1, 128], F32, kind="ExternalInput").ap()
    ga = nc.dram_tensor("ga", [128, NDK, RK], F32, kind="ExternalInput").ap()
    ua = nc.dram_tensor("ua", [128, NDK, RK], F32, kind="ExternalInput").ap()
    da = nc.dram_tensor("da", [128, NHC, RK], F32, kind="ExternalInput").ap()
    ck = nc.dram_tensor("ck", [128, S], F32, kind="ExternalInput").ap()
    sk = nc.dram_tensor("sk", [128, S], F32, kind="ExternalInput").ap()
    cq = nc.dram_tensor("cq", [128, TOK], F32, kind="ExternalInput").ap()
    sq = nc.dram_tensor("sq", [128, TOK], F32, kind="ExternalInput").ap()
    maskin = nc.dram_tensor("maskin", [128, 4, 512], BF16, kind="ExternalInput").ap()
    meta = nc.dram_tensor("meta", [1, 4], I32, kind="ExternalInput").ap()

    out = nc.dram_tensor("out", [128, NDC, TOK], F32, kind="ExternalOutput").ap()

    k_scr = nc.dram_tensor("k_scr", [NKV, 128, S], BF16, kind="Internal").ap()
    v_scr = nc.dram_tensor("v_scr", [NKT, 128, 1024], BF16, kind="Internal").ap()
    q_scr = nc.dram_tensor("q_scr", [NH, 128, TOK], BF16, kind="Internal").ap()
    r1_scr = nc.dram_tensor("r1_scr", [NDC, 128, TOK], F32, kind="Internal").ap()
    racc_scr = nc.dram_tensor("racc_scr", [NDC, 128, TOK], F32, kind="Internal").ap()

    meta_sb = nc.alloc_sbuf_tensor("meta_sb", [1, 4], I32)
    ones_bf = nc.alloc_sbuf_tensor("ones_bf", [128, 1], BF16)
    onesr_f = nc.alloc_sbuf_tensor("onesr_f", [1, 128], F32)
    negr_f = nc.alloc_sbuf_tensor("negr_f", [1, 128], F32)
    bias_sb = nc.alloc_sbuf_tensor("bias_sb", [128, 1], F32)
    strow = nc.alloc_sbuf_tensor("strow", [1, ST_LEN], F32)
    lrow = nc.alloc_sbuf_tensor("lrow", [1, 1024], F32)
    rrow = nc.alloc_sbuf_tensor("rrow", [1, 1024], F32)
    a_bc = nc.alloc_sbuf_tensor("a_bc", [128, 1024], F32)
    b_bc = nc.alloc_sbuf_tensor("b_bc", [128, 1024], F32)
    ug_full = nc.alloc_sbuf_tensor("ug_full", [128, 1024], BF16)
    uu_full = nc.alloc_sbuf_tensor("uu_full", [128, 1024], BF16)
    ud_full = nc.alloc_sbuf_tensor("ud_full", [128, 1024], BF16)
    ga_bf = nc.alloc_sbuf_tensor("ga_bf", [128, NDK * RK], BF16)
    ua_bf = nc.alloc_sbuf_tensor("ua_bf", [128, NDK * RK], BF16)
    da_bf = nc.alloc_sbuf_tensor("da_bf", [128, NHC * RK], BF16)

    AR = 170 * KB
    arena = nc.alloc_sbuf_tensor("arena", [128, AR], U8)
    abase = nc.lookup_mloc(arena).addr

    def place(name, shape, dtype, off):
        nbytes = int(np.prod(shape[1:])) * mybir.dt.size(dtype)
        assert off + nbytes <= AR, (name, off, nbytes)
        return nc.alloc_sbuf_tensor_at(name, shape, dtype, offset=abase + off)

    wk_bf = place("wk_bf", [128, NKV * NDK * 128], BF16, 0)
    wv_bf = place("wv_bf", [128, NDK * 1024], BF16, 0)
    xbuf = place("xbuf", [128, 2 * NDK * XCH], BF16, 64 * KB)
    ck_sb = place("ck_sb", [128, S], F32, 96 * KB)
    sk_sb = place("sk_sb", [128, S], F32, 112 * KB)
    sh_t = place("sh_t", [128, 2 * 512], F32, 128 * KB)
    t1_t = place("t1_t", [128, 2 * 512], F32, 132 * KB)
    t2_t = place("t2_t", [128, 2 * 512], F32, 136 * KB)
    kout = place("kout", [128, 2 * 512], BF16, 140 * KB)
    ksq = place("ksq", [128, 2 * 512], BF16, 142 * KB)
    vout = place("vout", [128, 2 * 512], BF16, 144 * KB)
    cq_sb = place("cq_sb", [128, TOK], F32, 146 * KB)
    sq_sb = place("sq_sb", [128, TOK], F32, 150 * KB)
    wq_bf = place("wq_bf", [128, 2 * NDK * 128], BF16, 154 * KB)
    xq_sb = place("xq_sb", [128, NDK * TOK], BF16, 64 * KB)
    kbuf = place("kbuf", [128, 2 * S], BF16, 0)
    vbuf = place("vbuf", [128, 2 * NKT * 128], BF16, 16 * KB)
    qbuf = place("qbuf", [128, 2 * TOK], BF16, 32 * KB)
    probs = place("probs", [128, 4 * 512], BF16, 36 * KB)
    rb_sb = place("rb_sb", [128, 2 * 512], F32, 40 * KB)
    mask_sb = place("mask_sb", [128, 4 * 512], BF16, 44 * KB)
    attn_res = place("attn_res", [128, NH * 1024], BF16, 104 * KB)
    wo_bf = place("wo_bf", [128, 2 * NH * 128], BF16, 0)
    xresb = place("xresb", [128, 4 * 512], F32, 16 * KB)
    r1buf = place("r1buf", [128, 4 * 512], F32, 24 * KB)
    r1bf = place("r1bf", [128, 2 * 512], BF16, 45 * KB)
    sqbf = place("sqbf", [128, 2 * 512], BF16, 47 * KB)
    r1ld = place("r1ld", [128, 4 * 512], F32, 32 * KB)
    x1f32 = place("x1f32", [128, 4 * 512], F32, 48 * KB)
    x1_bf = place("x1_bf", [128, NDK * TOK], BF16, 104 * KB)
    gbuf = place("gbuf", [128, 2 * (NDK + 1) * 128], BF16, 0)
    ubuf = place("ubuf", [128, 2 * (NDK + 1) * 128], BF16, 17 * KB)
    dbuf = place("dbuf", [128, 2 * 17 * 128], BF16, 34 * KB)
    tsil = place("tsil", [128, 2 * 512], BF16, 43 * KB)
    hidden = place("hidden", [128, 16 * 1024], BF16, 56 * KB)
    raccb = place("raccb", [128, 8 * 512], F32, 88 * KB)
    raccld8 = place("raccld8", [128, 4 * 512], F32, 0)
    outb8 = place("outb8", [128, 4 * 512], F32, 8 * KB)

    PA = nc.alloc_psum_tensor("PA", [128, 1024], F32)
    PB = nc.alloc_psum_tensor("PB", [128, 1024], F32)
    PC = nc.alloc_psum_tensor("PC", [128, 512], F32)
    PD = nc.alloc_psum_tensor("PD", [128, 512], F32)
    PE_ = nc.alloc_psum_tensor("PE_", [128, 512], F32)
    PF = nc.alloc_psum_tensor("PF", [128, 512], F32)

    sems = {}
    stack = contextlib.ExitStack()
    for name in SEM_NAMES:
        sems[name] = stack.enter_context(nc.semaphore(name))
    sm = type("SM", (), sems)
    st = {}
    for name, par in DMA_STREAMS:
        st[name] = PStream(
            [stack.enter_context(nc.semaphore(f"{name}{k}")) for k in range(par)]
        )

    plans = [plan_p4(j) for j in range(4)]
    pk = [PC, PD]
    pk4 = [PC, PD, PA[:, 0:512], PA[:, 512:1024]]

    def sl(t_, i, w=512, nslot=None):
        islot = i % (nslot if nslot else 2)
        return t_[:, islot * w : (islot + 1) * w]

    with stack, nc.Block() as block:

        @block.gpsimd
        def _(g):
            g.dma_start(out=meta_sb[:], in_=meta).then_inc(sm.dMSC, 16)
            g.wait_ge(sm.dMSC, 16)
            g.dma_start(
                out=ga_bf[:].rearrange("p (c r) -> p c r", c=NDK), in_=ga
            ).then_inc(sm.dMSC, 16)
            g.wait_ge(sm.dMSC, 32)
            g.dma_start(
                out=ua_bf[:].rearrange("p (c r) -> p c r", c=NDK), in_=ua
            ).then_inc(sm.dMSC, 16)
            g.wait_ge(sm.dMSC, 48)
            g.dma_start(
                out=da_bf[:].rearrange("p (c r) -> p c r", c=NHC), in_=da
            ).then_inc(sm.dMSC, 16)
            if PH >= 1:
                g.dma_start(
                    out=wk_bf[:].rearrange("p (kv c q) -> p kv c q", kv=NKV, c=NDK),
                    in_=wk,
                ).then_inc(sm.dW, 16)
                for tc in range(NCH):
                    if tc >= 2:
                        g.wait_ge(sm.cXK, tc - 1)
                    st["dX"].inc(g.dma_start(
                        out=xbuf[:, (tc % 2) * NDK * XCH : (tc % 2 + 1) * NDK * XCH]
                        .rearrange("p (c t) -> p c t", c=NDK),
                        in_=xfm[:, tc * XCH : (tc + 1) * XCH]
                        .rearrange("(c p) t -> p c t", p=128),
                    ), tc)
            if PH >= 2:
                g.wait_ge(sm.dW, 16)
                for kvw in range(NKV):
                    g.wait_ge(sm.cKM, (NCH - 1) * NKV + kvw + 1)
                    g.wait_ge(sm.dW, 16 * (kvw + 1))
                    g.dma_start(
                        out=wv_bf[:, kvw * 4 * 1024 : (kvw + 1) * 4 * 1024]
                        .rearrange("p (c v) -> p c v", c=4),
                        in_=wv[:, kvw * 4 : (kvw + 1) * 4, :],
                    ).then_inc(sm.dW, 16)
                for tc in range(NCH):
                    g.wait_ge(sm.cXK, NCH + tc - 1)
                    st["dX"].inc(g.dma_start(
                        out=xbuf[:, (tc % 2) * NDK * XCH : (tc % 2 + 1) * NDK * XCH]
                        .rearrange("p (c t) -> p c t", c=NDK),
                        in_=xfm[:, tc * XCH : (tc + 1) * XCH]
                        .rearrange("(c p) t -> p c t", p=128),
                    ), NCH + tc)
            if PH >= 3:
                g.wait_ge(sm.cKE, NCH * NKV)
                st["dX"].inc(g.dma_start(
                    out=xq_sb[:, 16 * TOK :].rearrange("p (c t) -> p c t", c=16),
                    in_=xres[:, 16:32, :],
                ), 2 * NCH)
                for qh in range(2):
                    st["dWQ"].inc(g.dma_start(
                        out=wq_bf[:, (qh % 2) * NDK * 128 : (qh % 2 + 1) * NDK * 128]
                        .rearrange("p (c q) -> p c q", c=NDK),
                        in_=wq[:, qh, :, :],
                    ), qh)
                g.wait_ge(sm.cXK, 2 * NCH)
                st["dX"].inc(g.dma_start(
                    out=xq_sb[:, : 16 * TOK].rearrange("p (c t) -> p c t", c=16),
                    in_=xres[:, 0:16, :],
                ), 2 * NCH + 1)
                g.wait_ge(sm.dMSC, 64)
                g.dma_start(
                    out=mask_sb[:].rearrange("p (m f) -> p m f", m=4), in_=maskin
                ).then_inc(sm.dMSC, 16)
                def _vload(kv):
                    for sub in range(4):
                        st["dVL"].inc(g.dma_start(
                            out=vbuf[
                                :, (kv % 2) * NKT * 128 + sub * 8 * 128 : (kv % 2) * NKT * 128 + (sub + 1) * 8 * 128
                            ].rearrange("p (kt d) -> p kt d", kt=8),
                            in_=v_scr[sub * 8 : (sub + 1) * 8, :, kv * 128 : (kv + 1) * 128].rearrange("kt p d -> p kt d"),
                        ), kv * 4 + sub)

                for qh in range(2, NH):
                    g.wait_ge(sm.cQM, 2 * (qh - 1))
                    st["dWQ"].inc(g.dma_start(
                        out=wq_bf[:, (qh % 2) * NDK * 128 : (qh % 2 + 1) * NDK * 128]
                        .rearrange("p (c q) -> p c q", c=NDK),
                        in_=wq[:, qh, :, :],
                    ), qh)
                    if PH >= 4 and qh == 2:
                        st["dVS"].wait_all(g, NVI)
                        _vload(0)
                    if PH >= 4 and qh == 6 and NH4 > 4:
                        _vload(1)
            if PH >= 4:
                for kv in range(2, min(NKV, (NH4 - 1) // 4 + 1)):
                    g.wait_ge(sm.cHEAD, kv - 1)
                    for sub in range(4):
                        st["dVL"].inc(g.dma_start(
                            out=vbuf[
                                :, (kv % 2) * NKT * 128 + sub * 8 * 128 : (kv % 2) * NKT * 128 + (sub + 1) * 8 * 128
                            ].rearrange("p (kt d) -> p kt d", kt=8),
                            in_=v_scr[sub * 8 : (sub + 1) * 8, :, kv * 128 : (kv + 1) * 128].rearrange("kt p d -> p kt d"),
                        ), kv * 4 + sub)
            if PH >= 5:
                for dc in range(NDC):
                    if dc < 2:
                        g.wait_ge(sm.cATN, 64)
                    else:
                        g.wait_ge(sm.cOM, 2 * (dc - 1))
                    st["dWO"].inc(g.dma_start(
                        out=wo_bf[:, (dc % 2) * NH * 128 : (dc % 2 + 1) * NH * 128]
                        .rearrange("p (c q) -> p c q", c=NH),
                        in_=wo[:, dc, :, :],
                    ), dc)
                    for qg in range(2):
                        i = dc * 2 + qg
                        if i >= 4:
                            g.wait_ge(sm.cR1, i - 3)
                        st["dXR"].inc(g.dma_start(
                            out=sl(xresb, i, nslot=4),
                            in_=xres[:, dc, qg * 512 : (qg + 1) * 512],
                        ), i)
            if PH >= 7:
                for p, (h0, h1) in enumerate(HG_BOUNDS):
                    for hgi in range(h1 - h0):
                        hc = h0 + hgi
                        if hc < 2:
                            st["dRAS"].wait_all(g, NDC * 2)
                            g.wait_ge(sm.cX1, NDC * 2)
                        else:
                            g.wait_ge(sm.cGM, 2 * (hc - 1))
                        st["dG"].inc(g.dma_start(
                            out=gbuf[
                                :, (hc % 2) * (NDK + 1) * 128 : (hc % 2 + 1) * (NDK + 1) * 128
                            ].rearrange("p (c q) -> p c q", c=NDK + 1),
                            in_=gw[:, hc, :, :],
                        ), hc)
                        if hc >= 2:
                            g.wait_ge(sm.cUM, 2 * (hc - 1))
                        st["dU"].inc(g.dma_start(
                            out=ubuf[
                                :, (hc % 2) * (NDK + 1) * 128 : (hc % 2 + 1) * (NDK + 1) * 128
                            ].rearrange("p (c q) -> p c q", c=NDK + 1),
                            in_=uw[:, hc, :, :],
                        ), hc)
                    ng = (h1 - h0) + (1 if p == NPASS - 1 else 0)
                    for dc in range(NDC):
                        gi = p * NDC + dc
                        if gi < 2:
                            st["dRAS"].wait_all(g, NDC * 2)
                        else:
                            g.wait_ge(sm.cDM, 2 * (gi - 1))
                        st["dD"].inc(g.dma_start(
                            out=dbuf[:, (gi % 2) * 17 * 128 : (gi % 2) * 17 * 128 + ng * 128]
                            .rearrange("p (c q) -> p c q", c=ng),
                            in_=dw[:, dc, h0 : h0 + ng, :],
                        ), gi)

        @block.sync
        def _(sy):
            sy.dma_start(out=ck_sb[:], in_=ck).then_inc(sm.dTAB, 16)
            sy.wait_ge(sm.dTAB, 16)
            sy.dma_start(out=sk_sb[:], in_=sk).then_inc(sm.dTAB, 16)
            sy.wait_ge(sm.dTAB, 32)
            sy.dma_start(out=cq_sb[:], in_=cq).then_inc(sm.dTAB, 16)
            sy.wait_ge(sm.dTAB, 48)
            sy.dma_start(out=sq_sb[:], in_=sq).then_inc(sm.dTAB, 16)
            for i in range(NCH * NKV):
                tc, kv = divmod(i, NKV)
                sy.wait_ge(sm.cKE, i + 1)
                st["dKS"].inc(sy.dma_start(
                    out=k_scr[kv, :, tc * XCH : (tc + 1) * XCH],
                    in_=kout[:, (i % 2) * 512 : (i % 2) * 512 + XCH],
                ), i)
            for i in range(NVI):
                tcts, vfh = divmod(i, 2)
                tc, tsub = divmod(tcts, 2)
                kt = tc * 2 + tsub
                sy.wait_ge(sm.cVE, i + 1)
                st["dVS"].inc(sy.dma_start(
                    out=v_scr[kt, :, vfh * 512 : (vfh + 1) * 512],
                    in_=sl(vout, i),
                ), i)
            if PH >= 4:
                def _kload(kv):
                    st["dKS"].wait_all(sy, NCH * NKV)
                    if kv < 2:
                        sy.wait_ge(sm.cVM, NVI)
                    else:
                        sy.wait_ge(sm.cHEAD, kv - 1)
                    st["dKL"].inc(sy.dma_start(
                        out=kbuf[:, (kv % 2) * S : (kv % 2 + 1) * S], in_=k_scr[kv, :, :]
                    ), kv)

                _kload(0)
                if NH4 > 4:
                    _kload(1)
            for i in range(NH * 2):
                qh, qg = divmod(i, 2)
                sy.wait_ge(sm.cQE, i + 1)
                st["dQS"].inc(sy.dma_start(
                    out=q_scr[qh, :, qg * 512 : (qg + 1) * 512], in_=sl(kout, i)
                ), i)
            if PH >= 4:
                for i in range(NH4):
                    st["dQS"].wait(sy, 2 * i + 1)
                    st["dQS"].wait(sy, 2 * i)
                    if i >= 2:
                        sy.wait_ge(sm.cQT, i - 1)
                    st["dQL"].inc(sy.dma_start(
                        out=qbuf[:, (i % 2) * TOK : (i % 2 + 1) * TOK], in_=q_scr[i, :, :]
                    ), i)
                    if i % 4 == 3 and i // 4 + 2 < NKV:
                        _kload(i // 4 + 2)
            if PH >= 5:
                for i in range(NDC * 2):
                    dc, qg = divmod(i, 2)
                    sy.wait_ge(sm.cR1, i + 1)
                    st["dR1S"].inc(sy.dma_start(
                        out=r1_scr[dc, :, qg * 512 : (qg + 1) * 512],
                        in_=sl(r1buf, i, nslot=4),
                    ), i)
            if PH >= 6:
                def _sp6(k):
                    dck, qgk = divmod(k, 2)
                    sy.wait_ge(sm.cHID, k + 1)
                    st["dRAS"].wait(sy, k - 4)
                    st["dRAS"].inc(sy.dma_start(
                        out=racc_scr[dck, :, qgk * 512 : (qgk + 1) * 512],
                        in_=sl(x1f32, k, nslot=4),
                    ), k)

                for i in range(NDC * 2):
                    dc, qg = divmod(i, 2)
                    if i < 4:
                        sy.wait_ge(sm.cATN, NI4)
                    else:
                        sy.wait_ge(sm.cX1, i - 3)
                    st["dR1S"].wait(sy, i)
                    st["dR1L"].inc(sy.dma_start(
                        out=sl(r1ld, i, nslot=4),
                        in_=r1_scr[dc, :, qg * 512 : (qg + 1) * 512],
                    ), i)
                    if i >= 2:
                        _sp6(i - 2)
                _sp6(NDC * 2 - 2)
                _sp6(NDC * 2 - 1)
            if PH >= 7:
                for p in range(NPASS):
                    def _sp7(k):
                        dck, qgk = divmod(k, 2)
                        gk = p * NDC * 2 + k
                        sy.wait_ge(sm.cUDE, gk + 1)
                        if p == NPASS - 1:
                            sy.wait_ge(sm.cST2, k + 1)
                        st["dRAS"].wait(sy, NDC * 2 + gk - 4)
                        st["dRAS"].inc(sy.dma_start(
                            out=racc_scr[dck, :, qgk * 512 : (qgk + 1) * 512],
                            in_=sl(raccb, k, nslot=8),
                        ), NDC * 2 + gk)

                    for i in range(NDC * 2):
                        dc, qg = divmod(i, 2)
                        gii = p * NDC * 2 + i
                        st["dRAS"].wait(sy, p * NDC * 2 + i)
                        if p == 0 and i < 8:
                            sy.wait_ge(sm.cOM, NDC * 2)
                        if gii >= 4:
                            sy.wait_ge(sm.cUDE, gii - 3)
                        if gii >= 8:
                            st["dRAS"].wait(sy, NDC * 2 + gii - 8)
                        st["dRAL"].inc(sy.dma_start(
                            out=sl(raccb, i, nslot=8),
                            in_=racc_scr[dc, :, qg * 512 : (qg + 1) * 512],
                        ), gii)
                        if i >= 4:
                            _sp7(i - 4)
                    for k in range(NDC * 2 - 4, NDC * 2):
                        _sp7(k)
            if PH >= 8:
                def _out8(k):
                    dck, qgk = divmod(k, 2)
                    sy.wait_ge(sm.cO8, k + 1)
                    st["dOUT"].inc(sy.dma_start(
                        out=out[:, dck, qgk * 512 : (qgk + 1) * 512],
                        in_=sl(outb8, k, nslot=4),
                    ), k)

                for i in range(NDC * 2):
                    dc, qg = divmod(i, 2)
                    sy.wait_ge(sm.cROW, 7)
                    sy.wait_ge(sm.cGM, 2 * NHC)
                    st["dRAS"].wait(sy, NDC * 2 + (NPASS - 1) * NDC * 2 + i)
                    if i >= 4:
                        sy.wait_ge(sm.cO8, i - 3)
                    st["dRAL"].inc(sy.dma_start(
                        out=sl(raccld8, i, nslot=4),
                        in_=racc_scr[dc, :, qg * 512 : (qg + 1) * 512],
                    ), NPASS * NDC * 2 + i)
                    if i >= 4:
                        _out8(i - 4)
                for k in range(NDC * 2 - 4, NDC * 2):
                    _out8(k)
                st["dOUT"].wait_all(sy, NDC * 2)
            if False:
                pass

            if PH == 1:
                st["dKS"].wait_all(sy, NCH * NKV)
            elif PH == 2:
                st["dVS"].wait_all(sy, NVI)
            elif PH == 3:
                st["dQS"].wait_all(sy, NH * 2)
            elif PH == 4:
                sy.wait_ge(sm.cATN, NI4)
            elif PH == 5:
                st["dR1S"].wait_all(sy, NDC * 2)
            elif PH == 6:
                st["dRAS"].wait_all(sy, NDC * 2)
            elif PH == 7:
                st["dRAS"].wait_all(sy, NDC * 2 * (NPASS + 1))

        @block.tensor
        def _(t):
            t.wait_ge(sm.cINI, 1)
            if PH >= 1:
                t.wait_ge(sm.dW, 16)
                for i in range(NCH * NKV + 2):
                    if i < NCH * NKV:
                        tc, kv = divmod(i, NKV)
                        if kv == 0:
                            st["dX"].wait(t, tc)
                        if i >= 4:
                            t.wait_ge(sm.cKE, i - 3)
                        for dk in range(NDK):
                            ins = t.matmul(
                                pk4[i % 4][:, 0:XCH],
                                wk_bf[:, (kv * NDK + dk) * 128 : (kv * NDK + dk) * 128 + 128],
                                xbuf[:, (tc % 2) * NDK * XCH + dk * XCH : (tc % 2) * NDK * XCH + dk * XCH + XCH],
                                start=(dk == 0), stop=(dk == NDK - 1),
                                skip_group_check=True,
                            )
                        ins.then_inc(sm.cKM, 1)
                    if i >= 2:
                        ti = i - 2
                        t.wait_ge(sm.cSH, ti + 1)
                        if ti >= 2:
                            t.wait_ge(sm.cKNR, ti - 1)
                        t.matmul(
                            (PE_ if ti % 2 == 0 else PF)[0:1, 0:XCH],
                            ones_bf[:, 0:1],
                            ksq[:, (ti % 2) * 512 : (ti % 2) * 512 + XCH],
                            start=True, stop=True, skip_group_check=True,
                        ).then_inc(sm.cKN, 1)
                    if i < NCH * NKV and i % NKV == NKV - 1:
                        t.nop().then_inc(sm.cXK, 1)

            if PH >= 2:
                t.wait_ge(sm.cKNR, NCH * NKV)
                for tcts in range(NVI // 2):
                    tc, tsub = divmod(tcts, 2)
                    if tcts % 2 == 0:
                        st["dX"].wait(t, NCH + tc)
                    if tcts == 0:
                        t.wait_ge(sm.cKE, NCH * NKV)
                    elif tcts == 1:
                        t.wait_ge(sm.dW, 16 + NKV * 16)
                    elif tcts >= 2:
                        t.wait_ge(sm.cVE, 2 * (tcts - 1))
                    for dk in range(NDK):
                        if tcts == 0 and dk % 4 == 0:
                            t.wait_ge(sm.dW, 16 + (dk // 4 + 1) * 16)
                        base = (tc % 2) * NDK * XCH + dk * XCH + tsub * 128
                        ins0 = t.matmul(
                            pk4[(tcts % 2) * 2][:],
                            xbuf[:, base : base + 128],
                            wv_bf[:, dk * 1024 : dk * 1024 + 512],
                            start=(dk == 0), stop=(dk == NDK - 1),
                            skip_group_check=True,
                        )
                        ins1 = t.matmul(
                            pk4[(tcts % 2) * 2 + 1][:],
                            xbuf[:, base : base + 128],
                            wv_bf[:, dk * 1024 + 512 : dk * 1024 + 1024],
                            start=(dk == 0), stop=(dk == NDK - 1),
                            skip_group_check=True,
                        )
                    ins0.then_inc(sm.cVM, 1)
                    ins1.then_inc(sm.cVM, 1)
                    if tcts % 2 == 1:
                        t.nop().then_inc(sm.cXK, 1)

            if PH >= 3:
                st["dX"].wait(t, 2 * NCH)
                st["dX"].wait(t, 2 * NCH + 1)
                for i in range(NH * 2 + 2):
                    if i < NH * 2 and i % 2 == 0:
                        qh = i // 2
                        st["dWQ"].wait(t, qh)
                        if qh == 0:
                            t.wait_ge(sm.cVE, NVI)
                        elif qh >= 2:
                            t.wait_ge(sm.cQE, 2 * (qh - 1))
                        for dk in range(NDK):
                            base = (qh % 2) * NDK * 128 + dk * 128
                            ins0 = t.matmul(
                                pk4[(qh % 2) * 2][:],
                                wq_bf[:, base : base + 128],
                                xq_sb[:, dk * TOK : dk * TOK + 512],
                                start=(dk == 0), stop=(dk == NDK - 1),
                                skip_group_check=True,
                            )
                            ins1 = t.matmul(
                                pk4[(qh % 2) * 2 + 1][:],
                                wq_bf[:, base : base + 128],
                                xq_sb[:, dk * TOK + 512 : dk * TOK + 1024],
                                start=(dk == 0), stop=(dk == NDK - 1),
                                skip_group_check=True,
                            )
                        ins0.then_inc(sm.cQM, 1)
                        ins1.then_inc(sm.cQM, 1)
                    if i >= 2:
                        ti = i - 2
                        t.wait_ge(sm.cQSH, ti + 1)
                        if ti >= 2:
                            t.wait_ge(sm.cQNR, ti - 1)
                        t.matmul(
                            (PE_ if ti % 2 == 0 else PF)[0:1, :],
                            ones_bf[:, 0:1], sl(ksq, ti),
                            start=True, stop=True, skip_group_check=True,
                        ).then_inc(sm.cQN, 1)
                t.wait_ge(sm.cMB, 1)
                t.matmul(
                    PF[:, 0:1], negr_f[:], strow[0:1, ST_M : ST_M + 1],
                    start=True, stop=True, skip_group_check=True,
                ).then_inc(sm.cMB, 1)

            if PH >= 4:
                t.wait_ge(sm.dMSC, 16)
                jr = t.alloc_register("jreg_t")
                t.reg_load(jr, meta_sb[0:1, 0:1])
                jv = bass_rust.RuntimeValue(val=jr, min_val=0, max_val=3)
                t.wait_ge(sm.cMB, 3)
                t.wait_ge(sm.cQE, 2 * NH)
                psc = [PA, PB]
                pav = [PC, PD]
                pnm = [PE_, PF]
                for j in nc.Switch(engines=[nc.tensor], index=[jv], n=4):
                    plan = plans[j]
                    for i in range(NI4):
                        it = plan[i]
                        kvq, qg = divmod(i, 2)
                        kv, qh4 = divmod(kvq, 4)
                        NK, ng, GG = it["NK"], it["ng"], it["GG"]
                        if qg == 0:
                            if qh4 == 0:
                                st["dKL"].wait(t, kv)
                                for _sub in range(4):
                                    st["dVL"].wait(t, kv * 4 + _sub)
                            st["dQL"].wait(t, kvq)
                        if i >= 2:
                            t.wait_ge(sm.cATN, i - 1)
                            t.wait_ge(sm.cRB, i - 1)
                        for step in range(ng + 2):
                            if step < ng:
                                gidx = step
                                gg = GG + gidx
                                if gg >= 2:
                                    kind, val = group_kind(plan, gg - 2)
                                    t.wait_ge(sm.cEXP if kind == "e" else sm.cPART, val)
                                for kk in range(2):
                                    kt = gidx * 2 + kk
                                    kb = (kv % 2) * S + kt * 128
                                    qb = (kvq % 2) * TOK + qg * 512
                                    ins = t.matmul(
                                        psc[gg % 2][:, kk * 512 : kk * 512 + 512],
                                        kbuf[:, kb : kb + 128],
                                        qbuf[:, qb : qb + 512],
                                        start=True, stop=True, skip_group_check=True,
                                    )
                                ins.then_inc(sm.cSC, 1)
                            if step >= 2:
                                gidx = step - 2
                                for kk in range(2):
                                    kt = gidx * 2 + kk
                                    slot = kt % 4
                                    if kt >= NK - 4:
                                        m = kt - (NK - 4)
                                        t.wait_ge(sm.cMSK, i * 4 + m + 1)
                                    else:
                                        t.wait_ge(sm.cEXP, it["E"] + gidx + 1)
                                    vb = (kv % 2) * NKT * 128 + kt * 128
                                    ins_s = t.matmul(
                                        pnm[i % 2][0:1, :],
                                        ones_bf[:, 0:1],
                                        probs[:, slot * 512 : slot * 512 + 512],
                                        start=(kt == 0), stop=(kt == NK - 1),
                                        skip_group_check=True,
                                    )
                                    if kt == NK - 1:
                                        ins_s.then_inc(sm.cAV, 1)
                                    ins_a = t.matmul(
                                        pav[i % 2][:],
                                        vbuf[:, vb : vb + 128],
                                        probs[:, slot * 512 : slot * 512 + 512],
                                        start=(kt == 0), stop=(kt == NK - 1),
                                        skip_group_check=True,
                                    )
                                    if kt % 2 == 1:
                                        ins_a.then_inc(sm.cPC, 1)
                        if qg == 1:
                            t.nop().then_inc(sm.cQT, 1)
                            if qh4 == 3:
                                t.nop().then_inc(sm.cHEAD, 1)
                        if i >= 1:
                            t.wait_ge(sm.cR, i)
                            t.matmul(
                                pnm[(i - 1) % 2][:], onesr_f[:],
                                rrow[0:1, ((i - 1) % 2) * 512 : ((i - 1) % 2) * 512 + 512],
                                start=True, stop=True, skip_group_check=True,
                            ).then_inc(sm.cBC, 1)
                    t.wait_ge(sm.cR, NI4)
                    t.matmul(
                        pnm[(NI4 - 1) % 2][:], onesr_f[:],
                        rrow[0:1, ((NI4 - 1) % 2) * 512 : ((NI4 - 1) % 2) * 512 + 512],
                        start=True, stop=True, skip_group_check=True,
                    ).then_inc(sm.cBC, 1)

            if PH >= 5:
                po = [PA[:, 0:512], PA[:, 512:1024], PB[:, 0:512], PB[:, 512:1024]]
                pst = [PC, PD]
                t.wait_ge(sm.cATN, 64)
                for i in range(NDC * 2 + 2):
                    if i < NDC * 2 and i % 2 == 0:
                        dc = i // 2
                        st["dWO"].wait(t, dc)
                        if i >= 4:
                            t.wait_ge(sm.cR1, i - 2)
                        for hc in range(NH):
                            wb = (dc % 2) * NH * 128 + hc * 128
                            ins0 = t.matmul(
                                po[i % 4],
                                wo_bf[:, wb : wb + 128],
                                attn_res[:, hc * 1024 : hc * 1024 + 512],
                                start=(hc == 0), stop=(hc == NH - 1),
                            )
                            ins1 = t.matmul(
                                po[(i + 1) % 4],
                                wo_bf[:, wb : wb + 128],
                                attn_res[:, hc * 1024 + 512 : hc * 1024 + 1024],
                                start=(hc == 0), stop=(hc == NH - 1),
                            )
                        ins0.then_inc(sm.cOM, 1)
                        ins1.then_inc(sm.cOM, 1)
                    if i >= 2:
                        i2 = i - 2
                        qg2 = i2 % 2
                        t.wait_ge(sm.cSQ, i2 + 1)
                        t.wait_ge(sm.cR1B, i2 + 1)
                        t.matmul(
                            pst[qg2][0:1, :], ones_bf[:, 0:1], sl(r1bf, i2),
                            start=(i2 // 2 == 0), stop=(i2 // 2 == NDC - 1),
                            skip_group_check=True,
                        )
                        t.matmul(
                            pst[qg2][32:33, :], ones_bf[:, 0:1], sl(sqbf, i2),
                            start=(i2 // 2 == 0), stop=(i2 // 2 == NDC - 1),
                            skip_group_check=True,
                        ).then_inc(sm.cST, 1)
                t.wait_ge(sm.cROW, 3)
                for qg in range(2):
                    t.matmul(
                        (PE_ if qg == 0 else PF)[:],
                        onesr_f[:], strow[0:1, ST_RSTD[qg] : ST_RSTD[qg] + 512],
                        start=True, stop=True, skip_group_check=True,
                    )
                    ins = t.matmul(
                        PA[:, qg * 512 : qg * 512 + 512],
                        onesr_f[:], strow[0:1, ST_B[qg] : ST_B[qg] + 512],
                        start=True, stop=True, skip_group_check=True,
                    )
                ins.then_inc(sm.cABC, 1)

            if PH >= 7:
                pg = [PA[:, 0:512], PA[:, 512:1024]]
                pu = [PB[:, 0:512], PB[:, 512:1024]]
                pd_ = [PC, PD]
                pud = PE_
                t.wait_ge(sm.cHID, NDC * 2)
                t.wait_ge(sm.cROW, 4)
                t.wait_ge(sm.dMSC, 64)
                for qg in range(2):
                    for dk in range(NDK):
                        t.matmul(
                            PA[32 * qg : 32 * qg + 16, 0:512],
                            ga_bf[:, dk * RK : dk * RK + RK],
                            x1_bf[:, dk * TOK + qg * 512 : dk * TOK + qg * 512 + 512],
                            start=(dk == 0), stop=(dk == NDK - 1),
                            skip_group_check=True,
                        )
                    for dk in range(NDK):
                        ins = t.matmul(
                            PB[32 * qg : 32 * qg + 16, 0:512],
                            ua_bf[:, dk * RK : dk * RK + RK],
                            x1_bf[:, dk * TOK + qg * 512 : dk * TOK + qg * 512 + 512],
                            start=(dk == 0), stop=(dk == NDK - 1),
                            skip_group_check=True,
                        )
                ins.then_inc(sm.cUG, 1)
                t.wait_ge(sm.cUG, 2)

                def _ud_mm(h0x, hcx):
                    hgix = hcx - h0x
                    for qg in range(2):
                        gi = hcx * 2 + qg
                        t.wait_ge(sm.cHID, NDC * 2 + gi + 1)
                        ins_ud = t.matmul(
                            pud[32 * qg : 32 * qg + 16, :],
                            da_bf[:, hcx * RK : hcx * RK + RK],
                            hidden[:, hgix * 1024 + qg * 512 : hgix * 1024 + qg * 512 + 512],
                            start=(hcx == 0), stop=(hcx == NHC - 1),
                            skip_group_check=True,
                        )
                        if hcx == NHC - 1 and qg == 1:
                            ins_ud.then_inc(sm.cUG, 1)

                for p, (h0, h1) in enumerate(HG_BOUNDS):
                    for hgi in range(h1 - h0):
                        hc = h0 + hgi
                        st["dG"].wait(t, hc)
                        if hc >= 1:
                            t.wait_ge(sm.cSIL, 2 * hc)
                            t.wait_ge(sm.cHID, NDC * 2 + 2 * hc)
                        lastg = [None, None]
                        for dk in range(NDK + 1):
                            gb = (hc % 2) * (NDK + 1) * 128 + dk * 128
                            for qg in range(2):
                                rhs = (
                                    x1_bf[:, dk * TOK + qg * 512 : dk * TOK + qg * 512 + 512]
                                    if dk < NDK
                                    else ug_full[:, qg * 512 : qg * 512 + 512]
                                )
                                lastg[qg] = t.matmul(
                                    pg[qg], gbuf[:, gb : gb + 128], rhs,
                                    start=(dk == 0), stop=(dk == NDK),
                                    skip_group_check=True,
                                )
                        lastg[0].then_inc(sm.cGM, 1)
                        lastg[1].then_inc(sm.cGM, 1)
                        st["dU"].wait(t, hc)
                        lastu = [None, None]
                        for dk in range(NDK + 1):
                            ub = (hc % 2) * (NDK + 1) * 128 + dk * 128
                            for qg in range(2):
                                rhs = (
                                    x1_bf[:, dk * TOK + qg * 512 : dk * TOK + qg * 512 + 512]
                                    if dk < NDK
                                    else uu_full[:, qg * 512 : qg * 512 + 512]
                                )
                                lastu[qg] = t.matmul(
                                    pu[qg], ubuf[:, ub : ub + 128], rhs,
                                    start=(dk == 0), stop=(dk == NDK),
                                    skip_group_check=True,
                                )
                        lastu[0].then_inc(sm.cUM, 1)
                        lastu[1].then_inc(sm.cUM, 1)
                        if hgi >= 1:
                            _ud_mm(h0, hc - 1)
                    _ud_mm(h0, h1 - 1)
                    if p == NPASS - 1:
                        t.wait_ge(sm.cUG, 4)
                    ng2 = (h1 - h0) + (1 if p == NPASS - 1 else 0)
                    for i in range(NDC * 2):
                        dc, qg = divmod(i, 2)
                        gi = p * NDC + dc
                        gii = p * NDC * 2 + i
                        if qg == 0:
                            st["dD"].wait(t, gi)
                        if gii >= 2:
                            t.wait_ge(sm.cUDE, gii - 1)
                        for hgj in range(ng2):
                            rhs = (
                                hidden[:, hgj * 1024 + qg * 512 : hgj * 1024 + qg * 512 + 512]
                                if hgj < h1 - h0
                                else ud_full[:, qg * 512 : qg * 512 + 512]
                            )
                            db = (gi % 2) * 17 * 128 + hgj * 128
                            ins = t.matmul(
                                pd_[gii % 2][:], dbuf[:, db : db + 128], rhs,
                                start=(hgj == 0), stop=(hgj == ng2 - 1),
                                skip_group_check=True,
                            )
                        ins.then_inc(sm.cDM, 1)
                        if p == NPASS - 1 and i >= 2:
                            i2 = i - 2
                            qg2 = i2 % 2
                            if i2 == 0:
                                t.wait_ge(sm.cSIL, NHC * 2)
                            t.wait_ge(sm.cST2, i2 + 1)
                            t.wait_ge(sm.cSQ, NDC * 2 + i2 + 1)
                            t.matmul(
                                (PA if qg2 == 0 else PB)[0:1, 0:512],
                                ones_bf[:, 0:1], sl(r1bf, i2),
                                start=(i2 // 2 == 0), stop=(i2 // 2 == NDC - 1),
                                skip_group_check=True,
                            )
                            t.matmul(
                                (PA if qg2 == 0 else PB)[32:33, 0:512],
                                ones_bf[:, 0:1], sl(sqbf, i2),
                                start=(i2 // 2 == 0), stop=(i2 // 2 == NDC - 1),
                                skip_group_check=True,
                            ).then_inc(sm.cST, 1)
                for i2 in range(NDC * 2 - 2, NDC * 2):
                    qg2 = i2 % 2
                    t.wait_ge(sm.cST2, i2 + 1)
                    t.wait_ge(sm.cSQ, NDC * 2 + i2 + 1)
                    t.matmul(
                        (PA if qg2 == 0 else PB)[0:1, 0:512],
                        ones_bf[:, 0:1], sl(r1bf, i2),
                        start=False, stop=(i2 // 2 == NDC - 1), skip_group_check=True,
                    )
                    t.matmul(
                        (PA if qg2 == 0 else PB)[32:33, 0:512],
                        ones_bf[:, 0:1], sl(sqbf, i2),
                        start=False, stop=(i2 // 2 == NDC - 1), skip_group_check=True,
                    ).then_inc(sm.cST, 1)
                t.wait_ge(sm.cROW, 6)
                t.wait_ge(sm.cUDE, NPASS * NDC * 2)
                for qg in range(2):
                    t.matmul(
                        (PC if qg == 0 else PD)[:],
                        onesr_f[:], strow[0:1, ST_RSTD[qg] : ST_RSTD[qg] + 512],
                        start=True, stop=True, skip_group_check=True,
                    )
                    ins = t.matmul(
                        (PE_ if qg == 0 else PF)[:],
                        onesr_f[:], strow[0:1, ST_B[qg] : ST_B[qg] + 512],
                        start=True, stop=True, skip_group_check=True,
                    )
                ins.then_inc(sm.cABC2, 1)

        @block.scalar
        def _(s):
            if PH >= 1:
                for i in range(NCH * NKV):
                    s.wait_ge(sm.cKM, i + 1)
                    if i >= 2:
                        s.wait_ge(sm.cKE, i - 1)
                        s.wait_ge(sm.cKN, i - 1)
                    s.copy(sh_t[0:64, (i % 2) * 512 : (i % 2) * 512 + XCH], pk4[i % 4][64:128, 0:XCH])
                    s.copy(sh_t[64:128, (i % 2) * 512 : (i % 2) * 512 + XCH], pk4[i % 4][0:64, 0:XCH])
                    s.activation(
                        ksq[:, (i % 2) * 512 : (i % 2) * 512 + XCH], pk4[i % 4][:, 0:XCH], AF.Square
                    ).then_inc(sm.cSH, 1)
            if PH >= 3:
                st["dVS"].wait_all(s, NVI)
                for i in range(NH * 2):
                    s.wait_ge(sm.cQM, i + 1)
                    if i >= 2:
                        s.wait_ge(sm.cQE, i - 1)
                        s.wait_ge(sm.cQN, i - 1)
                    pq = pk4[(i // 2 % 2) * 2 + i % 2]
                    s.copy(sh_t[0:64, (i % 2) * 512 : (i % 2 + 1) * 512], pq[64:128, :])
                    s.copy(sh_t[64:128, (i % 2) * 512 : (i % 2 + 1) * 512], pq[0:64, :])
                    s.activation(sl(ksq, i), pq[:], AF.Square).then_inc(sm.cQSH, 1)
                s.wait_ge(sm.cKNR, NCH * NKV)
                s.wait_ge(sm.cQNR, NH * 2)
                s.activation(strow[0:1, ST_LNK : ST_LNK + 1], strow[0:1, ST_KMAX : ST_KMAX + 1], AF.Ln)
                s.activation(
                    strow[0:1, ST_LNQ : ST_LNQ + 1],
                    strow[0:1, ST_QMAX : ST_QMAX + 1], AF.Ln,
                ).then_inc(sm.cSH, 2)
                s.wait_ge(sm.cROW, 1)
                s.activation(
                    strow[0:1, ST_M : ST_M + 1], strow[0:1, ST_LNS : ST_LNS + 1],
                    AF.Exp, bias=strow[0:1, ST_LNML : ST_LNML + 1], scale=0.5,
                ).then_inc(sm.cMB, 1)
                s.wait_ge(sm.cMB, 2)
                s.copy(bias_sb[:, 0:1], PF[:, 0:1]).then_inc(sm.cMB, 1)

            if PH >= 4:
                s.wait_ge(sm.dMSC, 16)
                jr = s.alloc_register("jreg_s")
                s.reg_load(jr, meta_sb[0:1, 0:1])
                jv = bass_rust.RuntimeValue(val=jr, min_val=0, max_val=3)
                pnm = [PE_, PF]
                for j in nc.Switch(engines=[nc.scalar], index=[jv], n=4):
                    plan = plans[j]
                    for i in range(NI4 + 1):
                        if i < NI4:
                            it = plan[i]
                            NK, ng, GG = it["NK"], it["ng"], it["GG"]
                            for gidx in range(ng):
                                gg = GG + gidx
                                s.wait_ge(sm.cSC, gg + 1)
                                if gg >= 2:
                                    s.wait_ge(sm.cPC, gg - 1)
                                ins = s.activation(
                                    probs[:, (gg % 2) * 1024 : (gg % 2) * 1024 + 1024],
                                    (PA if gg % 2 == 0 else PB)[:],
                                    AF.Exp, bias=bias_sb[:, 0:1], scale=SCALE,
                                )
                                if gidx < ng - 2:
                                    ins.then_inc(sm.cEXP, 1)
                                else:
                                    ins.then_inc(sm.cPART, 1)
                        if i >= 1:
                            ii = i - 1
                            s.wait_ge(sm.cAV, ii + 1)
                            s.copy(
                                lrow[0:1, (ii % 2) * 512 : (ii % 2) * 512 + 512],
                                pnm[ii % 2][0:1, :],
                            )
                            s.activation(
                                strow[0:1, ST_TMP : ST_TMP + 512],
                                lrow[0:1, (ii % 2) * 512 : (ii % 2) * 512 + 512],
                                AF.Ln,
                            )
                            s.activation(
                                rrow[0:1, (ii % 2) * 512 : (ii % 2) * 512 + 512],
                                strow[0:1, ST_TMP : ST_TMP + 512],
                                AF.Exp, scale=-1.0,
                            ).then_inc(sm.cR, 1)

            if PH >= 5:
                for i in range(NDC * 2):
                    s.wait_ge(sm.cR1, i + 1)
                    if i >= 2:
                        s.wait_ge(sm.cST, i - 1)
                    s.activation(
                        sl(sqbf, i), sl(r1buf, i, nslot=4), AF.Square
                    ).then_inc(sm.cSQ, 1)
                s.wait_ge(sm.cROW, 2)
                for qg in range(2):
                    s.activation(
                        strow[0:1, ST_TMP : ST_TMP + 512],
                        rrow[0:1, qg * 512 : qg * 512 + 512],
                        AF.Ln, bias=strow[0:1, ST_EPS : ST_EPS + 1],
                    )
                    ins = s.activation(
                        strow[0:1, ST_RSTD[qg] : ST_RSTD[qg] + 512],
                        strow[0:1, ST_TMP : ST_TMP + 512],
                        AF.Exp, scale=-0.5,
                    )
                ins.then_inc(sm.cROW2, 1)

            if PH >= 6:
                for i in range(NDC * 2):
                    dc, qg = divmod(i, 2)
                    if i == 0:
                        s.wait_ge(sm.cOM, NDC * 2)
                    s.wait_ge(sm.cHID, i + 1)
                    s.copy(
                        x1_bf[:, dc * TOK + qg * 512 : dc * TOK + qg * 512 + 512],
                        sl(x1f32, i, nslot=4),
                    ).then_inc(sm.cX1, 1)

            if PH >= 7:
                for gi in range(NHC * 2):
                    s.wait_ge(sm.cGM, gi + 1)
                    if gi >= 2:
                        s.wait_ge(sm.cHID, NDC * 2 + gi - 1)
                    s.activation(
                        sl(tsil, gi),
                        PA[:, (gi % 2) * 512 : (gi % 2) * 512 + 512],
                        AF.Silu,
                    ).then_inc(sm.cSIL, 1)
                for i in range(NDC * 2):
                    gii = (NPASS - 1) * NDC * 2 + i
                    s.wait_ge(sm.cUDE, gii + 1)
                    if i >= 2:
                        s.wait_ge(sm.cST, NDC * 2 + i - 1)
                    s.activation(
                        sl(sqbf, i), sl(raccb, i, nslot=8), AF.Square
                    ).then_inc(sm.cSQ, 1)
                s.wait_ge(sm.cROW, 5)
                for qg in range(2):
                    s.activation(
                        strow[0:1, ST_TMP : ST_TMP + 512],
                        rrow[0:1, qg * 512 : qg * 512 + 512],
                        AF.Ln, bias=strow[0:1, ST_EPS : ST_EPS + 1],
                    )
                    ins = s.activation(
                        strow[0:1, ST_RSTD[qg] : ST_RSTD[qg] + 512],
                        strow[0:1, ST_TMP : ST_TMP + 512],
                        AF.Exp, scale=-0.5,
                    )
                ins.then_inc(sm.cROW2, 1)

        @block.vector
        def _(v):
            v.memset(ones_bf[:], 1.0)
            v.memset(onesr_f[:], 1.0)
            v.memset(negr_f[:], -1.0)
            v.memset(strow[0:1, 0:8], 0.0)
            v.memset(strow[0:1, ST_LNML : ST_LNML + 1], LNML)
            v.memset(strow[0:1, ST_EPS : ST_EPS + 1], 1e-5)
            v.memset(ug_full[:], 0.0)
            v.memset(uu_full[:], 0.0)
            v.memset(ud_full[:], 0.0).then_inc(sm.cINI, 1)
            if PH >= 1:
                v.wait_ge(sm.dTAB, 32)
                for i in range(NCH * NKV + 2):
                    if i < NCH * NKV:
                        tc, kv = divmod(i, NKV)
                        v.wait_ge(sm.cSH, i + 1)
                        if i >= 2:
                            st["dKS"].wait(v, i - 2)
                        w0 = (i % 2) * 512
                        v.tensor_mul(
                            t1_t[:, w0 : w0 + XCH], pk4[i % 4][:, 0:XCH],
                            ck_sb[:, tc * XCH : (tc + 1) * XCH],
                        )
                        v.tensor_mul(
                            t2_t[:, w0 : w0 + XCH], sh_t[:, w0 : w0 + XCH],
                            sk_sb[:, tc * XCH : (tc + 1) * XCH],
                        )
                        v.tensor_add(
                            kout[:, w0 : w0 + XCH], t1_t[:, w0 : w0 + XCH],
                            t2_t[:, w0 : w0 + XCH],
                        ).then_inc(sm.cKE, 1)
                    if i >= 2:
                        ti = i - 2
                        v.wait_ge(sm.cKN, ti + 1)
                        v.reduce_max(
                            strow[0:1, ST_RT0 + (ti % 2) : ST_RT0 + (ti % 2) + 1],
                            (PE_ if ti % 2 == 0 else PF)[0:1, 0:XCH],
                            mybir.AxisListType.X,
                        )
                        v.tensor_max(
                            strow[0:1, ST_KMAX : ST_KMAX + 1],
                            strow[0:1, ST_KMAX : ST_KMAX + 1],
                            strow[0:1, ST_RT0 + (ti % 2) : ST_RT0 + (ti % 2) + 1],
                        ).then_inc(sm.cKNR, 1)
            if PH >= 2:
                for i in range(NVI):
                    v.wait_ge(sm.cVM, i + 1)
                    if i >= 2:
                        st["dVS"].wait(v, i - 2)
                    v.tensor_copy(sl(vout, i), pk4[i % 4][:]).then_inc(sm.cVE, 1)
            if PH >= 3:
                v.wait_ge(sm.dTAB, 64)
                for i in range(NH * 2 + 2):
                    if i < NH * 2:
                        qh, qg = divmod(i, 2)
                        v.wait_ge(sm.cQSH, i + 1)
                        if i >= 2:
                            st["dQS"].wait(v, i - 2)
                        v.tensor_mul(
                            sl(t1_t, i), pk4[(i // 2 % 2) * 2 + i % 2][:],
                            cq_sb[:, qg * 512 : (qg + 1) * 512],
                        )
                        v.tensor_mul(
                            sl(t2_t, i), sl(sh_t, i), sq_sb[:, qg * 512 : (qg + 1) * 512]
                        )
                        v.tensor_add(sl(kout, i), sl(t1_t, i), sl(t2_t, i)).then_inc(sm.cQE, 1)
                    if i >= 2:
                        ti = i - 2
                        v.wait_ge(sm.cQN, ti + 1)
                        v.reduce_max(
                            strow[0:1, ST_RT0 + 2 + (ti % 2) : ST_RT0 + 2 + (ti % 2) + 1],
                            (PE_ if ti % 2 == 0 else PF)[0:1, :],
                            mybir.AxisListType.X,
                        )
                        v.tensor_max(
                            strow[0:1, ST_QMAX : ST_QMAX + 1],
                            strow[0:1, ST_QMAX : ST_QMAX + 1],
                            strow[0:1, ST_RT0 + 2 + (ti % 2) : ST_RT0 + 2 + (ti % 2) + 1],
                        ).then_inc(sm.cQNR, 1)
                v.wait_ge(sm.cSH, NCH * NKV + 2)
                v.tensor_add(
                    strow[0:1, ST_LNS : ST_LNS + 1],
                    strow[0:1, ST_LNK : ST_LNK + 1],
                    strow[0:1, ST_LNQ : ST_LNQ + 1],
                ).then_inc(sm.cROW, 1)

            if PH >= 4:
                st["dQS"].wait_all(v, NH * 2)
                v.wait_ge(sm.dMSC, 80)
                pav = [PC, PD]
                pnm = [PE_, PF]
                for i in range(NI4 + 1):
                    if i < NI4:
                        for half in range(2):
                            v.wait_ge(sm.cPART, 2 * i + half + 1)
                            for m in ((0, 1) if half == 0 else (2, 3)):
                                v.tensor_mul(
                                    probs[:, m * 512 : (m + 1) * 512],
                                    probs[:, m * 512 : (m + 1) * 512],
                                    mask_sb[:, m * 512 : (m + 1) * 512],
                                ).then_inc(sm.cMSK, 1)
                    if i >= 1:
                        ii = i - 1
                        qh = ii // 2
                        qg = ii % 2
                        v.wait_ge(sm.cBC, ii + 1)
                        v.tensor_copy(sl(rb_sb, ii), pnm[ii % 2][:]).then_inc(sm.cRB, 1)
                        v.tensor_mul(
                            attn_res[:, qh * 1024 + qg * 512 : qh * 1024 + qg * 512 + 512],
                            sl(rb_sb, ii),
                            pav[ii % 2][:],
                        ).then_inc(sm.cATN, 1)

            if PH >= 5:
                po = [PA[:, 0:512], PA[:, 512:1024], PB[:, 0:512], PB[:, 512:1024]]
                for i in range(NDC * 2):
                    dc, qg = divmod(i, 2)
                    v.wait_ge(sm.cOM, i + 1)
                    st["dXR"].wait(v, i)
                    if i >= 4:
                        st["dR1S"].wait(v, i - 4)
                        v.wait_ge(sm.cSQ, i - 3)
                    v.tensor_add(
                        sl(r1buf, i, nslot=4), sl(xresb, i, nslot=4), po[i % 4]
                    ).then_inc(sm.cR1, 1)
                    if i >= 2:
                        v.wait_ge(sm.cST, i - 1)
                    v.tensor_copy(sl(r1bf, i), sl(r1buf, i, nslot=4)).then_inc(sm.cR1B, 1)
                v.wait_ge(sm.cST, NDC * 2)
                for qg in range(2):
                    pstq = PC if qg == 0 else PD
                    v.tensor_scalar_mul(
                        lrow[0:1, qg * 512 : qg * 512 + 512], pstq[0:1, :], 1.0 / D
                    )
                    v.tensor_scalar_mul(
                        rrow[0:1, qg * 512 : qg * 512 + 512], pstq[32:33, :], 1.0 / D
                    )
                for qg in range(2):
                    v.tensor_mul(
                        strow[0:1, ST_TMP : ST_TMP + 512],
                        lrow[0:1, qg * 512 : qg * 512 + 512],
                        lrow[0:1, qg * 512 : qg * 512 + 512],
                    )
                    ins = v.tensor_sub(
                        rrow[0:1, qg * 512 : qg * 512 + 512],
                        rrow[0:1, qg * 512 : qg * 512 + 512],
                        strow[0:1, ST_TMP : ST_TMP + 512],
                    )
                ins.then_inc(sm.cROW, 1)
                v.wait_ge(sm.cROW2, 1)
                for qg in range(2):
                    ins = v.tensor_mul(
                        strow[0:1, ST_B[qg] : ST_B[qg] + 512],
                        lrow[0:1, qg * 512 : qg * 512 + 512],
                        strow[0:1, ST_RSTD[qg] : ST_RSTD[qg] + 512],
                    )
                ins.then_inc(sm.cROW, 1)
                v.wait_ge(sm.cABC, 1)
                v.tensor_copy(a_bc[:, 0:512], PE_[:])
                v.tensor_copy(a_bc[:, 512:1024], PF[:])
                v.tensor_copy(b_bc[:, 0:512], PA[:, 0:512])
                v.tensor_copy(b_bc[:, 512:1024], PA[:, 512:1024]).then_inc(sm.cROW, 1)

            if PH >= 6:
                for i in range(NDC * 2):
                    dc, qg = divmod(i, 2)
                    st["dR1L"].wait(v, i)
                    if i >= 4:
                        st["dRAS"].wait(v, i - 4)
                        v.wait_ge(sm.cX1, i - 3)
                    v.tensor_mul(
                        sl(x1f32, i, nslot=4),
                        sl(r1ld, i, nslot=4),
                        a_bc[:, qg * 512 : (qg + 1) * 512],
                    )
                    v.tensor_sub(
                        sl(x1f32, i, nslot=4),
                        sl(x1f32, i, nslot=4),
                        b_bc[:, qg * 512 : (qg + 1) * 512],
                    ).then_inc(sm.cHID, 1)

            if PH >= 7:
                v.wait_ge(sm.cUG, 1)
                for qg in range(2):
                    v.tensor_copy(
                        ug_full[0:16, qg * 512 : (qg + 1) * 512],
                        PA[32 * qg : 32 * qg + 16, 0:512],
                    )
                    ins = v.tensor_copy(
                        uu_full[0:16, qg * 512 : (qg + 1) * 512],
                        PB[32 * qg : 32 * qg + 16, 0:512],
                    )
                ins.then_inc(sm.cUG, 1)
                for p, (h0, h1) in enumerate(HG_BOUNDS):
                    for hgi in range(h1 - h0):
                        hc = h0 + hgi
                        for qg in range(2):
                            gi = hc * 2 + qg
                            v.wait_ge(sm.cSIL, gi + 1)
                            v.wait_ge(sm.cUM, gi + 1)
                            if p >= 1 and hgi == 0 and qg == 0:
                                v.wait_ge(sm.cDM, p * NDC * 2)
                            v.tensor_mul(
                                hidden[:, hgi * 1024 + qg * 512 : hgi * 1024 + qg * 512 + 512],
                                sl(tsil, gi),
                                PB[:, (gi % 2) * 512 : (gi % 2) * 512 + 512],
                            ).then_inc(sm.cHID, 1)
                    if p == NPASS - 1:
                        v.wait_ge(sm.cUG, 3)
                        v.tensor_copy(ud_full[0:16, 0:512], PE_[0:16, :])
                        v.tensor_copy(ud_full[0:16, 512:1024], PE_[32:48, :]).then_inc(sm.cUG, 1)
                    for i in range(NDC * 2):
                        gii = p * NDC * 2 + i
                        v.wait_ge(sm.cDM, gii + 1)
                        st["dRAL"].wait(v, gii)
                        v.tensor_add(
                            sl(raccb, i, nslot=8),
                            sl(raccb, i, nslot=8),
                            (PC if gii % 2 == 0 else PD)[:],
                        ).then_inc(sm.cUDE, 1)
                        if p == NPASS - 1:
                            if i >= 2:
                                v.wait_ge(sm.cST, NDC * 2 + i - 1)
                            v.tensor_copy(
                                sl(r1bf, i), sl(raccb, i, nslot=8)
                            ).then_inc(sm.cST2, 1)
                v.wait_ge(sm.cST, NDC * 2 + NDC * 2)
                for qg in range(2):
                    pstq = PA if qg == 0 else PB
                    v.tensor_scalar_mul(
                        lrow[0:1, qg * 512 : qg * 512 + 512], pstq[0:1, 0:512], 1.0 / D
                    )
                    v.tensor_scalar_mul(
                        rrow[0:1, qg * 512 : qg * 512 + 512], pstq[32:33, 0:512], 1.0 / D
                    )
                for qg in range(2):
                    v.tensor_mul(
                        strow[0:1, ST_TMP : ST_TMP + 512],
                        lrow[0:1, qg * 512 : qg * 512 + 512],
                        lrow[0:1, qg * 512 : qg * 512 + 512],
                    )
                    ins = v.tensor_sub(
                        rrow[0:1, qg * 512 : qg * 512 + 512],
                        rrow[0:1, qg * 512 : qg * 512 + 512],
                        strow[0:1, ST_TMP : ST_TMP + 512],
                    )
                ins.then_inc(sm.cROW, 1)
                v.wait_ge(sm.cROW2, 2)
                for qg in range(2):
                    ins = v.tensor_mul(
                        strow[0:1, ST_B[qg] : ST_B[qg] + 512],
                        lrow[0:1, qg * 512 : qg * 512 + 512],
                        strow[0:1, ST_RSTD[qg] : ST_RSTD[qg] + 512],
                    )
                ins.then_inc(sm.cROW, 1)
                v.wait_ge(sm.cABC2, 1)
                v.tensor_copy(a_bc[:, 0:512], PC[:])
                v.tensor_copy(a_bc[:, 512:1024], PD[:])
                v.tensor_copy(b_bc[:, 0:512], PE_[:])
                v.tensor_copy(b_bc[:, 512:1024], PF[:]).then_inc(sm.cROW, 1)

            if PH >= 8:
                for i in range(NDC * 2):
                    dc, qg = divmod(i, 2)
                    st["dRAL"].wait(v, NPASS * NDC * 2 + i)
                    if i >= 4:
                        st["dOUT"].wait(v, i - 4)
                    v.tensor_mul(
                        sl(outb8, i, nslot=4),
                        sl(raccld8, i, nslot=4),
                        a_bc[:, qg * 512 : (qg + 1) * 512],
                    )
                    v.tensor_sub(
                        sl(outb8, i, nslot=4),
                        sl(outb8, i, nslot=4),
                        b_bc[:, qg * 512 : (qg + 1) * 512],
                    ).then_inc(sm.cO8, 1)

    return nc



_NC_CACHE = {}


def _host_prep(inputs):
    x = np.asarray(inputs["L0_attn_x"], dtype=np.float32)
    Wq = np.asarray(inputs["L0_attn_Wq"], dtype=np.float32)
    Wk = np.asarray(inputs["L0_attn_Wk"], dtype=np.float32)
    Wv = np.asarray(inputs["L0_attn_Wv"], dtype=np.float32)
    Wo = np.asarray(inputs["L0_attn_Wo"], dtype=np.float32)
    gwF = np.asarray(inputs["L0_ffn_gate_w"], dtype=np.float32)
    uwF = np.asarray(inputs["L0_ffn_up_w"], dtype=np.float32)
    dwF = np.asarray(inputs["L0_ffn_down_w"], dtype=np.float32)
    gaF = np.asarray(inputs["L0_ffn_gate_a"], dtype=np.float32)
    gbF = np.asarray(inputs["L0_ffn_gate_b"], dtype=np.float32)
    uaF = np.asarray(inputs["L0_ffn_up_a"], dtype=np.float32)
    ubF = np.asarray(inputs["L0_ffn_up_b"], dtype=np.float32)
    daF = np.asarray(inputs["L0_ffn_down_a"], dtype=np.float32)
    dbF = np.asarray(inputs["L0_ffn_down_b"], dtype=np.float32)

    xfm = [np.ascontiguousarray(x[b].T) for b in range(2)]

    wq_l = np.ascontiguousarray(Wq.reshape(NDK, 128, NH, 128).transpose(1, 2, 0, 3))
    wk_l = np.ascontiguousarray(Wk.reshape(NDK, 128, NKV, 128).transpose(1, 2, 0, 3))
    wv_l = np.ascontiguousarray(Wv.reshape(NDK, 128, 1024).transpose(1, 0, 2))
    wo_l = np.ascontiguousarray(Wo.reshape(NH, 128, NDC, 128).transpose(1, 2, 0, 3))

    def _wpad(wT, bT):
        t = wT.reshape(NDK, 128, NHC, 128).transpose(1, 2, 0, 3)
        pad = np.zeros((128, NHC, 1, 128), np.float32)
        pad[0:RK, :, 0, :] = bT.reshape(RK, NHC, 128).copy()
        return np.ascontiguousarray(np.concatenate([t, pad], axis=2))

    gw_l = _wpad(gwF.T, gbF.T)
    uw_l = _wpad(uwF.T, ubF.T)
    dt_ = dwF.T.reshape(NHC, 128, NDC, 128).transpose(1, 2, 0, 3)
    dpad = np.zeros((128, NDC, 1, 128), np.float32)
    dpad[0:RK, :, 0, :] = dbF.T.reshape(RK, NDC, 128)
    dw_l = np.ascontiguousarray(np.concatenate([dt_, dpad], axis=2))

    ga_l = np.ascontiguousarray(gaF.T.reshape(NDK, 128, RK).transpose(1, 0, 2))
    ua_l = np.ascontiguousarray(uaF.T.reshape(NDK, 128, RK).transpose(1, 0, 2))
    da_l = np.ascontiguousarray(daF.T.reshape(NHC, 128, RK).transpose(1, 0, 2))

    inv = 1.0 / (10000.0 ** (np.arange(0, 64, dtype=np.float64) / 64.0))
    pos = np.arange(S, dtype=np.float64)
    fr = np.outer(inv, pos)
    cos = np.cos(fr).astype(np.float32)
    sin = np.sin(fr).astype(np.float32)
    ck_l = np.ascontiguousarray(np.concatenate([cos, cos], 0))
    sk_l = np.ascontiguousarray(np.concatenate([-sin, sin], 0))

    mk = np.zeros((128, 4, 512), np.float32)
    pp = np.arange(128)[:, None]
    ff = np.arange(512)[None, :]
    for m in range(4):
        mk[:, m, :] = (128 * m + pp <= ff).astype(np.float32)
    mk_l = np.ascontiguousarray(mk.astype(ml_dtypes.bfloat16))

    in_maps = []
    tok_slices = []
    for c in range(8):
        b, j = divmod(c, 4)
        t0a = 512 * j
        t1a = 512 * (7 - j)
        tok_slices.append((b, t0a, t1a))
        xr = np.concatenate(
            [xfm[b][:, t0a : t0a + 512], xfm[b][:, t1a : t1a + 512]], axis=1
        )
        xres_l = np.ascontiguousarray(xr.reshape(NDC, 128, TOK).transpose(1, 0, 2))
        cq_l = np.ascontiguousarray(
            np.concatenate([ck_l[:, t0a : t0a + 512], ck_l[:, t1a : t1a + 512]], axis=1)
        )
        sq_l = np.ascontiguousarray(
            np.concatenate([sk_l[:, t0a : t0a + 512], sk_l[:, t1a : t1a + 512]], axis=1)
        )
        in_maps.append(
            dict(
                xfm=xfm[b], xres=xres_l, wq=wq_l, wk=wk_l, wv=wv_l, wo=wo_l,
                gw=gw_l, uw=uw_l, dw=dw_l, ga=ga_l, ua=ua_l, da=da_l,
                ck=ck_l, sk=sk_l, cq=cq_l, sq=sq_l, maskin=mk_l,
                meta=np.array([[j, 0, 0, 0]], np.int32),
            )
        )
    return in_maps, tok_slices


def kernel(**inputs):
    in_maps, tok_slices = _host_prep(inputs)
    if "nc" not in _NC_CACHE:
        _NC_CACHE["nc"] = build_program()
    nc = _NC_CACHE["nc"]
    res = run_bass_kernel_spmd(nc, in_maps, core_ids=list(range(8)))
    out = np.empty((2, S, D), np.float32)
    for c in range(8):
        b, t0a, t1a = tok_slices[c]
        o = np.asarray(res.results[c]["out"])
        om = o.transpose(1, 0, 2).reshape(D, TOK)
        out[b, t0a : t0a + 512, :] = om[:, 0:512].T
        out[b, t1a : t1a + 512, :] = om[:, 512:1024].T
    return out


# version 06
# speedup vs baseline: 1.0060x; 1.0060x over previous
import contextlib
import os
import numpy as np
import ml_dtypes
import bass_rust
import concourse.bass as bass
import concourse.mybir as mybir
from concourse.bass_utils import run_bass_kernel_spmd

F32 = mybir.dt.float32
BF16 = mybir.dt.bfloat16
I32 = mybir.dt.int32
U8 = mybir.dt.uint8
AF = mybir.ActivationFunctionType
OP = mybir.AluOpType

S = 4096
D = 4096
NH = 32
NKV = 8
DH = 11008
RK = 16
NDK = 32
NHC = 86
NDC = 32
XCH = 256
NCH = S // XCH
NVI = NCH * 4
NKT = 32
TOK = 1024
SCALE = float(1.0 / np.sqrt(128.0))
LNML = float(-0.5 * np.log(128.0))
HG_BOUNDS = [(0, 16), (16, 32), (32, 48), (48, 64), (64, 80), (80, 86)]
NPASS = len(HG_BOUNDS)
KB = 1024

ST_KMAX = 0
ST_QMAX = 1
ST_LNK = 2
ST_LNQ = 3
ST_LNS = 4
ST_M = 5
ST_LNML = 6
ST_EPS = 7
ST_RT0 = 8
ST_TMP = 12
ST_RSTD = [524, 1036]
ST_B = [1548, 2060]
ST_LEN = 2572

DMA_STREAMS = [
    ("dX", 2), ("dWQ", 2), ("dWO", 2), ("dG", 2), ("dU", 2), ("dD", 2),
    ("dVL", 4), ("dKL", 2), ("dQL", 2), ("dXR", 4), ("dKS", 2), ("dVS", 2),
    ("dQS", 2), ("dR1S", 4), ("dR1L", 4), ("dRAL", 4), ("dRAS", 4), ("dOUT", 2),
]
SEM_NAMES = [
    "dW", "dMSC", "dTAB",
    "cINI",
    "cKM", "cKN", "cXK", "cVM", "cQM", "cQN", "cMB", "cSC", "cPC", "cAV",
    "cBC", "cQT", "cHEAD", "cOM", "cST", "cABC", "cUG", "cGM", "cUM", "cDM",
    "cABC2",
    "cSH", "cQSH", "cEXP", "cPART", "cR", "cSQ", "cX1", "cSIL", "cROW2",
    "cKE", "cKNR", "cVE", "cQE", "cQNR", "cMSK", "cRB", "cATN", "cR1",
    "cR1B", "cROW", "cHID", "cUDE", "cST2", "cO8",
]


class PStream:

    def __init__(self, sems):
        self.sems = sems
        self.n = len(sems)

    def inc(self, ins, idx):
        return ins.then_inc(self.sems[idx % self.n], 16)

    def wait(self, eng, idx):
        if idx < 0:
            return
        eng.wait_ge(self.sems[idx % self.n], 16 * (idx // self.n + 1))

    def wait_all(self, eng, total):
        for k in range(self.n):
            cnt = (total - 1 - k) // self.n + 1 if total > k else 0
            if cnt:
                eng.wait_ge(self.sems[k], 16 * cnt)


def _nk(j, qg):
    return 4 * j + 4 if qg == 0 else 32 - 4 * j


def plan_p4(j):
    plan = []
    GG = 0
    E = 0
    for i in range(64):
        qg = i % 2
        NK = _nk(j, qg)
        ng = NK // 2
        plan.append(dict(NK=NK, ng=ng, GG=GG, E=E, P=2 * i))
        GG += ng
        E += ng - 2
    return plan


def group_kind(plan, gg):
    for it in plan:
        if gg < it["GG"] + it["ng"]:
            g = gg - it["GG"]
            if g < it["ng"] - 2:
                return ("e", it["E"] + g + 1)
            return ("p", it["P"] + (g - (it["ng"] - 2)) + 1)
    raise AssertionError(gg)


def build_program():
    PH = int(os.environ.get("KPH", "8"))
    NI4 = int(os.environ.get("KP4N", "64"))
    NH4 = min(NH, (NI4 + 1) // 2)
    nc = bass.Bass("TRN2", target_bir_lowering=False)

    xfm = nc.dram_tensor("xfm", [D, S], F32, kind="ExternalInput").ap()
    xres = nc.dram_tensor("xres", [128, NDC, TOK], F32, kind="ExternalInput").ap()
    wq = nc.dram_tensor("wq", [128, NH, NDK, 128], F32, kind="ExternalInput").ap()
    wk = nc.dram_tensor("wk", [128, NKV, NDK, 128], F32, kind="ExternalInput").ap()
    wv = nc.dram_tensor("wv", [128, NDK, 1024], F32, kind="ExternalInput").ap()
    wo = nc.dram_tensor("wo", [128, NDC, NH, 128], F32, kind="ExternalInput").ap()
    gw = nc.dram_tensor("gw", [128, NHC, NDK + 1, 128], F32, kind="ExternalInput").ap()
    uw = nc.dram_tensor("uw", [128, NHC, NDK + 1, 128], F32, kind="ExternalInput").ap()
    dw = nc.dram_tensor("dw", [128, NDC, NHC + 1, 128], F32, kind="ExternalInput").ap()
    ga = nc.dram_tensor("ga", [128, NDK, RK], F32, kind="ExternalInput").ap()
    ua = nc.dram_tensor("ua", [128, NDK, RK], F32, kind="ExternalInput").ap()
    da = nc.dram_tensor("da", [128, NHC, RK], F32, kind="ExternalInput").ap()
    ck = nc.dram_tensor("ck", [128, S], F32, kind="ExternalInput").ap()
    sk = nc.dram_tensor("sk", [128, S], F32, kind="ExternalInput").ap()
    cq = nc.dram_tensor("cq", [128, TOK], F32, kind="ExternalInput").ap()
    sq = nc.dram_tensor("sq", [128, TOK], F32, kind="ExternalInput").ap()
    maskin = nc.dram_tensor("maskin", [128, 4, 512], BF16, kind="ExternalInput").ap()
    meta = nc.dram_tensor("meta", [1, 4], I32, kind="ExternalInput").ap()

    out = nc.dram_tensor("out", [128, NDC, TOK], F32, kind="ExternalOutput").ap()

    k_scr = nc.dram_tensor("k_scr", [NKV, 128, S], BF16, kind="Internal").ap()
    v_scr = nc.dram_tensor("v_scr", [NKT, 128, 1024], BF16, kind="Internal").ap()
    q_scr = nc.dram_tensor("q_scr", [NH, 128, TOK], BF16, kind="Internal").ap()
    r1_scr = nc.dram_tensor("r1_scr", [NDC, 128, TOK], F32, kind="Internal").ap()
    racc_scr = nc.dram_tensor("racc_scr", [NDC, 128, TOK], F32, kind="Internal").ap()

    meta_sb = nc.alloc_sbuf_tensor("meta_sb", [1, 4], I32)
    ones_bf = nc.alloc_sbuf_tensor("ones_bf", [128, 1], BF16)
    onesr_f = nc.alloc_sbuf_tensor("onesr_f", [1, 128], F32)
    negr_f = nc.alloc_sbuf_tensor("negr_f", [1, 128], F32)
    bias_sb = nc.alloc_sbuf_tensor("bias_sb", [128, 1], F32)
    strow = nc.alloc_sbuf_tensor("strow", [1, ST_LEN], F32)
    lrow = nc.alloc_sbuf_tensor("lrow", [1, 1024], F32)
    rrow = nc.alloc_sbuf_tensor("rrow", [1, 1024], F32)
    a_bc = nc.alloc_sbuf_tensor("a_bc", [128, 1024], F32)
    b_bc = nc.alloc_sbuf_tensor("b_bc", [128, 1024], F32)
    ug_full = nc.alloc_sbuf_tensor("ug_full", [128, 1024], BF16)
    uu_full = nc.alloc_sbuf_tensor("uu_full", [128, 1024], BF16)
    ud_full = nc.alloc_sbuf_tensor("ud_full", [128, 1024], BF16)
    ga_bf = nc.alloc_sbuf_tensor("ga_bf", [128, NDK * RK], BF16)
    ua_bf = nc.alloc_sbuf_tensor("ua_bf", [128, NDK * RK], BF16)
    da_bf = nc.alloc_sbuf_tensor("da_bf", [128, NHC * RK], BF16)

    AR = 170 * KB
    arena = nc.alloc_sbuf_tensor("arena", [128, AR], U8)
    abase = nc.lookup_mloc(arena).addr

    def place(name, shape, dtype, off):
        nbytes = int(np.prod(shape[1:])) * mybir.dt.size(dtype)
        assert off + nbytes <= AR, (name, off, nbytes)
        return nc.alloc_sbuf_tensor_at(name, shape, dtype, offset=abase + off)

    wk_bf = place("wk_bf", [128, NKV * NDK * 128], BF16, 0)
    wv_bf = place("wv_bf", [128, NDK * 1024], BF16, 0)
    xbuf = place("xbuf", [128, 2 * NDK * XCH], BF16, 64 * KB)
    ck_sb = place("ck_sb", [128, S], F32, 96 * KB)
    sk_sb = place("sk_sb", [128, S], F32, 112 * KB)
    sh_t = place("sh_t", [128, 2 * 512], F32, 128 * KB)
    t1_t = place("t1_t", [128, 2 * 512], F32, 132 * KB)
    t2_t = place("t2_t", [128, 2 * 512], F32, 136 * KB)
    kout = place("kout", [128, 2 * 512], BF16, 140 * KB)
    ksq = place("ksq", [128, 2 * 512], BF16, 142 * KB)
    vout = place("vout", [128, 2 * 512], BF16, 144 * KB)
    cq_sb = place("cq_sb", [128, TOK], F32, 146 * KB)
    sq_sb = place("sq_sb", [128, TOK], F32, 150 * KB)
    wq_bf = place("wq_bf", [128, 2 * NDK * 128], BF16, 154 * KB)
    xq_sb = place("xq_sb", [128, NDK * TOK], BF16, 64 * KB)
    kbuf = place("kbuf", [128, 2 * S], BF16, 0)
    vbuf = place("vbuf", [128, 2 * NKT * 128], BF16, 16 * KB)
    qbuf = place("qbuf", [128, 2 * TOK], BF16, 32 * KB)
    probs = place("probs", [128, 4 * 512], BF16, 36 * KB)
    rb_sb = place("rb_sb", [128, 2 * 512], F32, 40 * KB)
    mask_sb = place("mask_sb", [128, 4 * 512], BF16, 44 * KB)
    attn_res = place("attn_res", [128, NH * 1024], BF16, 104 * KB)
    wo_bf = place("wo_bf", [128, 2 * NH * 128], BF16, 0)
    xresb = place("xresb", [128, 4 * 512], F32, 16 * KB)
    r1buf = place("r1buf", [128, 4 * 512], F32, 24 * KB)
    r1bf = place("r1bf", [128, 2 * 512], BF16, 45 * KB)
    sqbf = place("sqbf", [128, 2 * 512], BF16, 47 * KB)
    r1ld = place("r1ld", [128, 4 * 512], F32, 32 * KB)
    x1f32 = place("x1f32", [128, 4 * 512], F32, 48 * KB)
    x1_bf = place("x1_bf", [128, NDK * TOK], BF16, 104 * KB)
    gbuf = place("gbuf", [128, 2 * (NDK + 1) * 128], BF16, 0)
    ubuf = place("ubuf", [128, 2 * (NDK + 1) * 128], BF16, 17 * KB)
    dbuf = place("dbuf", [128, 2 * 17 * 128], BF16, 34 * KB)
    tsil = place("tsil", [128, 2 * 512], BF16, 43 * KB)
    hidden = place("hidden", [128, 16 * 1024], BF16, 56 * KB)
    raccb = place("raccb", [128, 8 * 512], F32, 88 * KB)
    raccld8 = place("raccld8", [128, 4 * 512], F32, 0)
    outb8 = place("outb8", [128, 4 * 512], F32, 8 * KB)

    PA = nc.alloc_psum_tensor("PA", [128, 1024], F32)
    PB = nc.alloc_psum_tensor("PB", [128, 1024], F32)
    PC = nc.alloc_psum_tensor("PC", [128, 512], F32)
    PD = nc.alloc_psum_tensor("PD", [128, 512], F32)
    PE_ = nc.alloc_psum_tensor("PE_", [128, 512], F32)
    PF = nc.alloc_psum_tensor("PF", [128, 512], F32)

    sems = {}
    stack = contextlib.ExitStack()
    for name in SEM_NAMES:
        sems[name] = stack.enter_context(nc.semaphore(name))
    sm = type("SM", (), sems)
    st = {}
    for name, par in DMA_STREAMS:
        st[name] = PStream(
            [stack.enter_context(nc.semaphore(f"{name}{k}")) for k in range(par)]
        )

    plans = [plan_p4(j) for j in range(4)]
    pk = [PC, PD]
    pk4 = [PC, PD, PA[:, 0:512], PA[:, 512:1024]]

    def sl(t_, i, w=512, nslot=None):
        islot = i % (nslot if nslot else 2)
        return t_[:, islot * w : (islot + 1) * w]

    with stack, nc.Block() as block:

        @block.gpsimd
        def _(g):
            g.dma_start(out=meta_sb[:], in_=meta).then_inc(sm.dMSC, 16)
            g.wait_ge(sm.dMSC, 16)
            g.dma_start(
                out=ga_bf[:].rearrange("p (c r) -> p c r", c=NDK), in_=ga
            ).then_inc(sm.dMSC, 16)
            g.wait_ge(sm.dMSC, 32)
            g.dma_start(
                out=ua_bf[:].rearrange("p (c r) -> p c r", c=NDK), in_=ua
            ).then_inc(sm.dMSC, 16)
            g.wait_ge(sm.dMSC, 48)
            g.dma_start(
                out=da_bf[:].rearrange("p (c r) -> p c r", c=NHC), in_=da
            ).then_inc(sm.dMSC, 16)
            if PH >= 1:
                g.dma_start(
                    out=wk_bf[:].rearrange("p (kv c q) -> p kv c q", kv=NKV, c=NDK),
                    in_=wk,
                ).then_inc(sm.dW, 16)
                for tc in range(NCH):
                    if tc >= 2:
                        g.wait_ge(sm.cXK, tc - 1)
                    st["dX"].inc(g.dma_start(
                        out=xbuf[:, (tc % 2) * NDK * XCH : (tc % 2 + 1) * NDK * XCH]
                        .rearrange("p (c t) -> p c t", c=NDK),
                        in_=xfm[:, tc * XCH : (tc + 1) * XCH]
                        .rearrange("(c p) t -> p c t", p=128),
                    ), tc)
            if PH >= 2:
                g.wait_ge(sm.dW, 16)
                for kvw in range(NKV):
                    g.wait_ge(sm.cKM, (NCH - 1) * NKV + kvw + 1)
                    g.wait_ge(sm.dW, 16 * (kvw + 1))
                    g.dma_start(
                        out=wv_bf[:, kvw * 4 * 1024 : (kvw + 1) * 4 * 1024]
                        .rearrange("p (c v) -> p c v", c=4),
                        in_=wv[:, kvw * 4 : (kvw + 1) * 4, :],
                    ).then_inc(sm.dW, 16)
                for tc in range(NCH):
                    g.wait_ge(sm.cXK, NCH + tc - 1)
                    st["dX"].inc(g.dma_start(
                        out=xbuf[:, (tc % 2) * NDK * XCH : (tc % 2 + 1) * NDK * XCH]
                        .rearrange("p (c t) -> p c t", c=NDK),
                        in_=xfm[:, tc * XCH : (tc + 1) * XCH]
                        .rearrange("(c p) t -> p c t", p=128),
                    ), NCH + tc)
            if PH >= 3:
                g.wait_ge(sm.cKE, NCH * NKV)
                st["dX"].inc(g.dma_start(
                    out=xq_sb[:, 16 * TOK :].rearrange("p (c t) -> p c t", c=16),
                    in_=xres[:, 16:32, :],
                ), 2 * NCH)
                for qh in range(2):
                    st["dWQ"].inc(g.dma_start(
                        out=wq_bf[:, (qh % 2) * NDK * 128 : (qh % 2 + 1) * NDK * 128]
                        .rearrange("p (c q) -> p c q", c=NDK),
                        in_=wq[:, qh, :, :],
                    ), qh)
                g.wait_ge(sm.cXK, 2 * NCH)
                st["dX"].inc(g.dma_start(
                    out=xq_sb[:, : 16 * TOK].rearrange("p (c t) -> p c t", c=16),
                    in_=xres[:, 0:16, :],
                ), 2 * NCH + 1)
                g.wait_ge(sm.dMSC, 64)
                g.dma_start(
                    out=mask_sb[:].rearrange("p (m f) -> p m f", m=4), in_=maskin
                ).then_inc(sm.dMSC, 16)
                def _vload(kv):
                    for sub in range(4):
                        st["dVL"].inc(g.dma_start(
                            out=vbuf[
                                :, (kv % 2) * NKT * 128 + sub * 8 * 128 : (kv % 2) * NKT * 128 + (sub + 1) * 8 * 128
                            ].rearrange("p (kt d) -> p kt d", kt=8),
                            in_=v_scr[sub * 8 : (sub + 1) * 8, :, kv * 128 : (kv + 1) * 128].rearrange("kt p d -> p kt d"),
                        ), kv * 4 + sub)

                for qh in range(2, NH):
                    g.wait_ge(sm.cQM, 2 * (qh - 1))
                    st["dWQ"].inc(g.dma_start(
                        out=wq_bf[:, (qh % 2) * NDK * 128 : (qh % 2 + 1) * NDK * 128]
                        .rearrange("p (c q) -> p c q", c=NDK),
                        in_=wq[:, qh, :, :],
                    ), qh)
                    if PH >= 4 and qh == 2:
                        st["dVS"].wait_all(g, NVI)
                        _vload(0)
                    if PH >= 4 and qh == 6 and NH4 > 4:
                        _vload(1)
            if PH >= 4:
                for kv in range(2, min(NKV, (NH4 - 1) // 4 + 1)):
                    g.wait_ge(sm.cHEAD, kv - 1)
                    for sub in range(4):
                        st["dVL"].inc(g.dma_start(
                            out=vbuf[
                                :, (kv % 2) * NKT * 128 + sub * 8 * 128 : (kv % 2) * NKT * 128 + (sub + 1) * 8 * 128
                            ].rearrange("p (kt d) -> p kt d", kt=8),
                            in_=v_scr[sub * 8 : (sub + 1) * 8, :, kv * 128 : (kv + 1) * 128].rearrange("kt p d -> p kt d"),
                        ), kv * 4 + sub)
            if PH >= 5:
                for dc in range(NDC):
                    if dc < 2:
                        g.wait_ge(sm.cATN, 64)
                    else:
                        g.wait_ge(sm.cOM, 2 * (dc - 1))
                    st["dWO"].inc(g.dma_start(
                        out=wo_bf[:, (dc % 2) * NH * 128 : (dc % 2 + 1) * NH * 128]
                        .rearrange("p (c q) -> p c q", c=NH),
                        in_=wo[:, dc, :, :],
                    ), dc)
                    for qg in range(2):
                        i = dc * 2 + qg
                        if i >= 4:
                            g.wait_ge(sm.cR1, i - 3)
                        st["dXR"].inc(g.dma_start(
                            out=sl(xresb, i, nslot=4),
                            in_=xres[:, dc, qg * 512 : (qg + 1) * 512],
                        ), i)
            if PH >= 7:
                for p, (h0, h1) in enumerate(HG_BOUNDS):
                    for hgi in range(h1 - h0):
                        hc = h0 + hgi
                        if hc < 2:
                            st["dRAS"].wait_all(g, NDC * 2)
                            g.wait_ge(sm.cX1, NDC * 2)
                        else:
                            g.wait_ge(sm.cGM, 2 * (hc - 1))
                        st["dG"].inc(g.dma_start(
                            out=gbuf[
                                :, (hc % 2) * (NDK + 1) * 128 : (hc % 2 + 1) * (NDK + 1) * 128
                            ].rearrange("p (c q) -> p c q", c=NDK + 1),
                            in_=gw[:, hc, :, :],
                        ), hc)
                        if hc >= 2:
                            g.wait_ge(sm.cUM, 2 * (hc - 1))
                        st["dU"].inc(g.dma_start(
                            out=ubuf[
                                :, (hc % 2) * (NDK + 1) * 128 : (hc % 2 + 1) * (NDK + 1) * 128
                            ].rearrange("p (c q) -> p c q", c=NDK + 1),
                            in_=uw[:, hc, :, :],
                        ), hc)
                    ng = (h1 - h0) + (1 if p == NPASS - 1 else 0)
                    for dc in range(NDC):
                        gi = p * NDC + dc
                        if gi < 2:
                            st["dRAS"].wait_all(g, NDC * 2)
                        else:
                            g.wait_ge(sm.cDM, 2 * (gi - 1))
                        st["dD"].inc(g.dma_start(
                            out=dbuf[:, (gi % 2) * 17 * 128 : (gi % 2) * 17 * 128 + ng * 128]
                            .rearrange("p (c q) -> p c q", c=ng),
                            in_=dw[:, dc, h0 : h0 + ng, :],
                        ), gi)

        @block.sync
        def _(sy):
            sy.dma_start(out=ck_sb[:], in_=ck).then_inc(sm.dTAB, 16)
            sy.wait_ge(sm.dTAB, 16)
            sy.dma_start(out=sk_sb[:], in_=sk).then_inc(sm.dTAB, 16)
            sy.wait_ge(sm.dTAB, 32)
            sy.dma_start(out=cq_sb[:], in_=cq).then_inc(sm.dTAB, 16)
            sy.wait_ge(sm.dTAB, 48)
            sy.dma_start(out=sq_sb[:], in_=sq).then_inc(sm.dTAB, 16)
            for i in range(NCH * NKV):
                tc, kv = divmod(i, NKV)
                sy.wait_ge(sm.cKE, i + 1)
                st["dKS"].inc(sy.dma_start(
                    out=k_scr[kv, :, tc * XCH : (tc + 1) * XCH],
                    in_=kout[:, (i % 2) * 512 : (i % 2) * 512 + XCH],
                ), i)
            for i in range(NVI):
                tcts, vfh = divmod(i, 2)
                tc, tsub = divmod(tcts, 2)
                kt = tc * 2 + tsub
                sy.wait_ge(sm.cVE, i + 1)
                st["dVS"].inc(sy.dma_start(
                    out=v_scr[kt, :, vfh * 512 : (vfh + 1) * 512],
                    in_=sl(vout, i),
                ), i)
            if PH >= 4:
                def _kload(kv):
                    st["dKS"].wait_all(sy, NCH * NKV)
                    if kv < 2:
                        sy.wait_ge(sm.cVM, NVI)
                    else:
                        sy.wait_ge(sm.cHEAD, kv - 1)
                    st["dKL"].inc(sy.dma_start(
                        out=kbuf[:, (kv % 2) * S : (kv % 2 + 1) * S], in_=k_scr[kv, :, :]
                    ), kv)

                _kload(0)
                if NH4 > 4:
                    _kload(1)
            for i in range(NH * 2):
                qh, qg = divmod(i, 2)
                sy.wait_ge(sm.cQE, i + 1)
                st["dQS"].inc(sy.dma_start(
                    out=q_scr[qh, :, qg * 512 : (qg + 1) * 512], in_=sl(kout, i)
                ), i)
            if PH >= 4:
                for i in range(NH4):
                    st["dQS"].wait(sy, 2 * i + 1)
                    st["dQS"].wait(sy, 2 * i)
                    if i >= 2:
                        sy.wait_ge(sm.cQT, i - 1)
                    st["dQL"].inc(sy.dma_start(
                        out=qbuf[:, (i % 2) * TOK : (i % 2 + 1) * TOK], in_=q_scr[i, :, :]
                    ), i)
                    if i % 4 == 3 and i // 4 + 2 < NKV:
                        _kload(i // 4 + 2)
            if PH >= 5:
                for i in range(NDC * 2):
                    dc, qg = divmod(i, 2)
                    sy.wait_ge(sm.cR1, i + 1)
                    st["dR1S"].inc(sy.dma_start(
                        out=r1_scr[dc, :, qg * 512 : (qg + 1) * 512],
                        in_=sl(r1buf, i, nslot=4),
                    ), i)
            if PH >= 6:
                def _sp6(k):
                    dck, qgk = divmod(k, 2)
                    sy.wait_ge(sm.cHID, k + 1)
                    st["dRAS"].wait(sy, k - 4)
                    st["dRAS"].inc(sy.dma_start(
                        out=racc_scr[dck, :, qgk * 512 : (qgk + 1) * 512],
                        in_=sl(x1f32, k, nslot=4),
                    ), k)

                for i in range(NDC * 2):
                    dc, qg = divmod(i, 2)
                    if i < 4:
                        sy.wait_ge(sm.cATN, NI4)
                    else:
                        sy.wait_ge(sm.cX1, i - 3)
                    st["dR1S"].wait(sy, i)
                    st["dR1L"].inc(sy.dma_start(
                        out=sl(r1ld, i, nslot=4),
                        in_=r1_scr[dc, :, qg * 512 : (qg + 1) * 512],
                    ), i)
                    if i >= 2:
                        _sp6(i - 2)
                _sp6(NDC * 2 - 2)
                _sp6(NDC * 2 - 1)
            if PH >= 7:
                for p in range(NPASS):
                    def _sp7(k):
                        dck, qgk = divmod(k, 2)
                        gk = p * NDC * 2 + k
                        sy.wait_ge(sm.cUDE, gk + 1)
                        if p == NPASS - 1:
                            sy.wait_ge(sm.cST2, k + 1)
                        st["dRAS"].wait(sy, NDC * 2 + gk - 4)
                        st["dRAS"].inc(sy.dma_start(
                            out=racc_scr[dck, :, qgk * 512 : (qgk + 1) * 512],
                            in_=sl(raccb, k, nslot=8),
                        ), NDC * 2 + gk)

                    for i in range(NDC * 2):
                        dc, qg = divmod(i, 2)
                        gii = p * NDC * 2 + i
                        st["dRAS"].wait(sy, p * NDC * 2 + i)
                        if p == 0 and i < 8:
                            sy.wait_ge(sm.cOM, NDC * 2)
                        if gii >= 4:
                            sy.wait_ge(sm.cUDE, gii - 3)
                        if gii >= 8:
                            st["dRAS"].wait(sy, NDC * 2 + gii - 8)
                        st["dRAL"].inc(sy.dma_start(
                            out=sl(raccb, i, nslot=8),
                            in_=racc_scr[dc, :, qg * 512 : (qg + 1) * 512],
                        ), gii)
                        if i >= 4:
                            _sp7(i - 4)
                    for k in range(NDC * 2 - 4, NDC * 2):
                        _sp7(k)
            if PH >= 8:
                def _out8(k):
                    dck, qgk = divmod(k, 2)
                    sy.wait_ge(sm.cO8, k + 1)
                    st["dOUT"].inc(sy.dma_start(
                        out=out[:, dck, qgk * 512 : (qgk + 1) * 512],
                        in_=sl(outb8, k, nslot=4),
                    ), k)

                for i in range(NDC * 2):
                    dc, qg = divmod(i, 2)
                    sy.wait_ge(sm.cROW, 7)
                    sy.wait_ge(sm.cGM, 2 * NHC)
                    st["dRAS"].wait(sy, NDC * 2 + (NPASS - 1) * NDC * 2 + i)
                    if i >= 4:
                        sy.wait_ge(sm.cO8, i - 3)
                    st["dRAL"].inc(sy.dma_start(
                        out=sl(raccld8, i, nslot=4),
                        in_=racc_scr[dc, :, qg * 512 : (qg + 1) * 512],
                    ), NPASS * NDC * 2 + i)
                    if i >= 4:
                        _out8(i - 4)
                for k in range(NDC * 2 - 4, NDC * 2):
                    _out8(k)
                st["dOUT"].wait_all(sy, NDC * 2)
            if False:
                pass

            if PH == 1:
                st["dKS"].wait_all(sy, NCH * NKV)
            elif PH == 2:
                st["dVS"].wait_all(sy, NVI)
            elif PH == 3:
                st["dQS"].wait_all(sy, NH * 2)
            elif PH == 4:
                sy.wait_ge(sm.cATN, NI4)
            elif PH == 5:
                st["dR1S"].wait_all(sy, NDC * 2)
            elif PH == 6:
                st["dRAS"].wait_all(sy, NDC * 2)
            elif PH == 7:
                st["dRAS"].wait_all(sy, NDC * 2 * (NPASS + 1))

        @block.tensor
        def _(t):
            t.wait_ge(sm.cINI, 1)
            if PH >= 1:
                t.wait_ge(sm.dW, 16)
                for i in range(NCH * NKV + 2):
                    if i < NCH * NKV:
                        tc, kv = divmod(i, NKV)
                        if kv == 0:
                            st["dX"].wait(t, tc)
                        if i >= 4:
                            t.wait_ge(sm.cKE, i - 3)
                        for dk in range(NDK):
                            ins = t.matmul(
                                pk4[i % 4][:, 0:XCH],
                                wk_bf[:, (kv * NDK + dk) * 128 : (kv * NDK + dk) * 128 + 128],
                                xbuf[:, (tc % 2) * NDK * XCH + dk * XCH : (tc % 2) * NDK * XCH + dk * XCH + XCH],
                                start=(dk == 0), stop=(dk == NDK - 1),
                                skip_group_check=True,
                            )
                        ins.then_inc(sm.cKM, 1)
                    if i >= 2:
                        ti = i - 2
                        t.wait_ge(sm.cSH, ti + 1)
                        if ti >= 2:
                            t.wait_ge(sm.cKNR, ti - 1)
                        t.matmul(
                            (PE_ if ti % 2 == 0 else PF)[0:1, 0:XCH],
                            ones_bf[:, 0:1],
                            ksq[:, (ti % 2) * 512 : (ti % 2) * 512 + XCH],
                            start=True, stop=True, skip_group_check=True,
                        ).then_inc(sm.cKN, 1)
                    if i < NCH * NKV and i % NKV == NKV - 1:
                        t.nop().then_inc(sm.cXK, 1)

            if PH >= 2:
                t.wait_ge(sm.cKNR, NCH * NKV)
                for tcts in range(NVI // 2):
                    tc, tsub = divmod(tcts, 2)
                    if tcts % 2 == 0:
                        st["dX"].wait(t, NCH + tc)
                    if tcts == 0:
                        t.wait_ge(sm.cKE, NCH * NKV)
                    elif tcts == 1:
                        t.wait_ge(sm.dW, 16 + NKV * 16)
                    elif tcts >= 2:
                        t.wait_ge(sm.cVE, 2 * (tcts - 1))
                    for dk in range(NDK):
                        if tcts == 0 and dk % 4 == 0:
                            t.wait_ge(sm.dW, 16 + (dk // 4 + 1) * 16)
                        base = (tc % 2) * NDK * XCH + dk * XCH + tsub * 128
                        ins0 = t.matmul(
                            pk4[(tcts % 2) * 2][:],
                            xbuf[:, base : base + 128],
                            wv_bf[:, dk * 1024 : dk * 1024 + 512],
                            start=(dk == 0), stop=(dk == NDK - 1),
                            skip_group_check=True,
                        )
                        ins1 = t.matmul(
                            pk4[(tcts % 2) * 2 + 1][:],
                            xbuf[:, base : base + 128],
                            wv_bf[:, dk * 1024 + 512 : dk * 1024 + 1024],
                            start=(dk == 0), stop=(dk == NDK - 1),
                            skip_group_check=True,
                        )
                    ins0.then_inc(sm.cVM, 1)
                    ins1.then_inc(sm.cVM, 1)
                    if tcts % 2 == 1:
                        t.nop().then_inc(sm.cXK, 1)

            if PH >= 3:
                st["dX"].wait(t, 2 * NCH)
                st["dX"].wait(t, 2 * NCH + 1)
                for i in range(NH * 2 + 2):
                    if i < NH * 2 and i % 2 == 0:
                        qh = i // 2
                        st["dWQ"].wait(t, qh)
                        if qh == 0:
                            t.wait_ge(sm.cVE, NVI)
                        elif qh >= 2:
                            t.wait_ge(sm.cQE, 2 * (qh - 1))
                        for dk in range(NDK):
                            base = (qh % 2) * NDK * 128 + dk * 128
                            ins0 = t.matmul(
                                pk4[(qh % 2) * 2][:],
                                wq_bf[:, base : base + 128],
                                xq_sb[:, dk * TOK : dk * TOK + 512],
                                start=(dk == 0), stop=(dk == NDK - 1),
                                skip_group_check=True,
                            )
                            ins1 = t.matmul(
                                pk4[(qh % 2) * 2 + 1][:],
                                wq_bf[:, base : base + 128],
                                xq_sb[:, dk * TOK + 512 : dk * TOK + 1024],
                                start=(dk == 0), stop=(dk == NDK - 1),
                                skip_group_check=True,
                            )
                        ins0.then_inc(sm.cQM, 1)
                        ins1.then_inc(sm.cQM, 1)
                    if i >= 2:
                        ti = i - 2
                        t.wait_ge(sm.cQSH, ti + 1)
                        if ti >= 2:
                            t.wait_ge(sm.cQNR, ti - 1)
                        t.matmul(
                            (PE_ if ti % 2 == 0 else PF)[0:1, :],
                            ones_bf[:, 0:1], sl(ksq, ti),
                            start=True, stop=True, skip_group_check=True,
                        ).then_inc(sm.cQN, 1)
                t.wait_ge(sm.cMB, 1)
                t.matmul(
                    PF[:, 0:1], negr_f[:], strow[0:1, ST_M : ST_M + 1],
                    start=True, stop=True, skip_group_check=True,
                ).then_inc(sm.cMB, 1)

            if PH >= 4:
                t.wait_ge(sm.dMSC, 16)
                jr = t.alloc_register("jreg_t")
                t.reg_load(jr, meta_sb[0:1, 0:1])
                jv = bass_rust.RuntimeValue(val=jr, min_val=0, max_val=3)
                t.wait_ge(sm.cMB, 3)
                t.wait_ge(sm.cQE, 2 * NH)
                psc = [PA, PB]
                pav = [PC, PD]
                pnm = [PE_, PF]
                for j in nc.Switch(engines=[nc.tensor], index=[jv], n=4):
                    plan = plans[j]
                    for i in range(NI4):
                        it = plan[i]
                        kvq, qg = divmod(i, 2)
                        kv, qh4 = divmod(kvq, 4)
                        NK, ng, GG = it["NK"], it["ng"], it["GG"]
                        if qg == 0:
                            if qh4 == 0:
                                st["dKL"].wait(t, kv)
                                for _sub in range(4):
                                    st["dVL"].wait(t, kv * 4 + _sub)
                            st["dQL"].wait(t, kvq)
                        if i >= 2:
                            t.wait_ge(sm.cATN, i - 1)
                            t.wait_ge(sm.cRB, i - 1)
                        for step in range(ng + 2):
                            if step < ng:
                                gidx = step
                                gg = GG + gidx
                                if gg >= 2:
                                    kind, val = group_kind(plan, gg - 2)
                                    t.wait_ge(sm.cEXP if kind == "e" else sm.cPART, val)
                                for kk in range(2):
                                    kt = gidx * 2 + kk
                                    kb = (kv % 2) * S + kt * 128
                                    qb = (kvq % 2) * TOK + qg * 512
                                    ins = t.matmul(
                                        psc[gg % 2][:, kk * 512 : kk * 512 + 512],
                                        kbuf[:, kb : kb + 128],
                                        qbuf[:, qb : qb + 512],
                                        start=True, stop=True, skip_group_check=True,
                                    )
                                ins.then_inc(sm.cSC, 1)
                            if step >= 2:
                                gidx = step - 2
                                for kk in range(2):
                                    kt = gidx * 2 + kk
                                    slot = kt % 4
                                    if kt >= NK - 4:
                                        m = kt - (NK - 4)
                                        t.wait_ge(sm.cMSK, i * 4 + m + 1)
                                    else:
                                        t.wait_ge(sm.cEXP, it["E"] + gidx + 1)
                                    vb = (kv % 2) * NKT * 128 + kt * 128
                                    ins_s = t.matmul(
                                        pnm[i % 2][0:1, :],
                                        ones_bf[:, 0:1],
                                        probs[:, slot * 512 : slot * 512 + 512],
                                        start=(kt == 0), stop=(kt == NK - 1),
                                        skip_group_check=True,
                                    )
                                    if kt == NK - 1:
                                        ins_s.then_inc(sm.cAV, 1)
                                    ins_a = t.matmul(
                                        pav[i % 2][:],
                                        vbuf[:, vb : vb + 128],
                                        probs[:, slot * 512 : slot * 512 + 512],
                                        start=(kt == 0), stop=(kt == NK - 1),
                                        skip_group_check=True,
                                    )
                                    if kt % 2 == 1:
                                        ins_a.then_inc(sm.cPC, 1)
                        if qg == 1:
                            t.nop().then_inc(sm.cQT, 1)
                            if qh4 == 3:
                                t.nop().then_inc(sm.cHEAD, 1)
                        if i >= 1:
                            t.wait_ge(sm.cR, i)
                            t.matmul(
                                pnm[(i - 1) % 2][:], onesr_f[:],
                                rrow[0:1, ((i - 1) % 2) * 512 : ((i - 1) % 2) * 512 + 512],
                                start=True, stop=True, skip_group_check=True,
                            ).then_inc(sm.cBC, 1)
                    t.wait_ge(sm.cR, NI4)
                    t.matmul(
                        pnm[(NI4 - 1) % 2][:], onesr_f[:],
                        rrow[0:1, ((NI4 - 1) % 2) * 512 : ((NI4 - 1) % 2) * 512 + 512],
                        start=True, stop=True, skip_group_check=True,
                    ).then_inc(sm.cBC, 1)

            if PH >= 5:
                po = [PA[:, 0:512], PA[:, 512:1024], PB[:, 0:512], PB[:, 512:1024]]
                pst = [PC, PD]
                t.wait_ge(sm.cATN, 64)
                for i in range(NDC * 2 + 2):
                    if i < NDC * 2 and i % 2 == 0:
                        dc = i // 2
                        st["dWO"].wait(t, dc)
                        if i >= 4:
                            t.wait_ge(sm.cR1, i - 2)
                        for hc in range(NH):
                            wb = (dc % 2) * NH * 128 + hc * 128
                            ins0 = t.matmul(
                                po[i % 4],
                                wo_bf[:, wb : wb + 128],
                                attn_res[:, hc * 1024 : hc * 1024 + 512],
                                start=(hc == 0), stop=(hc == NH - 1),
                            )
                            ins1 = t.matmul(
                                po[(i + 1) % 4],
                                wo_bf[:, wb : wb + 128],
                                attn_res[:, hc * 1024 + 512 : hc * 1024 + 1024],
                                start=(hc == 0), stop=(hc == NH - 1),
                            )
                        ins0.then_inc(sm.cOM, 1)
                        ins1.then_inc(sm.cOM, 1)
                    if i >= 2:
                        i2 = i - 2
                        qg2 = i2 % 2
                        t.wait_ge(sm.cSQ, i2 + 1)
                        t.wait_ge(sm.cR1B, i2 + 1)
                        t.matmul(
                            pst[qg2][0:1, :], ones_bf[:, 0:1], sl(r1bf, i2),
                            start=(i2 // 2 == 0), stop=(i2 // 2 == NDC - 1),
                            skip_group_check=True,
                        )
                        t.matmul(
                            pst[qg2][32:33, :], ones_bf[:, 0:1], sl(sqbf, i2),
                            start=(i2 // 2 == 0), stop=(i2 // 2 == NDC - 1),
                            skip_group_check=True,
                        ).then_inc(sm.cST, 1)
                t.wait_ge(sm.cROW, 3)
                for qg in range(2):
                    t.matmul(
                        (PE_ if qg == 0 else PF)[:],
                        onesr_f[:], strow[0:1, ST_RSTD[qg] : ST_RSTD[qg] + 512],
                        start=True, stop=True, skip_group_check=True,
                    )
                    ins = t.matmul(
                        PA[:, qg * 512 : qg * 512 + 512],
                        onesr_f[:], strow[0:1, ST_B[qg] : ST_B[qg] + 512],
                        start=True, stop=True, skip_group_check=True,
                    )
                ins.then_inc(sm.cABC, 1)

            if PH >= 7:
                pg = [PA[:, 0:512], PA[:, 512:1024]]
                pu = [PB[:, 0:512], PB[:, 512:1024]]
                pd_ = [PC, PD]
                pud = PE_
                t.wait_ge(sm.cHID, NDC * 2)
                t.wait_ge(sm.cROW, 4)
                t.wait_ge(sm.dMSC, 64)
                for qg in range(2):
                    for dk in range(NDK):
                        t.matmul(
                            PA[32 * qg : 32 * qg + 16, 0:512],
                            ga_bf[:, dk * RK : dk * RK + RK],
                            x1_bf[:, dk * TOK + qg * 512 : dk * TOK + qg * 512 + 512],
                            start=(dk == 0), stop=(dk == NDK - 1),
                            skip_group_check=True,
                        )
                    for dk in range(NDK):
                        ins = t.matmul(
                            PB[32 * qg : 32 * qg + 16, 0:512],
                            ua_bf[:, dk * RK : dk * RK + RK],
                            x1_bf[:, dk * TOK + qg * 512 : dk * TOK + qg * 512 + 512],
                            start=(dk == 0), stop=(dk == NDK - 1),
                            skip_group_check=True,
                        )
                ins.then_inc(sm.cUG, 1)
                t.wait_ge(sm.cUG, 2)

                def _ud_mm(h0x, hcx):
                    hgix = hcx - h0x
                    for qg in range(2):
                        gi = hcx * 2 + qg
                        t.wait_ge(sm.cHID, NDC * 2 + gi + 1)
                        ins_ud = t.matmul(
                            pud[32 * qg : 32 * qg + 16, :],
                            da_bf[:, hcx * RK : hcx * RK + RK],
                            hidden[:, hgix * 1024 + qg * 512 : hgix * 1024 + qg * 512 + 512],
                            start=(hcx == 0), stop=(hcx == NHC - 1),
                            skip_group_check=True,
                        )
                        if hcx == NHC - 1 and qg == 1:
                            ins_ud.then_inc(sm.cUG, 1)

                for p, (h0, h1) in enumerate(HG_BOUNDS):
                    for hgi in range(h1 - h0):
                        hc = h0 + hgi
                        st["dG"].wait(t, hc)
                        if hc >= 1:
                            t.wait_ge(sm.cSIL, 2 * hc)
                            t.wait_ge(sm.cHID, NDC * 2 + 2 * hc)
                        lastg = [None, None]
                        for dk in range(NDK + 1):
                            gb = (hc % 2) * (NDK + 1) * 128 + dk * 128
                            for qg in range(2):
                                rhs = (
                                    x1_bf[:, dk * TOK + qg * 512 : dk * TOK + qg * 512 + 512]
                                    if dk < NDK
                                    else ug_full[:, qg * 512 : qg * 512 + 512]
                                )
                                lastg[qg] = t.matmul(
                                    pg[qg], gbuf[:, gb : gb + 128], rhs,
                                    start=(dk == 0), stop=(dk == NDK),
                                    skip_group_check=True,
                                )
                        lastg[0].then_inc(sm.cGM, 1)
                        lastg[1].then_inc(sm.cGM, 1)
                        st["dU"].wait(t, hc)
                        lastu = [None, None]
                        for dk in range(NDK + 1):
                            ub = (hc % 2) * (NDK + 1) * 128 + dk * 128
                            for qg in range(2):
                                rhs = (
                                    x1_bf[:, dk * TOK + qg * 512 : dk * TOK + qg * 512 + 512]
                                    if dk < NDK
                                    else uu_full[:, qg * 512 : qg * 512 + 512]
                                )
                                lastu[qg] = t.matmul(
                                    pu[qg], ubuf[:, ub : ub + 128], rhs,
                                    start=(dk == 0), stop=(dk == NDK),
                                    skip_group_check=True,
                                )
                        lastu[0].then_inc(sm.cUM, 1)
                        lastu[1].then_inc(sm.cUM, 1)
                        if hgi >= 1:
                            _ud_mm(h0, hc - 1)
                    _ud_mm(h0, h1 - 1)
                    if p == NPASS - 1:
                        t.wait_ge(sm.cUG, 4)
                    ng2 = (h1 - h0) + (1 if p == NPASS - 1 else 0)
                    for i in range(NDC * 2):
                        dc, qg = divmod(i, 2)
                        gi = p * NDC + dc
                        gii = p * NDC * 2 + i
                        if qg == 0:
                            st["dD"].wait(t, gi)
                        if gii >= 2:
                            t.wait_ge(sm.cUDE, gii - 1)
                        for hgj in range(ng2):
                            rhs = (
                                hidden[:, hgj * 1024 + qg * 512 : hgj * 1024 + qg * 512 + 512]
                                if hgj < h1 - h0
                                else ud_full[:, qg * 512 : qg * 512 + 512]
                            )
                            db = (gi % 2) * 17 * 128 + hgj * 128
                            ins = t.matmul(
                                pd_[gii % 2][:], dbuf[:, db : db + 128], rhs,
                                start=(hgj == 0), stop=(hgj == ng2 - 1),
                                skip_group_check=True,
                            )
                        ins.then_inc(sm.cDM, 1)
                        if p == NPASS - 1 and i >= 2:
                            i2 = i - 2
                            qg2 = i2 % 2
                            if i2 == 0:
                                t.wait_ge(sm.cSIL, NHC * 2)
                            t.wait_ge(sm.cST2, i2 + 1)
                            t.wait_ge(sm.cSQ, NDC * 2 + i2 + 1)
                            t.matmul(
                                (PA if qg2 == 0 else PB)[0:1, 0:512],
                                ones_bf[:, 0:1], sl(r1bf, i2),
                                start=(i2 // 2 == 0), stop=(i2 // 2 == NDC - 1),
                                skip_group_check=True,
                            )
                            t.matmul(
                                (PA if qg2 == 0 else PB)[32:33, 0:512],
                                ones_bf[:, 0:1], sl(sqbf, i2),
                                start=(i2 // 2 == 0), stop=(i2 // 2 == NDC - 1),
                                skip_group_check=True,
                            ).then_inc(sm.cST, 1)
                for i2 in range(NDC * 2 - 2, NDC * 2):
                    qg2 = i2 % 2
                    t.wait_ge(sm.cST2, i2 + 1)
                    t.wait_ge(sm.cSQ, NDC * 2 + i2 + 1)
                    t.matmul(
                        (PA if qg2 == 0 else PB)[0:1, 0:512],
                        ones_bf[:, 0:1], sl(r1bf, i2),
                        start=False, stop=(i2 // 2 == NDC - 1), skip_group_check=True,
                    )
                    t.matmul(
                        (PA if qg2 == 0 else PB)[32:33, 0:512],
                        ones_bf[:, 0:1], sl(sqbf, i2),
                        start=False, stop=(i2 // 2 == NDC - 1), skip_group_check=True,
                    ).then_inc(sm.cST, 1)
                t.wait_ge(sm.cROW, 6)
                t.wait_ge(sm.cUDE, NPASS * NDC * 2)
                for qg in range(2):
                    t.matmul(
                        (PC if qg == 0 else PD)[:],
                        onesr_f[:], strow[0:1, ST_RSTD[qg] : ST_RSTD[qg] + 512],
                        start=True, stop=True, skip_group_check=True,
                    )
                    ins = t.matmul(
                        (PE_ if qg == 0 else PF)[:],
                        onesr_f[:], strow[0:1, ST_B[qg] : ST_B[qg] + 512],
                        start=True, stop=True, skip_group_check=True,
                    )
                ins.then_inc(sm.cABC2, 1)

        @block.scalar
        def _(s):
            if PH >= 1:
                for i in range(NCH * NKV):
                    s.wait_ge(sm.cKM, i + 1)
                    if i >= 2:
                        s.wait_ge(sm.cKE, i - 1)
                        s.wait_ge(sm.cKN, i - 1)
                    s.copy(sh_t[0:64, (i % 2) * 512 : (i % 2) * 512 + XCH], pk4[i % 4][64:128, 0:XCH])
                    s.copy(sh_t[64:128, (i % 2) * 512 : (i % 2) * 512 + XCH], pk4[i % 4][0:64, 0:XCH])
                    s.activation(
                        ksq[:, (i % 2) * 512 : (i % 2) * 512 + XCH], pk4[i % 4][:, 0:XCH], AF.Square
                    ).then_inc(sm.cSH, 1)
            if PH >= 3:
                st["dVS"].wait_all(s, NVI)
                for i in range(NH * 2):
                    s.wait_ge(sm.cQM, i + 1)
                    if i >= 2:
                        s.wait_ge(sm.cQE, i - 1)
                        s.wait_ge(sm.cQN, i - 1)
                    pq = pk4[(i // 2 % 2) * 2 + i % 2]
                    s.copy(sh_t[0:64, (i % 2) * 512 : (i % 2 + 1) * 512], pq[64:128, :])
                    s.copy(sh_t[64:128, (i % 2) * 512 : (i % 2 + 1) * 512], pq[0:64, :])
                    s.activation(sl(ksq, i), pq[:], AF.Square).then_inc(sm.cQSH, 1)
                s.wait_ge(sm.cKNR, NCH * NKV)
                s.wait_ge(sm.cQNR, NH * 2)
                s.activation(strow[0:1, ST_LNK : ST_LNK + 1], strow[0:1, ST_KMAX : ST_KMAX + 1], AF.Ln)
                s.activation(
                    strow[0:1, ST_LNQ : ST_LNQ + 1],
                    strow[0:1, ST_QMAX : ST_QMAX + 1], AF.Ln,
                ).then_inc(sm.cSH, 2)
                s.wait_ge(sm.cROW, 1)
                s.activation(
                    strow[0:1, ST_M : ST_M + 1], strow[0:1, ST_LNS : ST_LNS + 1],
                    AF.Exp, bias=strow[0:1, ST_LNML : ST_LNML + 1], scale=0.5,
                ).then_inc(sm.cMB, 1)
                s.wait_ge(sm.cMB, 2)
                s.copy(bias_sb[:, 0:1], PF[:, 0:1]).then_inc(sm.cMB, 1)

            if PH >= 4:
                s.wait_ge(sm.dMSC, 16)
                jr = s.alloc_register("jreg_s")
                s.reg_load(jr, meta_sb[0:1, 0:1])
                jv = bass_rust.RuntimeValue(val=jr, min_val=0, max_val=3)
                pnm = [PE_, PF]
                for j in nc.Switch(engines=[nc.scalar], index=[jv], n=4):
                    plan = plans[j]
                    for i in range(NI4 + 1):
                        if i < NI4:
                            it = plan[i]
                            NK, ng, GG = it["NK"], it["ng"], it["GG"]
                            for gidx in range(ng):
                                gg = GG + gidx
                                s.wait_ge(sm.cSC, gg + 1)
                                if gg >= 2:
                                    s.wait_ge(sm.cPC, gg - 1)
                                ins = s.activation(
                                    probs[:, (gg % 2) * 1024 : (gg % 2) * 1024 + 1024],
                                    (PA if gg % 2 == 0 else PB)[:],
                                    AF.Exp, bias=bias_sb[:, 0:1], scale=SCALE,
                                )
                                if gidx < ng - 2:
                                    ins.then_inc(sm.cEXP, 1)
                                else:
                                    ins.then_inc(sm.cPART, 1)
                        if i >= 1:
                            ii = i - 1
                            s.wait_ge(sm.cAV, ii + 1)
                            s.copy(
                                lrow[0:1, (ii % 2) * 512 : (ii % 2) * 512 + 512],
                                pnm[ii % 2][0:1, :],
                            )
                            s.activation(
                                strow[0:1, ST_TMP : ST_TMP + 512],
                                lrow[0:1, (ii % 2) * 512 : (ii % 2) * 512 + 512],
                                AF.Ln,
                            )
                            s.activation(
                                rrow[0:1, (ii % 2) * 512 : (ii % 2) * 512 + 512],
                                strow[0:1, ST_TMP : ST_TMP + 512],
                                AF.Exp, scale=-1.0,
                            ).then_inc(sm.cR, 1)

            if PH >= 5:
                for i in range(NDC * 2):
                    s.wait_ge(sm.cR1, i + 1)
                    if i >= 2:
                        s.wait_ge(sm.cST, i - 1)
                    s.activation(
                        sl(sqbf, i), sl(r1buf, i, nslot=4), AF.Square
                    ).then_inc(sm.cSQ, 1)
                s.wait_ge(sm.cROW, 2)
                for qg in range(2):
                    s.activation(
                        strow[0:1, ST_TMP : ST_TMP + 512],
                        rrow[0:1, qg * 512 : qg * 512 + 512],
                        AF.Ln, bias=strow[0:1, ST_EPS : ST_EPS + 1],
                    )
                    ins = s.activation(
                        strow[0:1, ST_RSTD[qg] : ST_RSTD[qg] + 512],
                        strow[0:1, ST_TMP : ST_TMP + 512],
                        AF.Exp, scale=-0.5,
                    )
                ins.then_inc(sm.cROW2, 1)

            if PH >= 6:
                for i in range(NDC * 2):
                    dc, qg = divmod(i, 2)
                    if i == 0:
                        s.wait_ge(sm.cOM, NDC * 2)
                    s.wait_ge(sm.cHID, i + 1)
                    s.copy(
                        x1_bf[:, dc * TOK + qg * 512 : dc * TOK + qg * 512 + 512],
                        sl(x1f32, i, nslot=4),
                    ).then_inc(sm.cX1, 1)

            if PH >= 7:
                for gi in range(NHC * 2):
                    s.wait_ge(sm.cGM, gi + 1)
                    if gi >= 2:
                        s.wait_ge(sm.cHID, NDC * 2 + gi - 1)
                    s.activation(
                        sl(tsil, gi),
                        PA[:, (gi % 2) * 512 : (gi % 2) * 512 + 512],
                        AF.Silu,
                    ).then_inc(sm.cSIL, 1)
                for i in range(NDC * 2):
                    gii = (NPASS - 1) * NDC * 2 + i
                    s.wait_ge(sm.cUDE, gii + 1)
                    if i >= 2:
                        s.wait_ge(sm.cST, NDC * 2 + i - 1)
                    s.activation(
                        sl(sqbf, i), sl(raccb, i, nslot=8), AF.Square
                    ).then_inc(sm.cSQ, 1)
                s.wait_ge(sm.cROW, 5)
                for qg in range(2):
                    s.activation(
                        strow[0:1, ST_TMP : ST_TMP + 512],
                        rrow[0:1, qg * 512 : qg * 512 + 512],
                        AF.Ln, bias=strow[0:1, ST_EPS : ST_EPS + 1],
                    )
                    ins = s.activation(
                        strow[0:1, ST_RSTD[qg] : ST_RSTD[qg] + 512],
                        strow[0:1, ST_TMP : ST_TMP + 512],
                        AF.Exp, scale=-0.5,
                    )
                ins.then_inc(sm.cROW2, 1)

        @block.vector
        def _(v):
            v.memset(ones_bf[:], 1.0)
            v.memset(onesr_f[:], 1.0)
            v.memset(negr_f[:], -1.0)
            v.memset(strow[0:1, 0:6], 0.0)
            v.memset(strow[0:1, ST_LNML : ST_LNML + 1], LNML)
            v.memset(strow[0:1, ST_EPS : ST_EPS + 1], 1e-5)
            v.memset(ug_full[:], 0.0)
            v.memset(uu_full[:], 0.0)
            v.memset(ud_full[:], 0.0).then_inc(sm.cINI, 1)
            if PH >= 1:
                v.wait_ge(sm.dTAB, 32)
                for i in range(NCH * NKV + 2):
                    if i < NCH * NKV:
                        tc, kv = divmod(i, NKV)
                        v.wait_ge(sm.cSH, i + 1)
                        if i >= 2:
                            st["dKS"].wait(v, i - 2)
                        w0 = (i % 2) * 512
                        v.tensor_mul(
                            t1_t[:, w0 : w0 + XCH], pk4[i % 4][:, 0:XCH],
                            ck_sb[:, tc * XCH : (tc + 1) * XCH],
                        )
                        v.tensor_mul(
                            t2_t[:, w0 : w0 + XCH], sh_t[:, w0 : w0 + XCH],
                            sk_sb[:, tc * XCH : (tc + 1) * XCH],
                        )
                        v.tensor_add(
                            kout[:, w0 : w0 + XCH], t1_t[:, w0 : w0 + XCH],
                            t2_t[:, w0 : w0 + XCH],
                        ).then_inc(sm.cKE, 1)
                    if i >= 2:
                        ti = i - 2
                        v.wait_ge(sm.cKN, ti + 1)
                        v.reduce_max(
                            strow[0:1, ST_RT0 + (ti % 2) : ST_RT0 + (ti % 2) + 1],
                            (PE_ if ti % 2 == 0 else PF)[0:1, 0:XCH],
                            mybir.AxisListType.X,
                        )
                        v.tensor_max(
                            strow[0:1, ST_KMAX : ST_KMAX + 1],
                            strow[0:1, ST_KMAX : ST_KMAX + 1],
                            strow[0:1, ST_RT0 + (ti % 2) : ST_RT0 + (ti % 2) + 1],
                        ).then_inc(sm.cKNR, 1)
            if PH >= 2:
                for i in range(NVI):
                    v.wait_ge(sm.cVM, i + 1)
                    if i >= 2:
                        st["dVS"].wait(v, i - 2)
                    v.tensor_copy(sl(vout, i), pk4[i % 4][:]).then_inc(sm.cVE, 1)
            if PH >= 3:
                v.wait_ge(sm.dTAB, 64)
                for i in range(NH * 2 + 2):
                    if i < NH * 2:
                        qh, qg = divmod(i, 2)
                        v.wait_ge(sm.cQSH, i + 1)
                        if i >= 2:
                            st["dQS"].wait(v, i - 2)
                        v.tensor_mul(
                            sl(t1_t, i), pk4[(i // 2 % 2) * 2 + i % 2][:],
                            cq_sb[:, qg * 512 : (qg + 1) * 512],
                        )
                        v.tensor_mul(
                            sl(t2_t, i), sl(sh_t, i), sq_sb[:, qg * 512 : (qg + 1) * 512]
                        )
                        v.tensor_add(sl(kout, i), sl(t1_t, i), sl(t2_t, i)).then_inc(sm.cQE, 1)
                    if i >= 2:
                        ti = i - 2
                        v.wait_ge(sm.cQN, ti + 1)
                        v.reduce_max(
                            strow[0:1, ST_RT0 + 2 + (ti % 2) : ST_RT0 + 2 + (ti % 2) + 1],
                            (PE_ if ti % 2 == 0 else PF)[0:1, :],
                            mybir.AxisListType.X,
                        )
                        v.tensor_max(
                            strow[0:1, ST_QMAX : ST_QMAX + 1],
                            strow[0:1, ST_QMAX : ST_QMAX + 1],
                            strow[0:1, ST_RT0 + 2 + (ti % 2) : ST_RT0 + 2 + (ti % 2) + 1],
                        ).then_inc(sm.cQNR, 1)
                v.wait_ge(sm.cSH, NCH * NKV + 2)
                v.tensor_add(
                    strow[0:1, ST_LNS : ST_LNS + 1],
                    strow[0:1, ST_LNK : ST_LNK + 1],
                    strow[0:1, ST_LNQ : ST_LNQ + 1],
                ).then_inc(sm.cROW, 1)

            if PH >= 4:
                st["dQS"].wait_all(v, NH * 2)
                v.wait_ge(sm.dMSC, 80)
                pav = [PC, PD]
                pnm = [PE_, PF]
                for i in range(NI4 + 1):
                    if i < NI4:
                        for half in range(2):
                            v.wait_ge(sm.cPART, 2 * i + half + 1)
                            for m in ((0, 1) if half == 0 else (2, 3)):
                                v.tensor_mul(
                                    probs[:, m * 512 : (m + 1) * 512],
                                    probs[:, m * 512 : (m + 1) * 512],
                                    mask_sb[:, m * 512 : (m + 1) * 512],
                                ).then_inc(sm.cMSK, 1)
                    if i >= 1:
                        ii = i - 1
                        qh = ii // 2
                        qg = ii % 2
                        v.wait_ge(sm.cBC, ii + 1)
                        v.tensor_copy(sl(rb_sb, ii), pnm[ii % 2][:]).then_inc(sm.cRB, 1)
                        v.tensor_mul(
                            attn_res[:, qh * 1024 + qg * 512 : qh * 1024 + qg * 512 + 512],
                            sl(rb_sb, ii),
                            pav[ii % 2][:],
                        ).then_inc(sm.cATN, 1)

            if PH >= 5:
                po = [PA[:, 0:512], PA[:, 512:1024], PB[:, 0:512], PB[:, 512:1024]]
                for i in range(NDC * 2):
                    dc, qg = divmod(i, 2)
                    v.wait_ge(sm.cOM, i + 1)
                    st["dXR"].wait(v, i)
                    if i >= 4:
                        st["dR1S"].wait(v, i - 4)
                        v.wait_ge(sm.cSQ, i - 3)
                    v.tensor_add(
                        sl(r1buf, i, nslot=4), sl(xresb, i, nslot=4), po[i % 4]
                    ).then_inc(sm.cR1, 1)
                    if i >= 2:
                        v.wait_ge(sm.cST, i - 1)
                    v.tensor_copy(sl(r1bf, i), sl(r1buf, i, nslot=4)).then_inc(sm.cR1B, 1)
                v.wait_ge(sm.cST, NDC * 2)
                for qg in range(2):
                    pstq = PC if qg == 0 else PD
                    v.tensor_scalar_mul(
                        lrow[0:1, qg * 512 : qg * 512 + 512], pstq[0:1, :], 1.0 / D
                    )
                    v.tensor_scalar_mul(
                        rrow[0:1, qg * 512 : qg * 512 + 512], pstq[32:33, :], 1.0 / D
                    )
                for qg in range(2):
                    v.tensor_mul(
                        strow[0:1, ST_TMP : ST_TMP + 512],
                        lrow[0:1, qg * 512 : qg * 512 + 512],
                        lrow[0:1, qg * 512 : qg * 512 + 512],
                    )
                    ins = v.tensor_sub(
                        rrow[0:1, qg * 512 : qg * 512 + 512],
                        rrow[0:1, qg * 512 : qg * 512 + 512],
                        strow[0:1, ST_TMP : ST_TMP + 512],
                    )
                ins.then_inc(sm.cROW, 1)
                v.wait_ge(sm.cROW2, 1)
                for qg in range(2):
                    ins = v.tensor_mul(
                        strow[0:1, ST_B[qg] : ST_B[qg] + 512],
                        lrow[0:1, qg * 512 : qg * 512 + 512],
                        strow[0:1, ST_RSTD[qg] : ST_RSTD[qg] + 512],
                    )
                ins.then_inc(sm.cROW, 1)
                v.wait_ge(sm.cABC, 1)
                v.tensor_copy(a_bc[:, 0:512], PE_[:])
                v.tensor_copy(a_bc[:, 512:1024], PF[:])
                v.tensor_copy(b_bc[:, 0:512], PA[:, 0:512])
                v.tensor_copy(b_bc[:, 512:1024], PA[:, 512:1024]).then_inc(sm.cROW, 1)

            if PH >= 6:
                for i in range(NDC * 2):
                    dc, qg = divmod(i, 2)
                    st["dR1L"].wait(v, i)
                    if i >= 4:
                        st["dRAS"].wait(v, i - 4)
                        v.wait_ge(sm.cX1, i - 3)
                    v.tensor_mul(
                        sl(x1f32, i, nslot=4),
                        sl(r1ld, i, nslot=4),
                        a_bc[:, qg * 512 : (qg + 1) * 512],
                    )
                    v.tensor_sub(
                        sl(x1f32, i, nslot=4),
                        sl(x1f32, i, nslot=4),
                        b_bc[:, qg * 512 : (qg + 1) * 512],
                    ).then_inc(sm.cHID, 1)

            if PH >= 7:
                v.wait_ge(sm.cUG, 1)
                for qg in range(2):
                    v.tensor_copy(
                        ug_full[0:16, qg * 512 : (qg + 1) * 512],
                        PA[32 * qg : 32 * qg + 16, 0:512],
                    )
                    ins = v.tensor_copy(
                        uu_full[0:16, qg * 512 : (qg + 1) * 512],
                        PB[32 * qg : 32 * qg + 16, 0:512],
                    )
                ins.then_inc(sm.cUG, 1)
                for p, (h0, h1) in enumerate(HG_BOUNDS):
                    for hgi in range(h1 - h0):
                        hc = h0 + hgi
                        for qg in range(2):
                            gi = hc * 2 + qg
                            v.wait_ge(sm.cSIL, gi + 1)
                            v.wait_ge(sm.cUM, gi + 1)
                            if p >= 1 and hgi == 0 and qg == 0:
                                v.wait_ge(sm.cDM, p * NDC * 2)
                            v.tensor_mul(
                                hidden[:, hgi * 1024 + qg * 512 : hgi * 1024 + qg * 512 + 512],
                                sl(tsil, gi),
                                PB[:, (gi % 2) * 512 : (gi % 2) * 512 + 512],
                            ).then_inc(sm.cHID, 1)
                    if p == NPASS - 1:
                        v.wait_ge(sm.cUG, 3)
                        v.tensor_copy(ud_full[0:16, 0:512], PE_[0:16, :])
                        v.tensor_copy(ud_full[0:16, 512:1024], PE_[32:48, :]).then_inc(sm.cUG, 1)
                    for i in range(NDC * 2):
                        gii = p * NDC * 2 + i
                        v.wait_ge(sm.cDM, gii + 1)
                        st["dRAL"].wait(v, gii)
                        v.tensor_add(
                            sl(raccb, i, nslot=8),
                            sl(raccb, i, nslot=8),
                            (PC if gii % 2 == 0 else PD)[:],
                        ).then_inc(sm.cUDE, 1)
                        if p == NPASS - 1:
                            if i >= 2:
                                v.wait_ge(sm.cST, NDC * 2 + i - 1)
                            v.tensor_copy(
                                sl(r1bf, i), sl(raccb, i, nslot=8)
                            ).then_inc(sm.cST2, 1)
                v.wait_ge(sm.cST, NDC * 2 + NDC * 2)
                for qg in range(2):
                    pstq = PA if qg == 0 else PB
                    v.tensor_scalar_mul(
                        lrow[0:1, qg * 512 : qg * 512 + 512], pstq[0:1, 0:512], 1.0 / D
                    )
                    v.tensor_scalar_mul(
                        rrow[0:1, qg * 512 : qg * 512 + 512], pstq[32:33, 0:512], 1.0 / D
                    )
                for qg in range(2):
                    v.tensor_mul(
                        strow[0:1, ST_TMP : ST_TMP + 512],
                        lrow[0:1, qg * 512 : qg * 512 + 512],
                        lrow[0:1, qg * 512 : qg * 512 + 512],
                    )
                    ins = v.tensor_sub(
                        rrow[0:1, qg * 512 : qg * 512 + 512],
                        rrow[0:1, qg * 512 : qg * 512 + 512],
                        strow[0:1, ST_TMP : ST_TMP + 512],
                    )
                ins.then_inc(sm.cROW, 1)
                v.wait_ge(sm.cROW2, 2)
                for qg in range(2):
                    ins = v.tensor_mul(
                        strow[0:1, ST_B[qg] : ST_B[qg] + 512],
                        lrow[0:1, qg * 512 : qg * 512 + 512],
                        strow[0:1, ST_RSTD[qg] : ST_RSTD[qg] + 512],
                    )
                ins.then_inc(sm.cROW, 1)
                v.wait_ge(sm.cABC2, 1)
                v.tensor_copy(a_bc[:, 0:512], PC[:])
                v.tensor_copy(a_bc[:, 512:1024], PD[:])
                v.tensor_copy(b_bc[:, 0:512], PE_[:])
                v.tensor_copy(b_bc[:, 512:1024], PF[:]).then_inc(sm.cROW, 1)

            if PH >= 8:
                for i in range(NDC * 2):
                    dc, qg = divmod(i, 2)
                    st["dRAL"].wait(v, NPASS * NDC * 2 + i)
                    if i >= 4:
                        st["dOUT"].wait(v, i - 4)
                    v.tensor_mul(
                        sl(outb8, i, nslot=4),
                        sl(raccld8, i, nslot=4),
                        a_bc[:, qg * 512 : (qg + 1) * 512],
                    )
                    v.tensor_sub(
                        sl(outb8, i, nslot=4),
                        sl(outb8, i, nslot=4),
                        b_bc[:, qg * 512 : (qg + 1) * 512],
                    ).then_inc(sm.cO8, 1)

    return nc



_NC_CACHE = {}


def _host_prep(inputs):
    x = np.asarray(inputs["L0_attn_x"], dtype=np.float32)
    Wq = np.asarray(inputs["L0_attn_Wq"], dtype=np.float32)
    Wk = np.asarray(inputs["L0_attn_Wk"], dtype=np.float32)
    Wv = np.asarray(inputs["L0_attn_Wv"], dtype=np.float32)
    Wo = np.asarray(inputs["L0_attn_Wo"], dtype=np.float32)
    gwF = np.asarray(inputs["L0_ffn_gate_w"], dtype=np.float32)
    uwF = np.asarray(inputs["L0_ffn_up_w"], dtype=np.float32)
    dwF = np.asarray(inputs["L0_ffn_down_w"], dtype=np.float32)
    gaF = np.asarray(inputs["L0_ffn_gate_a"], dtype=np.float32)
    gbF = np.asarray(inputs["L0_ffn_gate_b"], dtype=np.float32)
    uaF = np.asarray(inputs["L0_ffn_up_a"], dtype=np.float32)
    ubF = np.asarray(inputs["L0_ffn_up_b"], dtype=np.float32)
    daF = np.asarray(inputs["L0_ffn_down_a"], dtype=np.float32)
    dbF = np.asarray(inputs["L0_ffn_down_b"], dtype=np.float32)

    xfm = [np.ascontiguousarray(x[b].T) for b in range(2)]

    wq_l = np.ascontiguousarray(Wq.reshape(NDK, 128, NH, 128).transpose(1, 2, 0, 3))
    wk_l = np.ascontiguousarray(Wk.reshape(NDK, 128, NKV, 128).transpose(1, 2, 0, 3))
    wv_l = np.ascontiguousarray(Wv.reshape(NDK, 128, 1024).transpose(1, 0, 2))
    wo_l = np.ascontiguousarray(Wo.reshape(NH, 128, NDC, 128).transpose(1, 2, 0, 3))

    def _wpad(wT, bT):
        t = wT.reshape(NDK, 128, NHC, 128).transpose(1, 2, 0, 3)
        pad = np.zeros((128, NHC, 1, 128), np.float32)
        pad[0:RK, :, 0, :] = bT.reshape(RK, NHC, 128).copy()
        return np.ascontiguousarray(np.concatenate([t, pad], axis=2))

    gw_l = _wpad(gwF.T, gbF.T)
    uw_l = _wpad(uwF.T, ubF.T)
    dt_ = dwF.T.reshape(NHC, 128, NDC, 128).transpose(1, 2, 0, 3)
    dpad = np.zeros((128, NDC, 1, 128), np.float32)
    dpad[0:RK, :, 0, :] = dbF.T.reshape(RK, NDC, 128)
    dw_l = np.ascontiguousarray(np.concatenate([dt_, dpad], axis=2))

    ga_l = np.ascontiguousarray(gaF.T.reshape(NDK, 128, RK).transpose(1, 0, 2))
    ua_l = np.ascontiguousarray(uaF.T.reshape(NDK, 128, RK).transpose(1, 0, 2))
    da_l = np.ascontiguousarray(daF.T.reshape(NHC, 128, RK).transpose(1, 0, 2))

    inv = 1.0 / (10000.0 ** (np.arange(0, 64, dtype=np.float64) / 64.0))
    pos = np.arange(S, dtype=np.float64)
    fr = np.outer(inv, pos)
    cos = np.cos(fr).astype(np.float32)
    sin = np.sin(fr).astype(np.float32)
    ck_l = np.ascontiguousarray(np.concatenate([cos, cos], 0))
    sk_l = np.ascontiguousarray(np.concatenate([-sin, sin], 0))

    mk = np.zeros((128, 4, 512), np.float32)
    pp = np.arange(128)[:, None]
    ff = np.arange(512)[None, :]
    for m in range(4):
        mk[:, m, :] = (128 * m + pp <= ff).astype(np.float32)
    mk_l = np.ascontiguousarray(mk.astype(ml_dtypes.bfloat16))

    in_maps = []
    tok_slices = []
    for c in range(8):
        b, j = divmod(c, 4)
        t0a = 512 * j
        t1a = 512 * (7 - j)
        tok_slices.append((b, t0a, t1a))
        xr = np.concatenate(
            [xfm[b][:, t0a : t0a + 512], xfm[b][:, t1a : t1a + 512]], axis=1
        )
        xres_l = np.ascontiguousarray(xr.reshape(NDC, 128, TOK).transpose(1, 0, 2))
        cq_l = np.ascontiguousarray(
            np.concatenate([ck_l[:, t0a : t0a + 512], ck_l[:, t1a : t1a + 512]], axis=1)
        )
        sq_l = np.ascontiguousarray(
            np.concatenate([sk_l[:, t0a : t0a + 512], sk_l[:, t1a : t1a + 512]], axis=1)
        )
        in_maps.append(
            dict(
                xfm=xfm[b], xres=xres_l, wq=wq_l, wk=wk_l, wv=wv_l, wo=wo_l,
                gw=gw_l, uw=uw_l, dw=dw_l, ga=ga_l, ua=ua_l, da=da_l,
                ck=ck_l, sk=sk_l, cq=cq_l, sq=sq_l, maskin=mk_l,
                meta=np.array([[j, 0, 0, 0]], np.int32),
            )
        )
    return in_maps, tok_slices


def kernel(**inputs):
    in_maps, tok_slices = _host_prep(inputs)
    if "nc" not in _NC_CACHE:
        _NC_CACHE["nc"] = build_program()
    nc = _NC_CACHE["nc"]
    res = run_bass_kernel_spmd(nc, in_maps, core_ids=list(range(8)))
    out = np.empty((2, S, D), np.float32)
    for c in range(8):
        b, t0a, t1a = tok_slices[c]
        o = np.asarray(res.results[c]["out"])
        om = o.transpose(1, 0, 2).reshape(D, TOK)
        out[b, t0a : t0a + 512, :] = om[:, 0:512].T
        out[b, t1a : t1a + 512, :] = om[:, 512:1024].T
    return out


# version 07
# speedup vs baseline: 1.0592x; 1.0529x over previous
import contextlib
import os
import numpy as np
import ml_dtypes
import bass_rust
import concourse.bass as bass
import concourse.mybir as mybir
from concourse.bass_utils import run_bass_kernel_spmd

F32 = mybir.dt.float32
BF16 = mybir.dt.bfloat16
I32 = mybir.dt.int32
U8 = mybir.dt.uint8
AF = mybir.ActivationFunctionType
OP = mybir.AluOpType

S = 4096
D = 4096
NH = 32
NKV = 8
DH = 11008
RK = 16
NDK = 32
NHC = 86
NDC = 32
XCH = 256
NCH = S // XCH
NVI = NCH * 4
NKT = 32
TOK = 1024
SCALE = float(1.0 / np.sqrt(128.0))
LNML = float(-0.5 * np.log(128.0))
HG_BOUNDS = [(0, 16), (16, 32), (32, 48), (48, 64), (64, 80), (80, 86)]
NPASS = len(HG_BOUNDS)
KB = 1024

ST_KMAX = 0
ST_QMAX = 1
ST_LNK = 2
ST_LNQ = 3
ST_LNS = 4
ST_M = 5
ST_LNML = 6
ST_EPS = 7
ST_RT0 = 8
ST_TMP = 12
ST_RSTD = [524, 1036]
ST_B = [1548, 2060]
ST_LEN = 2572

DMA_STREAMS = [
    ("dX", 2), ("dWQ", 2), ("dWO", 2), ("dG", 2), ("dU", 2), ("dD", 2),
    ("dVL", 4), ("dKL", 2), ("dQL", 2), ("dXR", 4), ("dKS", 2), ("dVS", 2),
    ("dQS", 2), ("dR1S", 4), ("dR1L", 4), ("dRAL", 4), ("dRAS", 4), ("dOUT", 2),
]
SEM_NAMES = [
    "dW", "dMSC", "dTAB",
    "cINI",
    "cKM", "cKN", "cXK", "cVM", "cQM", "cQN", "cMB", "cSC", "cPC", "cAV",
    "cBC", "cQT", "cHEAD", "cOM", "cST", "cABC", "cUG", "cGM", "cUM", "cDM",
    "cABC2",
    "cSH", "cQSH", "cEXP", "cPART", "cR", "cSQ", "cX1", "cSIL", "cROW2",
    "cKE", "cKNR", "cVE", "cQE", "cQNR", "cMSK", "cRB", "cATN", "cR1",
    "cR1B", "cROW", "cHID", "cUDE", "cST2", "cO8",
]


class PStream:

    def __init__(self, sems):
        self.sems = sems
        self.n = len(sems)

    def inc(self, ins, idx):
        return ins.then_inc(self.sems[idx % self.n], 16)

    def wait(self, eng, idx):
        if idx < 0:
            return
        eng.wait_ge(self.sems[idx % self.n], 16 * (idx // self.n + 1))

    def wait_all(self, eng, total):
        for k in range(self.n):
            cnt = (total - 1 - k) // self.n + 1 if total > k else 0
            if cnt:
                eng.wait_ge(self.sems[k], 16 * cnt)


def _nk(j, qg):
    return 4 * j + 4 if qg == 0 else 32 - 4 * j


def plan_p4(j):
    plan = []
    GG = 0
    E = 0
    for i in range(64):
        qg = i % 2
        NK = _nk(j, qg)
        ng = NK // 2
        plan.append(dict(NK=NK, ng=ng, GG=GG, E=E, P=2 * i))
        GG += ng
        E += ng - 2
    return plan


def group_kind(plan, gg):
    for it in plan:
        if gg < it["GG"] + it["ng"]:
            g = gg - it["GG"]
            if g < it["ng"] - 2:
                return ("e", it["E"] + g + 1)
            return ("p", it["P"] + (g - (it["ng"] - 2)) + 1)
    raise AssertionError(gg)


def build_program():
    PH = int(os.environ.get("KPH", "8"))
    NI4 = int(os.environ.get("KP4N", "64"))
    NH4 = min(NH, (NI4 + 1) // 2)
    nc = bass.Bass("TRN2", target_bir_lowering=False)

    xfm = nc.dram_tensor("xfm", [D, S], F32, kind="ExternalInput").ap()
    xres = nc.dram_tensor("xres", [128, NDC, TOK], F32, kind="ExternalInput").ap()
    wq = nc.dram_tensor("wq", [128, NH, NDK, 128], F32, kind="ExternalInput").ap()
    wk = nc.dram_tensor("wk", [128, NKV, NDK, 128], F32, kind="ExternalInput").ap()
    wv = nc.dram_tensor("wv", [128, NDK, 1024], F32, kind="ExternalInput").ap()
    wo = nc.dram_tensor("wo", [128, NDC, NH, 128], F32, kind="ExternalInput").ap()
    gw = nc.dram_tensor("gw", [128, NHC, NDK + 1, 128], F32, kind="ExternalInput").ap()
    uw = nc.dram_tensor("uw", [128, NHC, NDK + 1, 128], F32, kind="ExternalInput").ap()
    dw = nc.dram_tensor("dw", [128, NDC, NHC + 1, 128], F32, kind="ExternalInput").ap()
    ga = nc.dram_tensor("ga", [128, NDK, RK], F32, kind="ExternalInput").ap()
    ua = nc.dram_tensor("ua", [128, NDK, RK], F32, kind="ExternalInput").ap()
    da = nc.dram_tensor("da", [128, NHC, RK], F32, kind="ExternalInput").ap()
    ck = nc.dram_tensor("ck", [128, S], F32, kind="ExternalInput").ap()
    sk = nc.dram_tensor("sk", [128, S], F32, kind="ExternalInput").ap()
    cq = nc.dram_tensor("cq", [128, TOK], F32, kind="ExternalInput").ap()
    sq = nc.dram_tensor("sq", [128, TOK], F32, kind="ExternalInput").ap()
    maskin = nc.dram_tensor("maskin", [128, 4, 512], BF16, kind="ExternalInput").ap()
    meta = nc.dram_tensor("meta", [1, 4], I32, kind="ExternalInput").ap()

    out = nc.dram_tensor("out", [128, NDC, TOK], F32, kind="ExternalOutput").ap()

    k_scr = nc.dram_tensor("k_scr", [NKV, 128, S], BF16, kind="Internal").ap()
    v_scr = nc.dram_tensor("v_scr", [NKT, 128, 1024], BF16, kind="Internal").ap()
    q_scr = nc.dram_tensor("q_scr", [NH, 128, TOK], BF16, kind="Internal").ap()
    r1_scr = nc.dram_tensor("r1_scr", [NDC, 128, TOK], F32, kind="Internal").ap()
    racc_scr = nc.dram_tensor("racc_scr", [NDC, 128, TOK], F32, kind="Internal").ap()

    meta_sb = nc.alloc_sbuf_tensor("meta_sb", [1, 4], I32)
    ones_bf = nc.alloc_sbuf_tensor("ones_bf", [128, 1], BF16)
    onesr_f = nc.alloc_sbuf_tensor("onesr_f", [1, 128], F32)
    negr_f = nc.alloc_sbuf_tensor("negr_f", [1, 128], F32)
    bias_sb = nc.alloc_sbuf_tensor("bias_sb", [128, 1], F32)
    strow = nc.alloc_sbuf_tensor("strow", [1, ST_LEN], F32)
    lrow = nc.alloc_sbuf_tensor("lrow", [1, 1024], F32)
    rrow = nc.alloc_sbuf_tensor("rrow", [1, 1024], F32)
    a_bc = nc.alloc_sbuf_tensor("a_bc", [128, 1024], F32)
    b_bc = nc.alloc_sbuf_tensor("b_bc", [128, 1024], F32)
    ug_full = nc.alloc_sbuf_tensor("ug_full", [128, 1024], BF16)
    uu_full = nc.alloc_sbuf_tensor("uu_full", [128, 1024], BF16)
    ud_full = nc.alloc_sbuf_tensor("ud_full", [128, 1024], BF16)
    ga_bf = nc.alloc_sbuf_tensor("ga_bf", [128, NDK * RK], BF16)
    ua_bf = nc.alloc_sbuf_tensor("ua_bf", [128, NDK * RK], BF16)
    da_bf = nc.alloc_sbuf_tensor("da_bf", [128, NHC * RK], BF16)

    AR = 170 * KB
    arena = nc.alloc_sbuf_tensor("arena", [128, AR], U8)
    abase = nc.lookup_mloc(arena).addr

    def place(name, shape, dtype, off):
        nbytes = int(np.prod(shape[1:])) * mybir.dt.size(dtype)
        assert off + nbytes <= AR, (name, off, nbytes)
        return nc.alloc_sbuf_tensor_at(name, shape, dtype, offset=abase + off)

    wk_bf = place("wk_bf", [128, NKV * NDK * 128], BF16, 0)
    wv_bf = place("wv_bf", [128, NDK * 1024], BF16, 0)
    xbuf = place("xbuf", [128, 2 * NDK * XCH], BF16, 64 * KB)
    ck_sb = place("ck_sb", [128, S], F32, 96 * KB)
    sk_sb = place("sk_sb", [128, S], F32, 112 * KB)
    sh_t = place("sh_t", [128, 2 * 512], F32, 128 * KB)
    t1_t = place("t1_t", [128, 2 * 512], F32, 132 * KB)
    t2_t = place("t2_t", [128, 2 * 512], F32, 136 * KB)
    kout = place("kout", [128, 2 * 512], BF16, 140 * KB)
    ksq = place("ksq", [128, 2 * 512], BF16, 142 * KB)
    vout = place("vout", [128, 2 * 512], BF16, 144 * KB)
    cq_sb = place("cq_sb", [128, TOK], F32, 146 * KB)
    sq_sb = place("sq_sb", [128, TOK], F32, 150 * KB)
    wq_bf = place("wq_bf", [128, 2 * NDK * 128], BF16, 154 * KB)
    xq_sb = place("xq_sb", [128, NDK * TOK], BF16, 64 * KB)
    kbuf = place("kbuf", [128, 2 * S], BF16, 0)
    vbuf = place("vbuf", [128, 2 * NKT * 128], BF16, 16 * KB)
    qbuf = place("qbuf", [128, 2 * TOK], BF16, 32 * KB)
    probs = place("probs", [128, 4 * 512], BF16, 36 * KB)
    rb_sb = place("rb_sb", [128, 2 * 512], F32, 40 * KB)
    mask_sb = place("mask_sb", [128, 4 * 512], BF16, 44 * KB)
    attn_res = place("attn_res", [128, NH * 1024], BF16, 104 * KB)
    wo_bf = place("wo_bf", [128, 2 * NH * 128], BF16, 0)
    xresb = place("xresb", [128, 4 * 512], F32, 16 * KB)
    r1buf = place("r1buf", [128, 4 * 512], F32, 24 * KB)
    r1bf = place("r1bf", [128, 2 * 512], BF16, 45 * KB)
    sqbf = place("sqbf", [128, 2 * 512], BF16, 47 * KB)
    r1ld = place("r1ld", [128, 4 * 512], F32, 32 * KB)
    x1f32 = place("x1f32", [128, 4 * 512], F32, 48 * KB)
    x1_bf = place("x1_bf", [128, NDK * TOK], BF16, 104 * KB)
    gbuf = place("gbuf", [128, 2 * (NDK + 1) * 128], BF16, 0)
    ubuf = place("ubuf", [128, 2 * (NDK + 1) * 128], BF16, 17 * KB)
    dbuf = place("dbuf", [128, 2 * 17 * 128], BF16, 34 * KB)
    tsil = place("tsil", [128, 2 * 512], BF16, 43 * KB)
    hidden = place("hidden", [128, 16 * 1024], BF16, 56 * KB)
    raccb = place("raccb", [128, 8 * 512], F32, 88 * KB)
    raccld8 = place("raccld8", [128, 4 * 512], F32, 0)
    outb8 = place("outb8", [128, 4 * 512], F32, 8 * KB)

    PA = nc.alloc_psum_tensor("PA", [128, 1024], F32)
    PB = nc.alloc_psum_tensor("PB", [128, 1024], F32)
    PC = nc.alloc_psum_tensor("PC", [128, 512], F32)
    PD = nc.alloc_psum_tensor("PD", [128, 512], F32)
    PE_ = nc.alloc_psum_tensor("PE_", [128, 512], F32)
    PF = nc.alloc_psum_tensor("PF", [128, 512], F32)

    sems = {}
    stack = contextlib.ExitStack()
    for name in SEM_NAMES:
        sems[name] = stack.enter_context(nc.semaphore(name))
    sm = type("SM", (), sems)
    st = {}
    for name, par in DMA_STREAMS:
        st[name] = PStream(
            [stack.enter_context(nc.semaphore(f"{name}{k}")) for k in range(par)]
        )

    plans = [plan_p4(j) for j in range(4)]
    pk = [PC, PD]
    pk4 = [PC, PD, PA[:, 0:512], PA[:, 512:1024]]

    def sl(t_, i, w=512, nslot=None):
        islot = i % (nslot if nslot else 2)
        return t_[:, islot * w : (islot + 1) * w]

    with stack, nc.Block() as block:

        @block.gpsimd
        def _(g):
            g.dma_start(out=meta_sb[:], in_=meta).then_inc(sm.dMSC, 16)
            g.wait_ge(sm.dMSC, 16)
            g.dma_start(
                out=ga_bf[:].rearrange("p (c r) -> p c r", c=NDK), in_=ga
            ).then_inc(sm.dMSC, 16)
            g.wait_ge(sm.dMSC, 32)
            g.dma_start(
                out=ua_bf[:].rearrange("p (c r) -> p c r", c=NDK), in_=ua
            ).then_inc(sm.dMSC, 16)
            g.wait_ge(sm.dMSC, 48)
            g.dma_start(
                out=da_bf[:].rearrange("p (c r) -> p c r", c=NHC), in_=da
            ).then_inc(sm.dMSC, 16)
            if PH >= 1:
                g.dma_start(
                    out=wk_bf[:].rearrange("p (kv c q) -> p kv c q", kv=NKV, c=NDK),
                    in_=wk,
                ).then_inc(sm.dW, 16)
                for tc in range(NCH):
                    if tc >= 2:
                        g.wait_ge(sm.cXK, tc - 1)
                    st["dX"].inc(g.dma_start(
                        out=xbuf[:, (tc % 2) * NDK * XCH : (tc % 2 + 1) * NDK * XCH]
                        .rearrange("p (c t) -> p c t", c=NDK),
                        in_=xfm[:, tc * XCH : (tc + 1) * XCH]
                        .rearrange("(c p) t -> p c t", p=128),
                    ), tc)
            if PH >= 2:
                g.wait_ge(sm.dW, 16)
                for kvw in range(NKV):
                    g.wait_ge(sm.cKM, (NCH - 1) * NKV + kvw + 1)
                    g.wait_ge(sm.dW, 16 * (kvw + 1))
                    g.dma_start(
                        out=wv_bf[:, kvw * 4 * 1024 : (kvw + 1) * 4 * 1024]
                        .rearrange("p (c v) -> p c v", c=4),
                        in_=wv[:, kvw * 4 : (kvw + 1) * 4, :],
                    ).then_inc(sm.dW, 16)
                for tc in range(NCH):
                    g.wait_ge(sm.cXK, NCH + tc - 1)
                    st["dX"].inc(g.dma_start(
                        out=xbuf[:, (tc % 2) * NDK * XCH : (tc % 2 + 1) * NDK * XCH]
                        .rearrange("p (c t) -> p c t", c=NDK),
                        in_=xfm[:, tc * XCH : (tc + 1) * XCH]
                        .rearrange("(c p) t -> p c t", p=128),
                    ), NCH + tc)
            if PH >= 3:
                g.wait_ge(sm.cKE, NCH * NKV)
                st["dX"].inc(g.dma_start(
                    out=xq_sb[:, 16 * TOK :].rearrange("p (c t) -> p c t", c=16),
                    in_=xres[:, 16:32, :],
                ), 2 * NCH)
                for qh in range(2):
                    st["dWQ"].inc(g.dma_start(
                        out=wq_bf[:, (qh % 2) * NDK * 128 : (qh % 2 + 1) * NDK * 128]
                        .rearrange("p (c q) -> p c q", c=NDK),
                        in_=wq[:, qh, :, :],
                    ), qh)
                g.wait_ge(sm.cXK, 2 * NCH)
                st["dX"].inc(g.dma_start(
                    out=xq_sb[:, : 16 * TOK].rearrange("p (c t) -> p c t", c=16),
                    in_=xres[:, 0:16, :],
                ), 2 * NCH + 1)
                g.wait_ge(sm.dMSC, 64)
                g.dma_start(
                    out=mask_sb[:].rearrange("p (m f) -> p m f", m=4), in_=maskin
                ).then_inc(sm.dMSC, 16)
                def _vload(kv):
                    for sub in range(4):
                        st["dVL"].inc(g.dma_start(
                            out=vbuf[
                                :, (kv % 2) * NKT * 128 + sub * 8 * 128 : (kv % 2) * NKT * 128 + (sub + 1) * 8 * 128
                            ].rearrange("p (kt d) -> p kt d", kt=8),
                            in_=v_scr[sub * 8 : (sub + 1) * 8, :, kv * 128 : (kv + 1) * 128].rearrange("kt p d -> p kt d"),
                        ), kv * 4 + sub)

                for qh in range(2, NH):
                    g.wait_ge(sm.cQM, 2 * (qh - 1))
                    st["dWQ"].inc(g.dma_start(
                        out=wq_bf[:, (qh % 2) * NDK * 128 : (qh % 2 + 1) * NDK * 128]
                        .rearrange("p (c q) -> p c q", c=NDK),
                        in_=wq[:, qh, :, :],
                    ), qh)
                    if PH >= 4 and qh == 2:
                        st["dVS"].wait_all(g, NVI)
                        _vload(0)
                    if PH >= 4 and qh == 6 and NH4 > 4:
                        _vload(1)
            if PH >= 4:
                for kv in range(2, min(NKV, (NH4 - 1) // 4 + 1)):
                    g.wait_ge(sm.cHEAD, kv - 1)
                    for sub in range(4):
                        st["dVL"].inc(g.dma_start(
                            out=vbuf[
                                :, (kv % 2) * NKT * 128 + sub * 8 * 128 : (kv % 2) * NKT * 128 + (sub + 1) * 8 * 128
                            ].rearrange("p (kt d) -> p kt d", kt=8),
                            in_=v_scr[sub * 8 : (sub + 1) * 8, :, kv * 128 : (kv + 1) * 128].rearrange("kt p d -> p kt d"),
                        ), kv * 4 + sub)
            if PH >= 5:
                for dc in range(NDC):
                    if dc < 2:
                        g.wait_ge(sm.cATN, 64)
                    else:
                        g.wait_ge(sm.cOM, 2 * (dc - 1))
                    st["dWO"].inc(g.dma_start(
                        out=wo_bf[:, (dc % 2) * NH * 128 : (dc % 2 + 1) * NH * 128]
                        .rearrange("p (c q) -> p c q", c=NH),
                        in_=wo[:, dc, :, :],
                    ), dc)
                    for qg in range(2):
                        i = dc * 2 + qg
                        if i >= 4:
                            g.wait_ge(sm.cR1, i - 3)
                        st["dXR"].inc(g.dma_start(
                            out=sl(xresb, i, nslot=4),
                            in_=xres[:, dc, qg * 512 : (qg + 1) * 512],
                        ), i)
            if PH >= 7:
                for p, (h0, h1) in enumerate(HG_BOUNDS):
                    for hgi in range(h1 - h0):
                        hc = h0 + hgi
                        if hc < 2:
                            st["dRAS"].wait_all(g, NDC * 2)
                            g.wait_ge(sm.cX1, NDC * 2)
                        else:
                            g.wait_ge(sm.cGM, 2 * (hc - 1))
                        st["dG"].inc(g.dma_start(
                            out=gbuf[
                                :, (hc % 2) * (NDK + 1) * 128 : (hc % 2 + 1) * (NDK + 1) * 128
                            ].rearrange("p (c q) -> p c q", c=NDK + 1),
                            in_=gw[:, hc, :, :],
                        ), hc)
                        if hc >= 2:
                            g.wait_ge(sm.cUM, 2 * (hc - 1))
                        st["dU"].inc(g.dma_start(
                            out=ubuf[
                                :, (hc % 2) * (NDK + 1) * 128 : (hc % 2 + 1) * (NDK + 1) * 128
                            ].rearrange("p (c q) -> p c q", c=NDK + 1),
                            in_=uw[:, hc, :, :],
                        ), hc)
                    ng = (h1 - h0) + (1 if p == NPASS - 1 else 0)
                    for dc in range(NDC):
                        gi = p * NDC + dc
                        if gi < 2:
                            st["dRAS"].wait_all(g, NDC * 2)
                        else:
                            g.wait_ge(sm.cDM, 2 * (gi - 1))
                        st["dD"].inc(g.dma_start(
                            out=dbuf[:, (gi % 2) * 17 * 128 : (gi % 2) * 17 * 128 + ng * 128]
                            .rearrange("p (c q) -> p c q", c=ng),
                            in_=dw[:, dc, h0 : h0 + ng, :],
                        ), gi)

        @block.sync
        def _(sy):
            sy.dma_start(out=ck_sb[:], in_=ck).then_inc(sm.dTAB, 16)
            sy.wait_ge(sm.dTAB, 16)
            sy.dma_start(out=sk_sb[:], in_=sk).then_inc(sm.dTAB, 16)
            sy.wait_ge(sm.dTAB, 32)
            sy.dma_start(out=cq_sb[:], in_=cq).then_inc(sm.dTAB, 16)
            sy.wait_ge(sm.dTAB, 48)
            sy.dma_start(out=sq_sb[:], in_=sq).then_inc(sm.dTAB, 16)
            for i in range(NCH * NKV):
                tc, kv = divmod(i, NKV)
                sy.wait_ge(sm.cKE, i + 1)
                st["dKS"].inc(sy.dma_start(
                    out=k_scr[kv, :, tc * XCH : (tc + 1) * XCH],
                    in_=kout[:, (i % 2) * 512 : (i % 2) * 512 + XCH],
                ), i)
            for i in range(NVI):
                tcts, vfh = divmod(i, 2)
                tc, tsub = divmod(tcts, 2)
                kt = tc * 2 + tsub
                sy.wait_ge(sm.cVE, i + 1)
                st["dVS"].inc(sy.dma_start(
                    out=v_scr[kt, :, vfh * 512 : (vfh + 1) * 512],
                    in_=sl(vout, i),
                ), i)
            if PH >= 4:
                def _kload(kv):
                    st["dKS"].wait_all(sy, NCH * NKV)
                    if kv < 2:
                        sy.wait_ge(sm.cVM, NVI)
                    else:
                        sy.wait_ge(sm.cHEAD, kv - 1)
                    st["dKL"].inc(sy.dma_start(
                        out=kbuf[:, (kv % 2) * S : (kv % 2 + 1) * S], in_=k_scr[kv, :, :]
                    ), kv)

                _kload(0)
                if NH4 > 4:
                    _kload(1)
            for i in range(NH * 2):
                qh, qg = divmod(i, 2)
                sy.wait_ge(sm.cQE, i + 1)
                st["dQS"].inc(sy.dma_start(
                    out=q_scr[qh, :, qg * 512 : (qg + 1) * 512], in_=sl(kout, i)
                ), i)
            if PH >= 4:
                for i in range(NH4):
                    st["dQS"].wait(sy, 2 * i + 1)
                    st["dQS"].wait(sy, 2 * i)
                    if i >= 2:
                        sy.wait_ge(sm.cQT, i - 1)
                    st["dQL"].inc(sy.dma_start(
                        out=qbuf[:, (i % 2) * TOK : (i % 2 + 1) * TOK], in_=q_scr[i, :, :]
                    ), i)
                    if i % 4 == 3 and i // 4 + 2 < NKV:
                        _kload(i // 4 + 2)
            if PH >= 5:
                for i in range(NDC * 2):
                    dc, qg = divmod(i, 2)
                    sy.wait_ge(sm.cR1, i + 1)
                    st["dR1S"].inc(sy.dma_start(
                        out=r1_scr[dc, :, qg * 512 : (qg + 1) * 512],
                        in_=sl(r1buf, i, nslot=4),
                    ), i)
            if PH >= 6:
                def _sp6(k):
                    dck, qgk = divmod(k, 2)
                    sy.wait_ge(sm.cHID, k + 1)
                    st["dRAS"].wait(sy, k - 4)
                    st["dRAS"].inc(sy.dma_start(
                        out=racc_scr[dck, :, qgk * 512 : (qgk + 1) * 512],
                        in_=sl(x1f32, k, nslot=4),
                    ), k)

                for i in range(NDC * 2):
                    dc, qg = divmod(i, 2)
                    if i < 4:
                        sy.wait_ge(sm.cATN, NI4)
                    else:
                        sy.wait_ge(sm.cX1, i - 3)
                    st["dR1S"].wait(sy, i)
                    st["dR1L"].inc(sy.dma_start(
                        out=sl(r1ld, i, nslot=4),
                        in_=r1_scr[dc, :, qg * 512 : (qg + 1) * 512],
                    ), i)
                    if i >= 2:
                        _sp6(i - 2)
                _sp6(NDC * 2 - 2)
                _sp6(NDC * 2 - 1)
            if PH >= 7:
                for p in range(NPASS):
                    def _sp7(k):
                        dck, qgk = divmod(k, 2)
                        gk = p * NDC * 2 + k
                        sy.wait_ge(sm.cUDE, gk + 1)
                        if p == NPASS - 1:
                            sy.wait_ge(sm.cST2, k + 1)
                        st["dRAS"].wait(sy, NDC * 2 + gk - 4)
                        st["dRAS"].inc(sy.dma_start(
                            out=racc_scr[dck, :, qgk * 512 : (qgk + 1) * 512],
                            in_=sl(raccb, k, nslot=8),
                        ), NDC * 2 + gk)

                    for i in range(NDC * 2):
                        dc, qg = divmod(i, 2)
                        gii = p * NDC * 2 + i
                        st["dRAS"].wait(sy, p * NDC * 2 + i)
                        if p == 0 and i < 8:
                            sy.wait_ge(sm.cOM, NDC * 2)
                        if gii >= 4:
                            sy.wait_ge(sm.cUDE, gii - 3)
                        if gii >= 8:
                            st["dRAS"].wait(sy, NDC * 2 + gii - 8)
                        st["dRAL"].inc(sy.dma_start(
                            out=sl(raccb, i, nslot=8),
                            in_=racc_scr[dc, :, qg * 512 : (qg + 1) * 512],
                        ), gii)
                        if i >= 4:
                            _sp7(i - 4)
                    for k in range(NDC * 2 - 4, NDC * 2):
                        _sp7(k)
            if PH >= 8:
                def _out8(k):
                    dck, qgk = divmod(k, 2)
                    sy.wait_ge(sm.cO8, k + 1)
                    st["dOUT"].inc(sy.dma_start(
                        out=out[:, dck, qgk * 512 : (qgk + 1) * 512],
                        in_=sl(outb8, k, nslot=4),
                    ), k)

                for i in range(NDC * 2):
                    dc, qg = divmod(i, 2)
                    sy.wait_ge(sm.cROW, 7)
                    sy.wait_ge(sm.cGM, 2 * NHC)
                    st["dRAS"].wait(sy, NDC * 2 + (NPASS - 1) * NDC * 2 + i)
                    if i >= 4:
                        sy.wait_ge(sm.cO8, i - 3)
                    st["dRAL"].inc(sy.dma_start(
                        out=sl(raccld8, i, nslot=4),
                        in_=racc_scr[dc, :, qg * 512 : (qg + 1) * 512],
                    ), NPASS * NDC * 2 + i)
                    if i >= 4:
                        _out8(i - 4)
                for k in range(NDC * 2 - 4, NDC * 2):
                    _out8(k)
                st["dOUT"].wait_all(sy, NDC * 2)
            if False:
                pass

            if PH == 1:
                st["dKS"].wait_all(sy, NCH * NKV)
            elif PH == 2:
                st["dVS"].wait_all(sy, NVI)
            elif PH == 3:
                st["dQS"].wait_all(sy, NH * 2)
            elif PH == 4:
                sy.wait_ge(sm.cATN, NI4)
            elif PH == 5:
                st["dR1S"].wait_all(sy, NDC * 2)
            elif PH == 6:
                st["dRAS"].wait_all(sy, NDC * 2)
            elif PH == 7:
                st["dRAS"].wait_all(sy, NDC * 2 * (NPASS + 1))

        @block.tensor
        def _(t):
            t.wait_ge(sm.cINI, 1)
            if PH >= 1:
                t.wait_ge(sm.dW, 16)
                for i in range(NCH * NKV + 2):
                    if i < NCH * NKV:
                        tc, kv = divmod(i, NKV)
                        if kv == 0:
                            st["dX"].wait(t, tc)
                        if i >= 4:
                            t.wait_ge(sm.cKE, i - 3)
                        for dk in range(NDK):
                            ins = t.matmul(
                                pk4[i % 4][:, 0:XCH],
                                wk_bf[:, (kv * NDK + dk) * 128 : (kv * NDK + dk) * 128 + 128],
                                xbuf[:, (tc % 2) * NDK * XCH + dk * XCH : (tc % 2) * NDK * XCH + dk * XCH + XCH],
                                start=(dk == 0), stop=(dk == NDK - 1),
                                skip_group_check=True,
                            )
                        ins.then_inc(sm.cKM, 1)
                    if i >= 2:
                        ti = i - 2
                        t.wait_ge(sm.cSH, ti + 1)
                        if ti >= 2:
                            t.wait_ge(sm.cKNR, ti - 1)
                        t.matmul(
                            (PE_ if ti % 2 == 0 else PF)[0:1, 0:XCH],
                            ones_bf[:, 0:1],
                            ksq[:, (ti % 2) * 512 : (ti % 2) * 512 + XCH],
                            start=True, stop=True, skip_group_check=True,
                        ).then_inc(sm.cKN, 1)
                    if i < NCH * NKV and i % NKV == NKV - 1:
                        t.nop().then_inc(sm.cXK, 1)

            if PH >= 2:
                t.wait_ge(sm.cKNR, NCH * NKV)
                for tcts in range(NVI // 2):
                    tc, tsub = divmod(tcts, 2)
                    if tcts % 2 == 0:
                        st["dX"].wait(t, NCH + tc)
                    if tcts == 0:
                        t.wait_ge(sm.cKE, NCH * NKV)
                    elif tcts == 1:
                        t.wait_ge(sm.dW, 16 + NKV * 16)
                    elif tcts >= 2:
                        t.wait_ge(sm.cVE, 2 * (tcts - 1))
                    for dk in range(NDK):
                        if tcts == 0 and dk % 4 == 0:
                            t.wait_ge(sm.dW, 16 + (dk // 4 + 1) * 16)
                        base = (tc % 2) * NDK * XCH + dk * XCH + tsub * 128
                        ins0 = t.matmul(
                            pk4[(tcts % 2) * 2][:],
                            xbuf[:, base : base + 128],
                            wv_bf[:, dk * 1024 : dk * 1024 + 512],
                            start=(dk == 0), stop=(dk == NDK - 1),
                            skip_group_check=True,
                        )
                        ins1 = t.matmul(
                            pk4[(tcts % 2) * 2 + 1][:],
                            xbuf[:, base : base + 128],
                            wv_bf[:, dk * 1024 + 512 : dk * 1024 + 1024],
                            start=(dk == 0), stop=(dk == NDK - 1),
                            skip_group_check=True,
                        )
                    ins0.then_inc(sm.cVM, 1)
                    ins1.then_inc(sm.cVM, 1)
                    if tcts % 2 == 1:
                        t.nop().then_inc(sm.cXK, 1)

            if PH >= 3:
                st["dX"].wait(t, 2 * NCH)
                st["dX"].wait(t, 2 * NCH + 1)
                for i in range(NH * 2 + 2):
                    if i < NH * 2 and i % 2 == 0:
                        qh = i // 2
                        st["dWQ"].wait(t, qh)
                        if qh == 0:
                            t.wait_ge(sm.cVE, NVI)
                        elif qh >= 2:
                            t.wait_ge(sm.cQE, 2 * (qh - 1))
                        for dk in range(NDK):
                            base = (qh % 2) * NDK * 128 + dk * 128
                            ins0 = t.matmul(
                                pk4[(qh % 2) * 2][:],
                                wq_bf[:, base : base + 128],
                                xq_sb[:, dk * TOK : dk * TOK + 512],
                                start=(dk == 0), stop=(dk == NDK - 1),
                                skip_group_check=True,
                            )
                            ins1 = t.matmul(
                                pk4[(qh % 2) * 2 + 1][:],
                                wq_bf[:, base : base + 128],
                                xq_sb[:, dk * TOK + 512 : dk * TOK + 1024],
                                start=(dk == 0), stop=(dk == NDK - 1),
                                skip_group_check=True,
                            )
                        ins0.then_inc(sm.cQM, 1)
                        ins1.then_inc(sm.cQM, 1)
                    if i >= 2:
                        ti = i - 2
                        t.wait_ge(sm.cQSH, ti + 1)
                        if ti >= 2:
                            t.wait_ge(sm.cQNR, ti - 1)
                        t.matmul(
                            (PE_ if ti % 2 == 0 else PF)[0:1, :],
                            ones_bf[:, 0:1], sl(ksq, ti),
                            start=True, stop=True, skip_group_check=True,
                        ).then_inc(sm.cQN, 1)
                t.wait_ge(sm.cMB, 1)
                t.matmul(
                    PF[:, 0:1], negr_f[:], strow[0:1, ST_M : ST_M + 1],
                    start=True, stop=True, skip_group_check=True,
                ).then_inc(sm.cMB, 1)

            if PH >= 4:
                t.wait_ge(sm.dMSC, 16)
                jr = t.alloc_register("jreg_t")
                t.reg_load(jr, meta_sb[0:1, 0:1])
                jv = bass_rust.RuntimeValue(val=jr, min_val=0, max_val=3)
                t.wait_ge(sm.cMB, 3)
                t.wait_ge(sm.cQE, 2 * NH)
                psc = [PA, PB]
                pav = [PC, PD]
                pnm = [PE_, PF]
                for j in nc.Switch(engines=[nc.tensor], index=[jv], n=4):
                    plan = plans[j]
                    for i in range(NI4):
                        it = plan[i]
                        kvq, qg = divmod(i, 2)
                        kv, qh4 = divmod(kvq, 4)
                        NK, ng, GG = it["NK"], it["ng"], it["GG"]
                        if qg == 0:
                            if qh4 == 0:
                                st["dKL"].wait(t, kv)
                                for _sub in range(4):
                                    st["dVL"].wait(t, kv * 4 + _sub)
                            st["dQL"].wait(t, kvq)
                        if i >= 2:
                            t.wait_ge(sm.cATN, i - 1)
                            t.wait_ge(sm.cRB, i - 1)
                        for step in range(ng + 2):
                            if step < ng:
                                gidx = step
                                gg = GG + gidx
                                if gg >= 2:
                                    kind, val = group_kind(plan, gg - 2)
                                    t.wait_ge(sm.cEXP if kind == "e" else sm.cPART, val)
                                for kk in range(2):
                                    kt = gidx * 2 + kk
                                    kb = (kv % 2) * S + kt * 128
                                    qb = (kvq % 2) * TOK + qg * 512
                                    ins = t.matmul(
                                        psc[gg % 2][:, kk * 512 : kk * 512 + 512],
                                        kbuf[:, kb : kb + 128],
                                        qbuf[:, qb : qb + 512],
                                        start=True, stop=True, skip_group_check=True,
                                    )
                                ins.then_inc(sm.cSC, 1)
                            if step >= 2:
                                gidx = step - 2
                                for kk in range(2):
                                    kt = gidx * 2 + kk
                                    slot = kt % 4
                                    if kt >= NK - 4:
                                        m = kt - (NK - 4)
                                        t.wait_ge(sm.cMSK, i * 4 + m + 1)
                                    else:
                                        t.wait_ge(sm.cEXP, it["E"] + gidx + 1)
                                    vb = (kv % 2) * NKT * 128 + kt * 128
                                    ins_s = t.matmul(
                                        pnm[i % 2][0:1, :],
                                        ones_bf[:, 0:1],
                                        probs[:, slot * 512 : slot * 512 + 512],
                                        start=(kt == 0), stop=(kt == NK - 1),
                                        skip_group_check=True,
                                    )
                                    if kt == NK - 1:
                                        ins_s.then_inc(sm.cAV, 1)
                                    ins_a = t.matmul(
                                        pav[i % 2][:],
                                        vbuf[:, vb : vb + 128],
                                        probs[:, slot * 512 : slot * 512 + 512],
                                        start=(kt == 0), stop=(kt == NK - 1),
                                        skip_group_check=True,
                                    )
                                    if kt % 2 == 1:
                                        ins_a.then_inc(sm.cPC, 1)
                        if qg == 1:
                            t.nop().then_inc(sm.cQT, 1)
                            if qh4 == 3:
                                t.nop().then_inc(sm.cHEAD, 1)
                        if i >= 1:
                            t.wait_ge(sm.cR, i)
                            t.matmul(
                                pnm[(i - 1) % 2][:], onesr_f[:],
                                rrow[0:1, ((i - 1) % 2) * 512 : ((i - 1) % 2) * 512 + 512],
                                start=True, stop=True, skip_group_check=True,
                            ).then_inc(sm.cBC, 1)
                    t.wait_ge(sm.cR, NI4)
                    t.matmul(
                        pnm[(NI4 - 1) % 2][:], onesr_f[:],
                        rrow[0:1, ((NI4 - 1) % 2) * 512 : ((NI4 - 1) % 2) * 512 + 512],
                        start=True, stop=True, skip_group_check=True,
                    ).then_inc(sm.cBC, 1)

            if PH >= 5:
                po = [PA[:, 0:512], PA[:, 512:1024], PB[:, 0:512], PB[:, 512:1024]]
                pst = [PC, PD]
                t.wait_ge(sm.cATN, 64)
                for i in range(NDC * 2 + 2):
                    if i < NDC * 2 and i % 2 == 0:
                        dc = i // 2
                        st["dWO"].wait(t, dc)
                        if i >= 4:
                            t.wait_ge(sm.cR1, i - 2)
                        for hc in range(NH):
                            wb = (dc % 2) * NH * 128 + hc * 128
                            ins0 = t.matmul(
                                po[i % 4],
                                wo_bf[:, wb : wb + 128],
                                attn_res[:, hc * 1024 : hc * 1024 + 512],
                                start=(hc == 0), stop=(hc == NH - 1),
                            )
                            ins1 = t.matmul(
                                po[(i + 1) % 4],
                                wo_bf[:, wb : wb + 128],
                                attn_res[:, hc * 1024 + 512 : hc * 1024 + 1024],
                                start=(hc == 0), stop=(hc == NH - 1),
                            )
                        ins0.then_inc(sm.cOM, 1)
                        ins1.then_inc(sm.cOM, 1)
                    if i >= 2:
                        i2 = i - 2
                        qg2 = i2 % 2
                        t.wait_ge(sm.cSQ, i2 + 1)
                        t.wait_ge(sm.cR1B, i2 + 1)
                        t.matmul(
                            pst[qg2][0:1, :], ones_bf[:, 0:1], sl(r1bf, i2),
                            start=(i2 // 2 == 0), stop=(i2 // 2 == NDC - 1),
                            skip_group_check=True,
                        )
                        t.matmul(
                            pst[qg2][32:33, :], ones_bf[:, 0:1], sl(sqbf, i2),
                            start=(i2 // 2 == 0), stop=(i2 // 2 == NDC - 1),
                            skip_group_check=True,
                        ).then_inc(sm.cST, 1)
                t.wait_ge(sm.cROW, 3)
                for qg in range(2):
                    t.matmul(
                        (PE_ if qg == 0 else PF)[:],
                        onesr_f[:], strow[0:1, ST_RSTD[qg] : ST_RSTD[qg] + 512],
                        start=True, stop=True, skip_group_check=True,
                    )
                    ins = t.matmul(
                        PA[:, qg * 512 : qg * 512 + 512],
                        onesr_f[:], strow[0:1, ST_B[qg] : ST_B[qg] + 512],
                        start=True, stop=True, skip_group_check=True,
                    )
                ins.then_inc(sm.cABC, 1)

            if PH >= 7:
                pg = [PA[:, 0:512], PA[:, 512:1024]]
                pu = [PB[:, 0:512], PB[:, 512:1024]]
                pd_ = [PC, PD]
                pud = PE_
                t.wait_ge(sm.cHID, NDC * 2)
                t.wait_ge(sm.cROW, 4)
                t.wait_ge(sm.dMSC, 64)
                for qg in range(2):
                    for dk in range(NDK):
                        t.matmul(
                            PA[32 * qg : 32 * qg + 16, 0:512],
                            ga_bf[:, dk * RK : dk * RK + RK],
                            x1_bf[:, dk * TOK + qg * 512 : dk * TOK + qg * 512 + 512],
                            start=(dk == 0), stop=(dk == NDK - 1),
                            skip_group_check=True,
                        )
                    for dk in range(NDK):
                        ins = t.matmul(
                            PB[32 * qg : 32 * qg + 16, 0:512],
                            ua_bf[:, dk * RK : dk * RK + RK],
                            x1_bf[:, dk * TOK + qg * 512 : dk * TOK + qg * 512 + 512],
                            start=(dk == 0), stop=(dk == NDK - 1),
                            skip_group_check=True,
                        )
                ins.then_inc(sm.cUG, 1)
                t.wait_ge(sm.cUG, 2)

                def _ud_mm(h0x, hcx):
                    hgix = hcx - h0x
                    for qg in range(2):
                        gi = hcx * 2 + qg
                        t.wait_ge(sm.cHID, NDC * 2 + gi + 1)
                        ins_ud = t.matmul(
                            pud[32 * qg : 32 * qg + 16, :],
                            da_bf[:, hcx * RK : hcx * RK + RK],
                            hidden[:, hgix * 1024 + qg * 512 : hgix * 1024 + qg * 512 + 512],
                            start=(hcx == 0), stop=(hcx == NHC - 1),
                            skip_group_check=True,
                        )
                        if hcx == NHC - 1 and qg == 1:
                            ins_ud.then_inc(sm.cUG, 1)

                for p, (h0, h1) in enumerate(HG_BOUNDS):
                    for hgi in range(h1 - h0):
                        hc = h0 + hgi
                        st["dG"].wait(t, hc)
                        if hc >= 1:
                            t.wait_ge(sm.cSIL, 2 * hc)
                            t.wait_ge(sm.cHID, NDC * 2 + 2 * hc)
                        lastg = [None, None]
                        for dk in range(NDK + 1):
                            gb = (hc % 2) * (NDK + 1) * 128 + dk * 128
                            for qg in range(2):
                                rhs = (
                                    x1_bf[:, dk * TOK + qg * 512 : dk * TOK + qg * 512 + 512]
                                    if dk < NDK
                                    else ug_full[:, qg * 512 : qg * 512 + 512]
                                )
                                lastg[qg] = t.matmul(
                                    pg[qg], gbuf[:, gb : gb + 128], rhs,
                                    start=(dk == 0), stop=(dk == NDK),
                                    skip_group_check=True,
                                )
                        lastg[0].then_inc(sm.cGM, 1)
                        lastg[1].then_inc(sm.cGM, 1)
                        st["dU"].wait(t, hc)
                        lastu = [None, None]
                        for dk in range(NDK + 1):
                            ub = (hc % 2) * (NDK + 1) * 128 + dk * 128
                            for qg in range(2):
                                rhs = (
                                    x1_bf[:, dk * TOK + qg * 512 : dk * TOK + qg * 512 + 512]
                                    if dk < NDK
                                    else uu_full[:, qg * 512 : qg * 512 + 512]
                                )
                                lastu[qg] = t.matmul(
                                    pu[qg], ubuf[:, ub : ub + 128], rhs,
                                    start=(dk == 0), stop=(dk == NDK),
                                    skip_group_check=True,
                                )
                        lastu[0].then_inc(sm.cUM, 1)
                        lastu[1].then_inc(sm.cUM, 1)
                        if hgi >= 1:
                            _ud_mm(h0, hc - 1)
                    _ud_mm(h0, h1 - 1)
                    if p == NPASS - 1:
                        t.wait_ge(sm.cUG, 4)
                    ng2 = (h1 - h0) + (1 if p == NPASS - 1 else 0)
                    for dc in range(NDC):
                        gi = p * NDC + dc
                        gii0 = p * NDC * 2 + 2 * dc
                        st["dD"].wait(t, gi)
                        if gii0 >= 2:
                            t.wait_ge(sm.cUDE, gii0)
                        lastd = [None, None]
                        for hgj in range(ng2):
                            db = (gi % 2) * 17 * 128 + hgj * 128
                            for qg in range(2):
                                rhs = (
                                    hidden[:, hgj * 1024 + qg * 512 : hgj * 1024 + qg * 512 + 512]
                                    if hgj < h1 - h0
                                    else ud_full[:, qg * 512 : qg * 512 + 512]
                                )
                                lastd[qg] = t.matmul(
                                    pd_[qg][:], dbuf[:, db : db + 128], rhs,
                                    start=(hgj == 0), stop=(hgj == ng2 - 1),
                                    skip_group_check=True,
                                )
                        lastd[0].then_inc(sm.cDM, 1)
                        lastd[1].then_inc(sm.cDM, 1)
                        if p == NPASS - 1 and dc >= 1:
                            for i2 in (2 * dc - 2, 2 * dc - 1):
                                qg2 = i2 % 2
                                if i2 == 0:
                                    t.wait_ge(sm.cSIL, NHC * 2)
                                t.wait_ge(sm.cST2, i2 + 1)
                                t.wait_ge(sm.cSQ, NDC * 2 + i2 + 1)
                                t.matmul(
                                    (PA if qg2 == 0 else PB)[0:1, 0:512],
                                    ones_bf[:, 0:1], sl(r1bf, i2),
                                    start=(i2 // 2 == 0), stop=(i2 // 2 == NDC - 1),
                                    skip_group_check=True,
                                )
                                t.matmul(
                                    (PA if qg2 == 0 else PB)[32:33, 0:512],
                                    ones_bf[:, 0:1], sl(sqbf, i2),
                                    start=(i2 // 2 == 0), stop=(i2 // 2 == NDC - 1),
                                    skip_group_check=True,
                                ).then_inc(sm.cST, 1)
                for i2 in range(NDC * 2 - 2, NDC * 2):
                    qg2 = i2 % 2
                    t.wait_ge(sm.cST2, i2 + 1)
                    t.wait_ge(sm.cSQ, NDC * 2 + i2 + 1)
                    t.matmul(
                        (PA if qg2 == 0 else PB)[0:1, 0:512],
                        ones_bf[:, 0:1], sl(r1bf, i2),
                        start=False, stop=(i2 // 2 == NDC - 1), skip_group_check=True,
                    )
                    t.matmul(
                        (PA if qg2 == 0 else PB)[32:33, 0:512],
                        ones_bf[:, 0:1], sl(sqbf, i2),
                        start=False, stop=(i2 // 2 == NDC - 1), skip_group_check=True,
                    ).then_inc(sm.cST, 1)
                t.wait_ge(sm.cROW, 6)
                t.wait_ge(sm.cUDE, NPASS * NDC * 2)
                for qg in range(2):
                    t.matmul(
                        (PC if qg == 0 else PD)[:],
                        onesr_f[:], strow[0:1, ST_RSTD[qg] : ST_RSTD[qg] + 512],
                        start=True, stop=True, skip_group_check=True,
                    )
                    ins = t.matmul(
                        (PE_ if qg == 0 else PF)[:],
                        onesr_f[:], strow[0:1, ST_B[qg] : ST_B[qg] + 512],
                        start=True, stop=True, skip_group_check=True,
                    )
                ins.then_inc(sm.cABC2, 1)

        @block.scalar
        def _(s):
            if PH >= 1:
                for i in range(NCH * NKV):
                    s.wait_ge(sm.cKM, i + 1)
                    if i >= 2:
                        s.wait_ge(sm.cKE, i - 1)
                        s.wait_ge(sm.cKN, i - 1)
                    s.copy(sh_t[0:64, (i % 2) * 512 : (i % 2) * 512 + XCH], pk4[i % 4][64:128, 0:XCH])
                    s.copy(sh_t[64:128, (i % 2) * 512 : (i % 2) * 512 + XCH], pk4[i % 4][0:64, 0:XCH])
                    s.activation(
                        ksq[:, (i % 2) * 512 : (i % 2) * 512 + XCH], pk4[i % 4][:, 0:XCH], AF.Square
                    ).then_inc(sm.cSH, 1)
            if PH >= 3:
                st["dVS"].wait_all(s, NVI)
                for i in range(NH * 2):
                    s.wait_ge(sm.cQM, i + 1)
                    if i >= 2:
                        s.wait_ge(sm.cQE, i - 1)
                        s.wait_ge(sm.cQN, i - 1)
                    pq = pk4[(i // 2 % 2) * 2 + i % 2]
                    s.copy(sh_t[0:64, (i % 2) * 512 : (i % 2 + 1) * 512], pq[64:128, :])
                    s.copy(sh_t[64:128, (i % 2) * 512 : (i % 2 + 1) * 512], pq[0:64, :])
                    s.activation(sl(ksq, i), pq[:], AF.Square).then_inc(sm.cQSH, 1)
                s.wait_ge(sm.cKNR, NCH * NKV)
                s.wait_ge(sm.cQNR, NH * 2)
                s.activation(strow[0:1, ST_LNK : ST_LNK + 1], strow[0:1, ST_KMAX : ST_KMAX + 1], AF.Ln)
                s.activation(
                    strow[0:1, ST_LNQ : ST_LNQ + 1],
                    strow[0:1, ST_QMAX : ST_QMAX + 1], AF.Ln,
                ).then_inc(sm.cSH, 2)
                s.wait_ge(sm.cROW, 1)
                s.activation(
                    strow[0:1, ST_M : ST_M + 1], strow[0:1, ST_LNS : ST_LNS + 1],
                    AF.Exp, bias=strow[0:1, ST_LNML : ST_LNML + 1], scale=0.5,
                ).then_inc(sm.cMB, 1)
                s.wait_ge(sm.cMB, 2)
                s.copy(bias_sb[:, 0:1], PF[:, 0:1]).then_inc(sm.cMB, 1)

            if PH >= 4:
                s.wait_ge(sm.dMSC, 16)
                jr = s.alloc_register("jreg_s")
                s.reg_load(jr, meta_sb[0:1, 0:1])
                jv = bass_rust.RuntimeValue(val=jr, min_val=0, max_val=3)
                pnm = [PE_, PF]
                for j in nc.Switch(engines=[nc.scalar], index=[jv], n=4):
                    plan = plans[j]
                    for i in range(NI4 + 1):
                        if i < NI4:
                            it = plan[i]
                            NK, ng, GG = it["NK"], it["ng"], it["GG"]
                            for gidx in range(ng):
                                gg = GG + gidx
                                s.wait_ge(sm.cSC, gg + 1)
                                if gg >= 2:
                                    s.wait_ge(sm.cPC, gg - 1)
                                ins = s.activation(
                                    probs[:, (gg % 2) * 1024 : (gg % 2) * 1024 + 1024],
                                    (PA if gg % 2 == 0 else PB)[:],
                                    AF.Exp, bias=bias_sb[:, 0:1], scale=SCALE,
                                )
                                if gidx < ng - 2:
                                    ins.then_inc(sm.cEXP, 1)
                                else:
                                    ins.then_inc(sm.cPART, 1)
                        if i >= 1:
                            ii = i - 1
                            s.wait_ge(sm.cAV, ii + 1)
                            s.copy(
                                lrow[0:1, (ii % 2) * 512 : (ii % 2) * 512 + 512],
                                pnm[ii % 2][0:1, :],
                            )
                            s.activation(
                                strow[0:1, ST_TMP : ST_TMP + 512],
                                lrow[0:1, (ii % 2) * 512 : (ii % 2) * 512 + 512],
                                AF.Ln,
                            )
                            s.activation(
                                rrow[0:1, (ii % 2) * 512 : (ii % 2) * 512 + 512],
                                strow[0:1, ST_TMP : ST_TMP + 512],
                                AF.Exp, scale=-1.0,
                            ).then_inc(sm.cR, 1)

            if PH >= 5:
                for i in range(NDC * 2):
                    s.wait_ge(sm.cR1, i + 1)
                    if i >= 2:
                        s.wait_ge(sm.cST, i - 1)
                    s.activation(
                        sl(sqbf, i), sl(r1buf, i, nslot=4), AF.Square
                    ).then_inc(sm.cSQ, 1)
                s.wait_ge(sm.cROW, 2)
                for qg in range(2):
                    s.activation(
                        strow[0:1, ST_TMP : ST_TMP + 512],
                        rrow[0:1, qg * 512 : qg * 512 + 512],
                        AF.Ln, bias=strow[0:1, ST_EPS : ST_EPS + 1],
                    )
                    ins = s.activation(
                        strow[0:1, ST_RSTD[qg] : ST_RSTD[qg] + 512],
                        strow[0:1, ST_TMP : ST_TMP + 512],
                        AF.Exp, scale=-0.5,
                    )
                ins.then_inc(sm.cROW2, 1)

            if PH >= 6:
                for i in range(NDC * 2):
                    dc, qg = divmod(i, 2)
                    if i == 0:
                        s.wait_ge(sm.cOM, NDC * 2)
                    s.wait_ge(sm.cHID, i + 1)
                    s.copy(
                        x1_bf[:, dc * TOK + qg * 512 : dc * TOK + qg * 512 + 512],
                        sl(x1f32, i, nslot=4),
                    ).then_inc(sm.cX1, 1)

            if PH >= 7:
                for gi in range(NHC * 2):
                    s.wait_ge(sm.cGM, gi + 1)
                    if gi >= 2:
                        s.wait_ge(sm.cHID, NDC * 2 + gi - 1)
                    s.activation(
                        sl(tsil, gi),
                        PA[:, (gi % 2) * 512 : (gi % 2) * 512 + 512],
                        AF.Silu,
                    ).then_inc(sm.cSIL, 1)
                for i in range(NDC * 2):
                    gii = (NPASS - 1) * NDC * 2 + i
                    s.wait_ge(sm.cUDE, gii + 1)
                    if i >= 2:
                        s.wait_ge(sm.cST, NDC * 2 + i - 1)
                    s.activation(
                        sl(sqbf, i), sl(raccb, i, nslot=8), AF.Square
                    ).then_inc(sm.cSQ, 1)
                s.wait_ge(sm.cROW, 5)
                for qg in range(2):
                    s.activation(
                        strow[0:1, ST_TMP : ST_TMP + 512],
                        rrow[0:1, qg * 512 : qg * 512 + 512],
                        AF.Ln, bias=strow[0:1, ST_EPS : ST_EPS + 1],
                    )
                    ins = s.activation(
                        strow[0:1, ST_RSTD[qg] : ST_RSTD[qg] + 512],
                        strow[0:1, ST_TMP : ST_TMP + 512],
                        AF.Exp, scale=-0.5,
                    )
                ins.then_inc(sm.cROW2, 1)

        @block.vector
        def _(v):
            v.memset(ones_bf[:], 1.0)
            v.memset(onesr_f[:], 1.0)
            v.memset(negr_f[:], -1.0)
            v.memset(strow[0:1, 0:6], 0.0)
            v.memset(strow[0:1, ST_LNML : ST_LNML + 1], LNML)
            v.memset(strow[0:1, ST_EPS : ST_EPS + 1], 1e-5)
            v.memset(ug_full[:], 0.0)
            v.memset(uu_full[:], 0.0)
            v.memset(ud_full[:], 0.0).then_inc(sm.cINI, 1)
            if PH >= 1:
                v.wait_ge(sm.dTAB, 32)
                for i in range(NCH * NKV + 2):
                    if i < NCH * NKV:
                        tc, kv = divmod(i, NKV)
                        v.wait_ge(sm.cSH, i + 1)
                        if i >= 2:
                            st["dKS"].wait(v, i - 2)
                        w0 = (i % 2) * 512
                        v.tensor_mul(
                            t1_t[:, w0 : w0 + XCH], pk4[i % 4][:, 0:XCH],
                            ck_sb[:, tc * XCH : (tc + 1) * XCH],
                        )
                        v.tensor_mul(
                            t2_t[:, w0 : w0 + XCH], sh_t[:, w0 : w0 + XCH],
                            sk_sb[:, tc * XCH : (tc + 1) * XCH],
                        )
                        v.tensor_add(
                            kout[:, w0 : w0 + XCH], t1_t[:, w0 : w0 + XCH],
                            t2_t[:, w0 : w0 + XCH],
                        ).then_inc(sm.cKE, 1)
                    if i >= 2:
                        ti = i - 2
                        v.wait_ge(sm.cKN, ti + 1)
                        v.reduce_max(
                            strow[0:1, ST_RT0 + (ti % 2) : ST_RT0 + (ti % 2) + 1],
                            (PE_ if ti % 2 == 0 else PF)[0:1, 0:XCH],
                            mybir.AxisListType.X,
                        )
                        v.tensor_max(
                            strow[0:1, ST_KMAX : ST_KMAX + 1],
                            strow[0:1, ST_KMAX : ST_KMAX + 1],
                            strow[0:1, ST_RT0 + (ti % 2) : ST_RT0 + (ti % 2) + 1],
                        ).then_inc(sm.cKNR, 1)
            if PH >= 2:
                for i in range(NVI):
                    v.wait_ge(sm.cVM, i + 1)
                    if i >= 2:
                        st["dVS"].wait(v, i - 2)
                    v.tensor_copy(sl(vout, i), pk4[i % 4][:]).then_inc(sm.cVE, 1)
            if PH >= 3:
                v.wait_ge(sm.dTAB, 64)
                for i in range(NH * 2 + 2):
                    if i < NH * 2:
                        qh, qg = divmod(i, 2)
                        v.wait_ge(sm.cQSH, i + 1)
                        if i >= 2:
                            st["dQS"].wait(v, i - 2)
                        v.tensor_mul(
                            sl(t1_t, i), pk4[(i // 2 % 2) * 2 + i % 2][:],
                            cq_sb[:, qg * 512 : (qg + 1) * 512],
                        )
                        v.tensor_mul(
                            sl(t2_t, i), sl(sh_t, i), sq_sb[:, qg * 512 : (qg + 1) * 512]
                        )
                        v.tensor_add(sl(kout, i), sl(t1_t, i), sl(t2_t, i)).then_inc(sm.cQE, 1)
                    if i >= 2:
                        ti = i - 2
                        v.wait_ge(sm.cQN, ti + 1)
                        v.reduce_max(
                            strow[0:1, ST_RT0 + 2 + (ti % 2) : ST_RT0 + 2 + (ti % 2) + 1],
                            (PE_ if ti % 2 == 0 else PF)[0:1, :],
                            mybir.AxisListType.X,
                        )
                        v.tensor_max(
                            strow[0:1, ST_QMAX : ST_QMAX + 1],
                            strow[0:1, ST_QMAX : ST_QMAX + 1],
                            strow[0:1, ST_RT0 + 2 + (ti % 2) : ST_RT0 + 2 + (ti % 2) + 1],
                        ).then_inc(sm.cQNR, 1)
                v.wait_ge(sm.cSH, NCH * NKV + 2)
                v.tensor_add(
                    strow[0:1, ST_LNS : ST_LNS + 1],
                    strow[0:1, ST_LNK : ST_LNK + 1],
                    strow[0:1, ST_LNQ : ST_LNQ + 1],
                ).then_inc(sm.cROW, 1)

            if PH >= 4:
                st["dQS"].wait_all(v, NH * 2)
                v.wait_ge(sm.dMSC, 80)
                pav = [PC, PD]
                pnm = [PE_, PF]
                for i in range(NI4 + 1):
                    if i < NI4:
                        for half in range(2):
                            v.wait_ge(sm.cPART, 2 * i + half + 1)
                            for m in ((0, 1) if half == 0 else (2, 3)):
                                v.tensor_mul(
                                    probs[:, m * 512 : (m + 1) * 512],
                                    probs[:, m * 512 : (m + 1) * 512],
                                    mask_sb[:, m * 512 : (m + 1) * 512],
                                ).then_inc(sm.cMSK, 1)
                    if i >= 1:
                        ii = i - 1
                        qh = ii // 2
                        qg = ii % 2
                        v.wait_ge(sm.cBC, ii + 1)
                        v.tensor_copy(sl(rb_sb, ii), pnm[ii % 2][:]).then_inc(sm.cRB, 1)
                        v.tensor_mul(
                            attn_res[:, qh * 1024 + qg * 512 : qh * 1024 + qg * 512 + 512],
                            sl(rb_sb, ii),
                            pav[ii % 2][:],
                        ).then_inc(sm.cATN, 1)

            if PH >= 5:
                po = [PA[:, 0:512], PA[:, 512:1024], PB[:, 0:512], PB[:, 512:1024]]
                for i in range(NDC * 2):
                    dc, qg = divmod(i, 2)
                    v.wait_ge(sm.cOM, i + 1)
                    st["dXR"].wait(v, i)
                    if i >= 4:
                        st["dR1S"].wait(v, i - 4)
                        v.wait_ge(sm.cSQ, i - 3)
                    v.tensor_add(
                        sl(r1buf, i, nslot=4), sl(xresb, i, nslot=4), po[i % 4]
                    ).then_inc(sm.cR1, 1)
                    if i >= 2:
                        v.wait_ge(sm.cST, i - 1)
                    v.tensor_copy(sl(r1bf, i), sl(r1buf, i, nslot=4)).then_inc(sm.cR1B, 1)
                v.wait_ge(sm.cST, NDC * 2)
                for qg in range(2):
                    pstq = PC if qg == 0 else PD
                    v.tensor_scalar_mul(
                        lrow[0:1, qg * 512 : qg * 512 + 512], pstq[0:1, :], 1.0 / D
                    )
                    v.tensor_scalar_mul(
                        rrow[0:1, qg * 512 : qg * 512 + 512], pstq[32:33, :], 1.0 / D
                    )
                for qg in range(2):
                    v.tensor_mul(
                        strow[0:1, ST_TMP : ST_TMP + 512],
                        lrow[0:1, qg * 512 : qg * 512 + 512],
                        lrow[0:1, qg * 512 : qg * 512 + 512],
                    )
                    ins = v.tensor_sub(
                        rrow[0:1, qg * 512 : qg * 512 + 512],
                        rrow[0:1, qg * 512 : qg * 512 + 512],
                        strow[0:1, ST_TMP : ST_TMP + 512],
                    )
                ins.then_inc(sm.cROW, 1)
                v.wait_ge(sm.cROW2, 1)
                for qg in range(2):
                    ins = v.tensor_mul(
                        strow[0:1, ST_B[qg] : ST_B[qg] + 512],
                        lrow[0:1, qg * 512 : qg * 512 + 512],
                        strow[0:1, ST_RSTD[qg] : ST_RSTD[qg] + 512],
                    )
                ins.then_inc(sm.cROW, 1)
                v.wait_ge(sm.cABC, 1)
                v.tensor_copy(a_bc[:, 0:512], PE_[:])
                v.tensor_copy(a_bc[:, 512:1024], PF[:])
                v.tensor_copy(b_bc[:, 0:512], PA[:, 0:512])
                v.tensor_copy(b_bc[:, 512:1024], PA[:, 512:1024]).then_inc(sm.cROW, 1)

            if PH >= 6:
                for i in range(NDC * 2):
                    dc, qg = divmod(i, 2)
                    st["dR1L"].wait(v, i)
                    if i >= 4:
                        st["dRAS"].wait(v, i - 4)
                        v.wait_ge(sm.cX1, i - 3)
                    v.tensor_mul(
                        sl(x1f32, i, nslot=4),
                        sl(r1ld, i, nslot=4),
                        a_bc[:, qg * 512 : (qg + 1) * 512],
                    )
                    v.tensor_sub(
                        sl(x1f32, i, nslot=4),
                        sl(x1f32, i, nslot=4),
                        b_bc[:, qg * 512 : (qg + 1) * 512],
                    ).then_inc(sm.cHID, 1)

            if PH >= 7:
                v.wait_ge(sm.cUG, 1)
                for qg in range(2):
                    v.tensor_copy(
                        ug_full[0:16, qg * 512 : (qg + 1) * 512],
                        PA[32 * qg : 32 * qg + 16, 0:512],
                    )
                    ins = v.tensor_copy(
                        uu_full[0:16, qg * 512 : (qg + 1) * 512],
                        PB[32 * qg : 32 * qg + 16, 0:512],
                    )
                ins.then_inc(sm.cUG, 1)
                for p, (h0, h1) in enumerate(HG_BOUNDS):
                    for hgi in range(h1 - h0):
                        hc = h0 + hgi
                        for qg in range(2):
                            gi = hc * 2 + qg
                            v.wait_ge(sm.cSIL, gi + 1)
                            v.wait_ge(sm.cUM, gi + 1)
                            if p >= 1 and hgi == 0 and qg == 0:
                                v.wait_ge(sm.cDM, p * NDC * 2)
                            v.tensor_mul(
                                hidden[:, hgi * 1024 + qg * 512 : hgi * 1024 + qg * 512 + 512],
                                sl(tsil, gi),
                                PB[:, (gi % 2) * 512 : (gi % 2) * 512 + 512],
                            ).then_inc(sm.cHID, 1)
                    if p == NPASS - 1:
                        v.wait_ge(sm.cUG, 3)
                        v.tensor_copy(ud_full[0:16, 0:512], PE_[0:16, :])
                        v.tensor_copy(ud_full[0:16, 512:1024], PE_[32:48, :]).then_inc(sm.cUG, 1)
                    for i in range(NDC * 2):
                        gii = p * NDC * 2 + i
                        v.wait_ge(sm.cDM, gii + 1)
                        st["dRAL"].wait(v, gii)
                        v.tensor_add(
                            sl(raccb, i, nslot=8),
                            sl(raccb, i, nslot=8),
                            (PC if gii % 2 == 0 else PD)[:],
                        ).then_inc(sm.cUDE, 1)
                        if p == NPASS - 1:
                            if i >= 2:
                                v.wait_ge(sm.cST, NDC * 2 + i - 1)
                            v.tensor_copy(
                                sl(r1bf, i), sl(raccb, i, nslot=8)
                            ).then_inc(sm.cST2, 1)
                v.wait_ge(sm.cST, NDC * 2 + NDC * 2)
                for qg in range(2):
                    pstq = PA if qg == 0 else PB
                    v.tensor_scalar_mul(
                        lrow[0:1, qg * 512 : qg * 512 + 512], pstq[0:1, 0:512], 1.0 / D
                    )
                    v.tensor_scalar_mul(
                        rrow[0:1, qg * 512 : qg * 512 + 512], pstq[32:33, 0:512], 1.0 / D
                    )
                for qg in range(2):
                    v.tensor_mul(
                        strow[0:1, ST_TMP : ST_TMP + 512],
                        lrow[0:1, qg * 512 : qg * 512 + 512],
                        lrow[0:1, qg * 512 : qg * 512 + 512],
                    )
                    ins = v.tensor_sub(
                        rrow[0:1, qg * 512 : qg * 512 + 512],
                        rrow[0:1, qg * 512 : qg * 512 + 512],
                        strow[0:1, ST_TMP : ST_TMP + 512],
                    )
                ins.then_inc(sm.cROW, 1)
                v.wait_ge(sm.cROW2, 2)
                for qg in range(2):
                    ins = v.tensor_mul(
                        strow[0:1, ST_B[qg] : ST_B[qg] + 512],
                        lrow[0:1, qg * 512 : qg * 512 + 512],
                        strow[0:1, ST_RSTD[qg] : ST_RSTD[qg] + 512],
                    )
                ins.then_inc(sm.cROW, 1)
                v.wait_ge(sm.cABC2, 1)
                v.tensor_copy(a_bc[:, 0:512], PC[:])
                v.tensor_copy(a_bc[:, 512:1024], PD[:])
                v.tensor_copy(b_bc[:, 0:512], PE_[:])
                v.tensor_copy(b_bc[:, 512:1024], PF[:]).then_inc(sm.cROW, 1)

            if PH >= 8:
                for i in range(NDC * 2):
                    dc, qg = divmod(i, 2)
                    st["dRAL"].wait(v, NPASS * NDC * 2 + i)
                    if i >= 4:
                        st["dOUT"].wait(v, i - 4)
                    v.tensor_mul(
                        sl(outb8, i, nslot=4),
                        sl(raccld8, i, nslot=4),
                        a_bc[:, qg * 512 : (qg + 1) * 512],
                    )
                    v.tensor_sub(
                        sl(outb8, i, nslot=4),
                        sl(outb8, i, nslot=4),
                        b_bc[:, qg * 512 : (qg + 1) * 512],
                    ).then_inc(sm.cO8, 1)

    return nc



_NC_CACHE = {}


def _host_prep(inputs):
    x = np.asarray(inputs["L0_attn_x"], dtype=np.float32)
    Wq = np.asarray(inputs["L0_attn_Wq"], dtype=np.float32)
    Wk = np.asarray(inputs["L0_attn_Wk"], dtype=np.float32)
    Wv = np.asarray(inputs["L0_attn_Wv"], dtype=np.float32)
    Wo = np.asarray(inputs["L0_attn_Wo"], dtype=np.float32)
    gwF = np.asarray(inputs["L0_ffn_gate_w"], dtype=np.float32)
    uwF = np.asarray(inputs["L0_ffn_up_w"], dtype=np.float32)
    dwF = np.asarray(inputs["L0_ffn_down_w"], dtype=np.float32)
    gaF = np.asarray(inputs["L0_ffn_gate_a"], dtype=np.float32)
    gbF = np.asarray(inputs["L0_ffn_gate_b"], dtype=np.float32)
    uaF = np.asarray(inputs["L0_ffn_up_a"], dtype=np.float32)
    ubF = np.asarray(inputs["L0_ffn_up_b"], dtype=np.float32)
    daF = np.asarray(inputs["L0_ffn_down_a"], dtype=np.float32)
    dbF = np.asarray(inputs["L0_ffn_down_b"], dtype=np.float32)

    xfm = [np.ascontiguousarray(x[b].T) for b in range(2)]

    wq_l = np.ascontiguousarray(Wq.reshape(NDK, 128, NH, 128).transpose(1, 2, 0, 3))
    wk_l = np.ascontiguousarray(Wk.reshape(NDK, 128, NKV, 128).transpose(1, 2, 0, 3))
    wv_l = np.ascontiguousarray(Wv.reshape(NDK, 128, 1024).transpose(1, 0, 2))
    wo_l = np.ascontiguousarray(Wo.reshape(NH, 128, NDC, 128).transpose(1, 2, 0, 3))

    def _wpad(wT, bT):
        t = wT.reshape(NDK, 128, NHC, 128).transpose(1, 2, 0, 3)
        pad = np.zeros((128, NHC, 1, 128), np.float32)
        pad[0:RK, :, 0, :] = bT.reshape(RK, NHC, 128).copy()
        return np.ascontiguousarray(np.concatenate([t, pad], axis=2))

    gw_l = _wpad(gwF.T, gbF.T)
    uw_l = _wpad(uwF.T, ubF.T)
    dt_ = dwF.T.reshape(NHC, 128, NDC, 128).transpose(1, 2, 0, 3)
    dpad = np.zeros((128, NDC, 1, 128), np.float32)
    dpad[0:RK, :, 0, :] = dbF.T.reshape(RK, NDC, 128)
    dw_l = np.ascontiguousarray(np.concatenate([dt_, dpad], axis=2))

    ga_l = np.ascontiguousarray(gaF.T.reshape(NDK, 128, RK).transpose(1, 0, 2))
    ua_l = np.ascontiguousarray(uaF.T.reshape(NDK, 128, RK).transpose(1, 0, 2))
    da_l = np.ascontiguousarray(daF.T.reshape(NHC, 128, RK).transpose(1, 0, 2))

    inv = 1.0 / (10000.0 ** (np.arange(0, 64, dtype=np.float64) / 64.0))
    pos = np.arange(S, dtype=np.float64)
    fr = np.outer(inv, pos)
    cos = np.cos(fr).astype(np.float32)
    sin = np.sin(fr).astype(np.float32)
    ck_l = np.ascontiguousarray(np.concatenate([cos, cos], 0))
    sk_l = np.ascontiguousarray(np.concatenate([-sin, sin], 0))

    mk = np.zeros((128, 4, 512), np.float32)
    pp = np.arange(128)[:, None]
    ff = np.arange(512)[None, :]
    for m in range(4):
        mk[:, m, :] = (128 * m + pp <= ff).astype(np.float32)
    mk_l = np.ascontiguousarray(mk.astype(ml_dtypes.bfloat16))

    in_maps = []
    tok_slices = []
    for c in range(8):
        b, j = divmod(c, 4)
        t0a = 512 * j
        t1a = 512 * (7 - j)
        tok_slices.append((b, t0a, t1a))
        xr = np.concatenate(
            [xfm[b][:, t0a : t0a + 512], xfm[b][:, t1a : t1a + 512]], axis=1
        )
        xres_l = np.ascontiguousarray(xr.reshape(NDC, 128, TOK).transpose(1, 0, 2))
        cq_l = np.ascontiguousarray(
            np.concatenate([ck_l[:, t0a : t0a + 512], ck_l[:, t1a : t1a + 512]], axis=1)
        )
        sq_l = np.ascontiguousarray(
            np.concatenate([sk_l[:, t0a : t0a + 512], sk_l[:, t1a : t1a + 512]], axis=1)
        )
        in_maps.append(
            dict(
                xfm=xfm[b], xres=xres_l, wq=wq_l, wk=wk_l, wv=wv_l, wo=wo_l,
                gw=gw_l, uw=uw_l, dw=dw_l, ga=ga_l, ua=ua_l, da=da_l,
                ck=ck_l, sk=sk_l, cq=cq_l, sq=sq_l, maskin=mk_l,
                meta=np.array([[j, 0, 0, 0]], np.int32),
            )
        )
    return in_maps, tok_slices


def kernel(**inputs):
    in_maps, tok_slices = _host_prep(inputs)
    if "nc" not in _NC_CACHE:
        _NC_CACHE["nc"] = build_program()
    nc = _NC_CACHE["nc"]
    res = run_bass_kernel_spmd(nc, in_maps, core_ids=list(range(8)))
    out = np.empty((2, S, D), np.float32)
    for c in range(8):
        b, t0a, t1a = tok_slices[c]
        o = np.asarray(res.results[c]["out"])
        om = o.transpose(1, 0, 2).reshape(D, TOK)
        out[b, t0a : t0a + 512, :] = om[:, 0:512].T
        out[b, t1a : t1a + 512, :] = om[:, 512:1024].T
    return out
